# version 03
# baseline (speedup 1.0000x reference)
import numpy as np
import ml_dtypes
from contextlib import ExitStack

import concourse.bass as bass
import concourse.mybir as mybir
from concourse.bass_utils import run_bass_kernel_spmd

F32 = mybir.dt.float32
BF16 = mybir.dt.bfloat16
AF = mybir.ActivationFunctionType
ALU = mybir.AluOpType
AX = mybir.AxisListType

D = 1024
T = 2048
BC = 2
TC = 512
NTC = T // TC
ND = D // 128
EPS = 1e-5
NEG = -1.0e30
GELU_C = 0.7978845608028654

FFN_DEPTH = 6
WINDOW = 256
DIM_INNER_PAD = 2816
W1_COLS = 2 * DIM_INNER_PAD


def _blob_entries():
    e = [("l0a_ln1_g", 8), ("l0a_ln1_b", 8), ("l0a_c_attn_b", 24),
         ("l0a_attn_proj_b", 8), ("l0a_lnx_g", 8), ("l0a_lnx_b", 8),
         ("l0a_q_attn_b", 8), ("l0a_x_kv_b", 16), ("l0a_x_proj_b", 8),
         ("l0a_ln2_g", 8), ("l0a_ln2_b", 8), ("l0a_fc_b", 32),
         ("l0a_mlp_proj_b", 8)]
    for i in range(FFN_DEPTH):
        e += [(f"ffn{i}_ln_g", 8), (f"ffn{i}_ln_b", 8),
              (f"ffn{i}_b1", W1_COLS // 128), (f"ffn{i}_b2", 8)]
    return e

BLOB_ENTRIES = _blob_entries()
BLOB_INDEX = {}
_c = 0
for _n, _k in BLOB_ENTRIES:
    BLOB_INDEX[_n] = _c
    _c += _k
BLOB_COLS = _c

ROWG_ENTRIES = (["l0a_ln1_g", "l0a_lnx_g", "l0a_ln2_g"]
                + [f"ffn{i}_ln_g" for i in range(FFN_DEPTH)])
ROWG_INDEX = {n: i * D for i, n in enumerate(ROWG_ENTRIES)}
ROWG_COLS = D * len(ROWG_ENTRIES)


def host_constants(inp):
    f32 = np.float32
    bf = ml_dtypes.bfloat16
    c = {}
    c["encT"] = np.ascontiguousarray(
        np.asarray(inp["L0_attn_encoder_hidden_states"], dtype=f32)[0].T)
    c["scorer"] = np.ascontiguousarray(
        np.asarray(inp["L0_attn_scorer_logits"], dtype=f32).reshape(1, T))

    blob = np.zeros((128, BLOB_COLS), dtype=f32)

    def put(name, vec):
        k = BLOB_INDEX[name]
        v = np.asarray(vec, dtype=f32).reshape(-1)
        blob[:, k:k + v.size // 128] = v.reshape(-1, 128).T

    put("l0a_ln1_g", inp["L0_attn_ln1_g"]); put("l0a_ln1_b", inp["L0_attn_ln1_b"])
    put("l0a_c_attn_b", inp["L0_attn_c_attn_b"])
    put("l0a_attn_proj_b", inp["L0_attn_attn_proj_b"])
    put("l0a_lnx_g", inp["L0_attn_lnx_g"]); put("l0a_lnx_b", inp["L0_attn_lnx_b"])
    put("l0a_q_attn_b", inp["L0_attn_q_attn_b"])
    put("l0a_x_kv_b", inp["L0_attn_x_kv_b"])
    put("l0a_x_proj_b", inp["L0_attn_x_proj_b"])
    put("l0a_ln2_g", inp["L0_attn_ln2_g"]); put("l0a_ln2_b", inp["L0_attn_ln2_b"])
    put("l0a_fc_b", inp["L0_attn_fc_b"])
    put("l0a_mlp_proj_b", inp["L0_attn_mlp_proj_b"])
    b1 = np.asarray(inp["L0_ffn_b1"], dtype=f32)
    for i in range(FFN_DEPTH):
        put(f"ffn{i}_ln_g", inp["L0_ffn_ln_g"][i])
        put(f"ffn{i}_ln_b", inp["L0_ffn_ln_b"][i])
        b1p = np.zeros(W1_COLS, dtype=f32)
        b1p[0:2730] = b1[i, 0:2730]
        b1p[DIM_INNER_PAD:DIM_INNER_PAD + 2730] = b1[i, 2730:5460]
        put(f"ffn{i}_b1", b1p)
        put(f"ffn{i}_b2", inp["L0_ffn_b2"][i])
    c["blob"] = blob

    for nm, key in [("l0a_c_attn_w", "L0_attn_c_attn_w"),
                    ("l0a_attn_proj_w", "L0_attn_attn_proj_w"),
                    ("l0a_q_attn_w", "L0_attn_q_attn_w"),
                    ("l0a_x_kv_w", "L0_attn_x_kv_w"),
                    ("l0a_x_proj_w", "L0_attn_x_proj_w"),
                    ("l0a_fc_w", "L0_attn_fc_w"),
                    ("l0a_mlp_proj_w", "L0_attn_mlp_proj_w"),
                    ("ffn_wqkv", "L0_ffn_Wqkv"), ("ffn_wo", "L0_ffn_Wo"),
                    ("moe_wr", "L1_ffn_Wr"), ("moe_wg", "L1_ffn_Wg"),
                    ("moe_wu", "L1_ffn_Wu"), ("moe_wd", "L1_ffn_Wd"),
                    ("moe_sg", "L1_ffn_Sg"), ("moe_su", "L1_ffn_Su"),
                    ("moe_sd", "L1_ffn_Sd")]:
        c[nm] = np.asarray(inp[key], dtype=f32)

    w1 = np.asarray(inp["L0_ffn_W1"], dtype=f32)
    w1p = np.zeros((FFN_DEPTH, D, W1_COLS), dtype=f32)
    w1p[:, :, 0:2730] = w1[:, :, 0:2730]
    w1p[:, :, DIM_INNER_PAD:DIM_INNER_PAD + 2730] = w1[:, :, 2730:5460]
    c["ffn_w1"] = w1p
    w2 = np.asarray(inp["L0_ffn_W2"], dtype=f32)
    w2p = np.zeros((FFN_DEPTH, DIM_INNER_PAD, D), dtype=f32)
    w2p[:, 0:2730, :] = w2
    c["ffn_w2"] = w2p
    c["ffn_qs"] = np.ascontiguousarray(
        np.asarray(inp["L0_ffn_q_scale"], dtype=f32).reshape(1, -1))
    c["ffn_ks"] = np.ascontiguousarray(
        np.asarray(inp["L0_ffn_k_scale"], dtype=f32).reshape(1, -1))

    c["l1a_wq"] = np.ascontiguousarray(np.asarray(inp["L1_attn_Wq"], dtype=f32).T)
    c["l1a_wk"] = np.ascontiguousarray(np.asarray(inp["L1_attn_Wk"], dtype=f32).T)
    c["l1a_wv"] = np.ascontiguousarray(np.asarray(inp["L1_attn_Wv"], dtype=f32).T)
    c["l1a_wo"] = np.ascontiguousarray(np.asarray(inp["L1_attn_Wo"], dtype=f32).T)
    c["l1a_wd"] = np.ascontiguousarray(
        np.asarray(inp["L1_attn_Wd"], dtype=f32).reshape(8, 128).T)
    c["l1a_bd"] = np.asarray(inp["L1_attn_bd"], dtype=f32).reshape(1, 1)
    c["l1a_os"] = np.asarray(inp["L1_attn_out_scale"], dtype=f32).reshape(1, 1)

    c["ident_bf"] = np.eye(128, dtype=bf)
    c["ident_f32"] = np.eye(128, dtype=f32)
    ii = np.arange(128)[:, None]; jj = np.arange(128)[None, :]
    c["m_causal"] = np.where(jj > ii, NEG, 0.0).astype(f32)
    c["m_tril2"] = np.where(jj > ii + 2, NEG, 0.0).astype(f32)
    c["m_lower"] = np.where(jj < ii, NEG, 0.0).astype(f32)
    c["m01_upper"] = (jj > ii).astype(bf)
    me = np.zeros((128, 2), dtype=f32)
    me[0:126, 0] = NEG
    me[0:127, 1] = NEG
    c["m_edge2"] = me
    rb = np.zeros((1, 2048), dtype=f32)
    rb[0, 0:1024] = np.asarray(inp["L0_attn_c_attn_b"], dtype=f32)[2048:3072]
    rb[0, 1024:2048] = np.asarray(inp["L0_attn_x_kv_b"], dtype=f32)[1024:2048]
    c["rowbias"] = rb

    half = np.arange(0, 64, 2, dtype=np.float64)
    inv_freq = 1.0 / (10000.0 ** (half / 64.0))
    t = np.arange(T, dtype=np.float64)
    freqs = np.concatenate([t[:, None] * inv_freq[None, :]] * 2, -1)
    base_scale = (half + 0.4 * 64) / (1.4 * 64)
    power = (t - T // 2) / (WINDOW // 2)
    scale = np.concatenate([base_scale[None, :] ** power[:, None]] * 2, -1)
    c["rot_c1"] = (np.cos(freqs) * scale).astype(f32)
    c["rot_s1"] = (np.sin(freqs) * scale).astype(f32)
    c["rot_c2"] = (np.cos(freqs) / scale).astype(f32)
    c["rot_s2"] = (np.sin(freqs) / scale).astype(f32)
    return c


class Sem:
    def __init__(self, es, nc, name):
        self.h = es.enter_context(nc.semaphore(name))
        self.n = 0

    def inc(self, instr, k=1):
        instr.then_inc(self.h, k)
        self.n += k
        return self.n


class Prog:
    def __init__(self, taps=(), upto=None):
        self.nc = bass.Bass("TRN2", target_bir_lowering=False)
        self.es = ExitStack()
        self.taps = list(taps)
        self.upto = upto
        self.inputs = {}
        self.sb = {}
        self.dram = {}
        self.sems = {}
        self.uid = 0

    def inp(self, name, shape, dt=F32):
        ap = self.nc.dram_tensor(name, list(shape), dt, kind="ExternalInput").ap()
        self.inputs[name] = ap
        return ap

    def input(self, name):
        if name not in self.inputs:
            shape, dt = INPUT_SHAPES[name]
            self.inp(name, shape, dt)
        return self.inputs[name]

    def scratch(self, name, shape, dt=F32):
        ap = self.nc.dram_tensor(name, list(shape), dt, kind="Internal").ap()
        self.dram[name] = ap
        return ap

    def out(self, name, shape, dt=F32):
        return self.nc.dram_tensor(name, list(shape), dt, kind="ExternalOutput").ap()

    def sbuf(self, name, shape, dt=F32):
        t = self.es.enter_context(self.nc.sbuf_tensor("sb_" + name, list(shape), dt))
        self.sb[name] = t
        return t

    def psum(self, name, shape, dt=F32):
        t = self.es.enter_context(self.nc.psum_tensor(name, list(shape), dt))
        self.sb[name] = t
        return t

    def sem(self, name):
        if name not in self.sems:
            self.sems[name] = Sem(self.es, self.nc, name)
        return self.sems[name]

    def sems_list(self, *names):
        return [self.sem(n) for n in names]


def blob_col(P, name, k=0):
    cidx = BLOB_INDEX[name] + k
    return P.sb["blob"][:, cidx:cidx + 1]


def fmv(ap, t0, tw=TC):
    return ap[:, t0:t0 + tw].rearrange("(c p) t -> p c t", p=128)



def alloc_common(P):
    P.sbuf("blob", [128, BLOB_COLS], F32)
    P.sbuf("ident_bf", [128, 128], BF16)
    P.sbuf("ident_f32", [128, 128], F32)
    P.sbuf("m_causal", [128, 128], F32)
    P.sbuf("m_tril2", [128, 128], F32)
    P.sbuf("m_lower", [128, 128], F32)
    P.sbuf("m01_upper", [128, 128], BF16)
    P.sbuf("m_edge2", [128, 2], F32)
    P.sbuf("rowbias_sb", [1, 2048], F32)
    P.sbuf("attst", [128, 64], F32)
    P.sbuf("scorer_sb", [1, T], F32)
    P.sbuf("ffn_qs_sb", [1, FFN_DEPTH * 64], F32)
    P.sbuf("ffn_ks_sb", [1, FFN_DEPTH * 64], F32)
    P.sbuf("l1a_bd_sb", [1, 1], F32)
    P.sbuf("l1a_os_sb", [1, 1], F32)
    for nm in ["rot_c1", "rot_s1", "rot_c2", "rot_s2"]:
        P.sbuf(nm + "_sb", [128, 16 * 64], BF16)
    P.sbuf("ones_row_bf", [1, 128], BF16)
    P.sbuf("ones_row_f32", [1, 128], F32)
    P.sbuf("ones_col_bf", [128, 1], BF16)
    P.sbuf("eps_col", [128, 1], F32)
    P.sbuf("eps8_col", [128, 1], F32)
    P.sbuf("rows", [1, 4 * TC], F32)
    P.sbuf("wka", [128, 12288], F32)
    P.sbuf("wkb", [128, 8192], F32)
    P.sbuf("wkc", [128, 2048], F32)
    P.sbuf("bfa", [128, 16384], BF16)
    P.sbuf("bfb", [128, 4096], BF16)
    P.sbuf("wgt", [128, 8192], BF16)
    P.psum("PS0", [128, 512], F32)
    P.psum("PS1", [128, 512], F32)
    P.psum("PS2", [128, 512], F32)
    P.psum("PS3", [128, 512], F32)
    P.psum("PS4", [128, 512], F32)
    P.psum("PBF", [128, 1024], BF16)
    P.psum("PRA", [1, 512], F32)
    P.psum("PRB", [1, 512], F32)


INPUT_SHAPES = {
    "xT": ([BC, D, T], F32), "encT": ([D, T], F32), "scorer": ([1, T], F32),
    "blob": ([128, BLOB_COLS], F32),
    "ident_bf": ([128, 128], BF16), "ident_f32": ([128, 128], F32),
    "m_causal": ([128, 128], F32), "m_tril2": ([128, 128], F32),
    "m_lower": ([128, 128], F32), "m01_upper": ([128, 128], BF16),
    "m_edge2": ([128, 2], F32), "rowbias": ([1, 2048], F32),
    "rot_c1": ([T, 64], F32), "rot_s1": ([T, 64], F32),
    "rot_c2": ([T, 64], F32), "rot_s2": ([T, 64], F32),
    "ffn_qs": ([1, FFN_DEPTH * 64], F32), "ffn_ks": ([1, FFN_DEPTH * 64], F32),
    "l0a_c_attn_w": ([D, 3 * D], F32), "l0a_attn_proj_w": ([D, D], F32),
    "l0a_q_attn_w": ([D, D], F32), "l0a_x_kv_w": ([D, 2 * D], F32),
    "l0a_x_proj_w": ([D, D], F32), "l0a_fc_w": ([D, 4 * D], F32),
    "l0a_mlp_proj_w": ([4 * D, D], F32),
    "ffn_wqkv": ([FFN_DEPTH, D, 3 * D], F32), "ffn_wo": ([FFN_DEPTH, D, D], F32),
    "ffn_w1": ([FFN_DEPTH, D, W1_COLS], F32),
    "ffn_w2": ([FFN_DEPTH, DIM_INNER_PAD, D], F32),
    "l1a_wq": ([D, D], F32), "l1a_wk": ([D, D], F32), "l1a_wv": ([D, D], F32),
    "l1a_wo": ([D, D], F32), "l1a_wd": ([128, 8], F32),
    "l1a_bd": ([1, 1], F32), "l1a_os": ([1, 1], F32),
    "moe_wr": ([D, 16], F32), "moe_wg": ([16, D, 512], F32),
    "moe_wu": ([16, D, 512], F32), "moe_wd": ([16, 512, D], F32),
    "moe_sg": ([D, D], F32), "moe_su": ([D, D], F32), "moe_sd": ([D, D], F32),
}


def _unused_declare_inputs(P):
    P.inp("xT", [BC, D, T])
    P.inp("encT", [D, T])
    P.inp("scorer", [1, T])
    P.inp("blob", [128, BLOB_COLS])
    P.inp("ident_bf", [128, 128], BF16)
    P.inp("ident_f32", [128, 128])
    P.inp("m_causal", [128, 128]); P.inp("m_tril2", [128, 128]); P.inp("m_lower", [128, 128])
    P.inp("m01_upper", [128, 128], BF16)
    P.inp("rot_c1", [T, 64]); P.inp("rot_s1", [T, 64])
    P.inp("rot_c2", [T, 64]); P.inp("rot_s2", [T, 64])
    P.inp("ffn_qs", [1, FFN_DEPTH * 64]); P.inp("ffn_ks", [1, FFN_DEPTH * 64])
    P.inp("l0a_c_attn_w", [D, 3 * D]); P.inp("l0a_attn_proj_w", [D, D])
    P.inp("l0a_q_attn_w", [D, D]); P.inp("l0a_x_kv_w", [D, 2 * D])
    P.inp("l0a_x_proj_w", [D, D]); P.inp("l0a_fc_w", [D, 4 * D])
    P.inp("l0a_mlp_proj_w", [4 * D, D])
    P.inp("ffn_wqkv", [FFN_DEPTH, D, 3 * D]); P.inp("ffn_wo", [FFN_DEPTH, D, D])
    P.inp("ffn_w1", [FFN_DEPTH, D, W1_COLS]); P.inp("ffn_w2", [FFN_DEPTH, DIM_INNER_PAD, D])
    P.inp("l1a_wq", [D, D]); P.inp("l1a_wk", [D, D]); P.inp("l1a_wv", [D, D])
    P.inp("l1a_wo", [D, D]); P.inp("l1a_wd", [D, 1])
    P.inp("l1a_bd", [1, 1]); P.inp("l1a_os", [1, 1])
    P.inp("moe_wr", [D, 16]); P.inp("moe_wg", [16, D, 512]); P.inp("moe_wu", [16, D, 512])
    P.inp("moe_wd", [16, 512, D])
    P.inp("moe_sg", [D, D]); P.inp("moe_su", [D, D]); P.inp("moe_sd", [D, D])


def alloc_scratch(P):
    for b in range(BC):
        P.scratch(f"H{b}", [D, T])
        P.scratch(f"G{b}", [D, T])
        P.scratch(f"X{b}", [D, T])
        P.scratch(f"Y{b}", [D, T])
        P.scratch(f"LN{b}", [D, T], BF16)
        P.scratch(f"QF{b}", [D, T], BF16)
        P.scratch(f"KF{b}", [D, T], BF16)
        P.scratch(f"VT{b}", [16, 128, D], BF16)
        P.scratch(f"AO{b}", [D, T], BF16)
        P.scratch(f"HH{b}", [W1_COLS, T], BF16)
        P.scratch(f"HG{b}", [DIM_INNER_PAD, T], BF16)
        P.scratch(f"BO{b}", [D, T])
    P.scratch("KENC", [D, T], BF16)
    P.scratch("VENC", [16, 128, D], BF16)
    P.scratch("KAUG", [16 * 65, T], BF16)
    for b in range(BC):
        P.scratch(f"QKVT{b}", [16, 128, 3 * D], BF16)
    P.scratch("LDROW", [1, T])
    P.scratch("CWF", [16, T])



ENGINES = ("sync", "gpsimd", "scalar", "vector", "tensor")


class Op:
    __slots__ = ("eng", "fn", "reads", "writes", "hazard", "dma",
                 "deps", "needed", "sem", "tok", "tag", "gate")

    def __init__(self, eng, fn, reads, writes, hazard, dma, tag=""):
        self.eng = eng
        self.fn = fn
        self.reads = tuple(reads)
        self.writes = tuple(writes)
        self.hazard = hazard
        self.dma = dma
        self.deps = []
        self.needed = False
        self.sem = None
        self.tok = 0
        self.tag = tag
        self.gate = None


class Sched:
    def __init__(self, P):
        self.P = P
        self.ops = []

    def add(self, eng, fn, r=(), w=(), hazard=False, dma=False, tag=""):
        op = Op(eng, fn, r, w, hazard, dma, tag)
        self.ops.append(op)
        return op

    def dma(self, q, out, in_, r=(), w=(), tag="", **kw):
        assert q in ("sync", "gpsimd", "scalar")
        return self.add(q, lambda e: e.dma_start(out=out, in_=in_, **kw),
                        r=r, w=w, dma=True, tag=tag)

    def mm(self, out, lhsT, rhs, start, stop, r=(), w=(), tag="", **kw):
        return self.add("tensor",
                        lambda e: e.matmul(out, lhsT, rhs, start=start, stop=stop, **kw),
                        r=r, w=w, tag=tag)

    def trans(self, out, in_, ident, r=(), w=(), tag=""):
        return self.add("tensor", lambda e: e.transpose(out, in_, ident),
                        r=r, w=w, tag=tag)

    def act(self, out, in_, func, r=(), w=(), bias=0.0, scale=1.0,
            accum_out=None, hazard=False, tag=""):
        if accum_out is not None:
            return self.add("scalar",
                            lambda e: e.activation(out, in_, func, bias=bias,
                                                   scale=scale, accum_out=accum_out),
                            r=r, w=w, hazard=True, tag=tag)
        return self.add("scalar",
                        lambda e: e.activation(out, in_, func, bias=bias, scale=scale),
                        r=r, w=w, hazard=hazard, tag=tag)

    def vop(self, fn, r=(), w=(), hazard=False, tag=""):
        return self.add("vector", fn, r=r, w=w, hazard=hazard, tag=tag)

    def pop(self, fn, r=(), w=(), tag=""):
        return self.add("gpsimd", fn, r=r, w=w, tag=tag)

    def plan_and_emit(self, block_name="prog"):
        P = self.P
        nc = P.nc
        last_w = {}
        readers = {}
        for op in self.ops:
            dep = {}
            for k in op.reads:
                lw = last_w.get(k)
                if lw is not None:
                    dep[id(lw)] = lw
            for k in op.writes:
                lw = last_w.get(k)
                if lw is not None:
                    dep[id(lw)] = lw
                for rd in readers.get(k, ()):
                    dep[id(rd)] = rd
            keep = []
            for d in dep.values():
                if d is op:
                    continue
                if (d.eng == op.eng and not d.dma and not d.hazard
                        and not op.hazard):
                    continue
                keep.append(d)
            op.deps = keep
            for d in keep:
                d.needed = True
            for k in op.reads:
                readers.setdefault(k, []).append(op)
            for k in op.writes:
                last_w[k] = op
                readers[k] = [op] if k in op.reads else []
        NDSEM = 24
        csem = {e: P.sem("c_" + e) for e in ENGINES}
        dpool = [P.sem(f"d{j}") for j in range(NDSEM)]
        di = 0
        for op in self.ops:
            if op.dma:
                s = dpool[di % NDSEM]
                di += 1
                op.gate = s.n
                s.n += 16
                op.sem, op.tok = s, s.n
            elif op.needed or op.hazard:
                s = csem[op.eng]
                s.n += 1
                op.sem, op.tok = s, s.n
        per_eng = {e: [o for o in self.ops if o.eng == e] for e in ENGINES}

        def body(eng_name):
            def f(handle):
                seen = {}
                for op in per_eng[eng_name]:
                    for d in op.deps:
                        assert d.sem is not None
                        key = d.sem.h
                        if seen.get(key, -1) < d.tok:
                            handle.wait_ge(d.sem.h, d.tok)
                            seen[key] = d.tok
                    if op.dma and op.gate and seen.get(op.sem.h, -1) < op.gate:
                        handle.wait_ge(op.sem.h, op.gate)
                        seen[op.sem.h] = op.gate
                    ins = op.fn(handle)
                    if op.sem is not None:
                        if ins is None:
                            ins = handle.engine_nop() if not op.dma else None
                        assert ins is not None, f"op {op.tag} returned no instruction"
                        ins.then_inc(op.sem.h, 16 if op.dma else 1)
            return f

        with nc.Block(block_name) as b:
            b.sync(body("sync"))
            b.gpsimd(body("gpsimd"))
            b.scalar(body("scalar"))
            b.vector(body("vector"))
            b.tensor(body("tensor"))
        self.ops = []


def emit_clear_sems(P):
    nc = P.nc
    for e in ENGINES:
        P.sem("c_" + e)
    for e in ("sync", "gpsimd", "scalar"):
        P.sem("q_" + e)
    with nc.Block("init") as b:
        @b.sync
        def _(sync):
            for s in P.sems.values():
                sync.sem_clear(s.h)


def emit_consts(S):
    P = S.P
    pairs = [("blob", "blob"), ("ident_bf", "ident_bf"),
             ("ident_f32", "ident_f32"), ("m_causal", "m_causal"),
             ("m_tril2", "m_tril2"), ("m_lower", "m_lower"),
             ("m01_upper", "m01_upper"), ("m_edge2", "m_edge2"),
             ("rowbias_sb", "rowbias"), ("scorer_sb", "scorer"),
             ("ffn_qs_sb", "ffn_qs"), ("ffn_ks_sb", "ffn_ks"),
             ("l1a_bd_sb", "l1a_bd"), ("l1a_os_sb", "l1a_os")]
    for sb_name, in_name in pairs:
        S.dma("sync", P.sb[sb_name][:], P.input(in_name)[:, :], w=[sb_name])
    for nm in ["rot_c1", "rot_s1", "rot_c2", "rot_s2"]:
        S.dma("gpsimd", P.sb[nm + "_sb"][:],
              P.input(nm).rearrange("(c p) d -> p c d", p=128), w=[nm])
    S.vop(lambda v: v.memset(P.sb["ones_row_bf"][:], 1.0), w=["ones_row_bf"])
    S.vop(lambda v: v.memset(P.sb["ones_row_f32"][:], 1.0), w=["ones_row_f32"])
    S.vop(lambda v: v.memset(P.sb["ones_col_bf"][:], 1.0), w=["ones_col_bf"])
    S.vop(lambda v: v.memset(P.sb["eps_col"][:], EPS), w=["eps_col"])
    S.vop(lambda v: v.memset(P.sb["eps8_col"][:], 1e-8), w=["eps8_col"])


def sl(arena, lo_col, width):
    lo = lo_col // 512
    hi = (lo_col + width + 511) // 512
    return [f"{arena}.s{j}" for j in range(lo, hi)]


def emit_ln(S, tag, src, dst_bf=None, dst_f32=None, res=None, gname=None,
            shift=False):
    P = S.P
    src, ksrc = src
    res, kres = res if res is not None else (None, None)
    dst_bf, kdbf = dst_bf if dst_bf is not None else (None, None)
    dst_f32, kd32 = dst_f32 if dst_f32 is not None else (None, None)
    wka, wkb, wkc = P.sb["wka"], P.sb["wkb"], P.sb["wkc"]
    bfa, bfb = P.sb["bfa"], P.sb["bfb"]
    rows = P.sb["rows"]
    half = ND // 2
    HW = ND * TC

    for i in range(NTC):
        pi = i & 1
        t0 = i * TC
        x32 = wka[:, pi * HW:(pi + 1) * HW]
        resb = wka[:, 8192:8192 + HW]
        t1 = wkb[:, pi * HW:(pi + 1) * HW]
        asb = wkc[:, pi * 512:(pi + 1) * 512]
        csb = wkc[:, 1024 + pi * 512:1024 + (pi + 1) * 512]
        xbf = bfa[:, pi * HW:(pi + 1) * HW]
        sq = bfa[:, 8192 + pi * HW:8192 + (pi + 1) * HW]
        ybf = bfb[:, 0:HW]
        kx = sl("wka", pi * HW, HW)
        kr = sl("wka", 8192, HW)
        kt = sl("wkb", pi * HW, HW)
        kab = sl("wkc", pi * 512, 512)
        kcb = sl("wkc", 1024 + pi * 512, 512)
        kxb = sl("bfa", pi * HW, HW)
        ksq = sl("bfa", 8192 + pi * HW, HW)
        ky = sl("bfb", 0, HW)
        ka, kc = ("PS0", "PS1") if pi == 0 else ("PS2", "PS3")
        ps_a = P.sb["PS0"] if pi == 0 else P.sb["PS2"]
        ps_c = P.sb["PS1"] if pi == 0 else P.sb["PS3"]
        pra, prb = P.sb["PRA"], P.sb["PRB"]
        krow = ["rows"]
        rs = lambda k: rows[0:1, k * TC:(k + 1) * TC]

        lq = "gpsimd" if src.dtype == BF16 else "sync"
        if not shift:
            S.dma(lq, x32.rearrange("p (c t) -> p c t", c=ND), fmv(src, t0),
                  r=[ksrc], w=kx, tag=f"{tag}.x{i}")
        else:
            S.dma(lq, x32[:, 0:half * TC].rearrange("p (c t) -> p c t", c=half),
                  src[0:half * 128, t0:t0 + TC].rearrange("(c p) t -> p c t", p=128),
                  r=[ksrc], w=kx, tag=f"{tag}.xa{i}")
            if t0 == 0:
                S.dma(lq,
                      x32[:, half * TC:HW].rearrange("p (c t) -> p c t", c=half)[:, :, 1:TC],
                      src[half * 128:D, 0:TC - 1].rearrange("(c p) t -> p c t", p=128),
                      r=[ksrc], w=kx, tag=f"{tag}.xs{i}")
                S.vop(lambda v, _x=x32: [v.memset(_x[:, d * TC:d * TC + 1], 0.0)
                                         for d in range(half, ND)][-1],
                      w=kx, tag=f"{tag}.z{i}")
            else:
                S.dma(lq,
                      x32[:, half * TC:HW].rearrange("p (c t) -> p c t", c=half),
                      src[half * 128:D, t0 - 1:t0 + TC - 1]
                      .rearrange("(c p) t -> p c t", p=128),
                      r=[ksrc], w=kx, tag=f"{tag}.xs{i}")
        if res is not None:
            S.dma("sync", resb.rearrange("p (c t) -> p c t", c=ND), fmv(res, t0),
                  r=[kres], w=kr, tag=f"{tag}.res{i}")
        S.dma("gpsimd", xbf, x32, r=kx, w=kxb, tag=f"{tag}.cast{i}")
        S.act(sq, x32, AF.Square, r=kx, w=ksq, tag=f"{tag}.sqr{i}")
        for d in range(ND):
            S.mm(pra[:, :], P.sb["ones_col_bf"][:], xbf[:, d * TC:(d + 1) * TC],
                 d == 0, d == ND - 1, r=kxb + ["ones_col_bf"], w=["PRA"],
                 tag=f"{tag}.sm{i}")
        for d in range(ND):
            S.mm(prb[:, :], P.sb["ones_col_bf"][:], sq[:, d * TC:(d + 1) * TC],
                 d == 0, d == ND - 1, r=ksq + ["ones_col_bf"], w=["PRB"],
                 tag=f"{tag}.sv{i}")
        S.vop(lambda v, a=rs(0), b_=pra: v.tensor_scalar(a, b_[:, :], -1.0 / D, None, ALU.mult),
              r=["PRA"], w=krow, tag=f"{tag}.r0_{i}")
        S.vop(lambda v, a=rs(1), b_=prb: v.tensor_scalar(a, b_[:, :], 1.0 / D, None, ALU.mult),
              r=["PRB"], w=krow, tag=f"{tag}.r1_{i}")
        S.vop(lambda v, a=rs(2), b_=rs(0): v.tensor_tensor(a, b_, b_, ALU.mult),
              r=krow, w=krow, tag=f"{tag}.r2_{i}")
        S.vop(lambda v, a=rs(1), b_=rs(2): v.tensor_tensor(a, a, b_, ALU.subtract),
              r=krow, w=krow, tag=f"{tag}.r3_{i}")
        S.act(rs(2), rs(1), AF.Sqrt, bias=P.sb["eps_col"][0:1, :],
              r=krow + ["eps_col"], w=krow, tag=f"{tag}.sqrt{i}")
        S.vop(lambda v, a=rs(3), b_=rs(2): v.reciprocal(a, b_),
              r=krow, w=krow, hazard=True, tag=f"{tag}.recip{i}")
        S.vop(lambda v, a=rs(2), b_=rs(0), c_=rs(3): v.tensor_tensor(a, b_, c_, ALU.mult),
              r=krow, w=krow, tag=f"{tag}.c{i}")
        S.mm(ps_a[:], P.sb["ones_row_f32"][:], rs(3), True, True,
             r=krow + ["ones_row_f32"], w=[ka], tag=f"{tag}.ba{i}")
        S.mm(ps_c[:], P.sb["ones_row_f32"][:], rs(2), True, True,
             r=krow + ["ones_row_f32"], w=[kc], tag=f"{tag}.bc{i}")
        S.add("scalar", lambda e, a=asb, b_=ps_a: e.copy(a, b_[:]),
              r=[ka], w=kab, tag=f"{tag}.cpa{i}")
        S.add("scalar", lambda e, a=csb, b_=ps_c: e.copy(a, b_[:]),
              r=[kc], w=kcb, tag=f"{tag}.cpc{i}")
        S.pop(lambda e, o=t1, a=x32, b_=asb:
              e.tensor_tensor(o.rearrange("p (c t) -> p c t", c=ND),
                              a.rearrange("p (c t) -> p c t", c=ND),
                              b_.unsqueeze(1).broadcast_to([128, ND, TC]),
                              ALU.mult),
              r=kx + kab, w=kt, tag=f"{tag}.t1_{i}")
        S.vop(lambda v, o=t1, b_=csb:
              v.tensor_tensor(o.rearrange("p (c t) -> p c t", c=ND),
                              o.rearrange("p (c t) -> p c t", c=ND),
                              b_.unsqueeze(1).broadcast_to([128, ND, TC]),
                              ALU.add),
              r=kt + kcb, w=kt, tag=f"{tag}.t2_{i}")
        if gname is not None:
            for d in range(ND):
                S.vop(lambda v, o=ybf[:, d * TC:(d + 1) * TC],
                      a=t1[:, d * TC:(d + 1) * TC],
                      g=blob_col(P, gname + "_g", d),
                      bb=blob_col(P, gname + "_b", d).broadcast_to([128, TC]):
                      v.scalar_tensor_tensor(o, a, g, bb, ALU.mult, ALU.add),
                      r=kt + ["blob"], w=ky, tag=f"{tag}.gb{i}")
        elif dst_bf is not None:
            S.vop(lambda v, o=ybf, a=t1: v.tensor_copy(o, a),
                  r=kt, w=ky, tag=f"{tag}.cpy{i}")
        if res is not None:
            S.pop(lambda e, o=t1, b_=resb: e.tensor_tensor(o, o, b_, ALU.add),
                  r=kt + kr, w=kt, tag=f"{tag}.radd{i}")
        if dst_f32 is not None:
            S.dma("sync", fmv(dst_f32, t0), t1.rearrange("p (c t) -> p c t", c=ND),
                  r=kt, w=[kd32], tag=f"{tag}.st32_{i}")
        if dst_bf is not None:
            S.dma("sync", fmv(dst_bf, t0), ybf.rearrange("p (c t) -> p c t", c=ND),
                  r=ky, w=[kdbf], tag=f"{tag}.stbf{i}")


def emit_proj(S, tag, src, w_dram, out, din, dout, bias=None, func=None,
              res=None, scale_col=None, w_slice=None, bias_off=0, out_f32=False):
    P = S.P
    src_ap, ksrc = src
    w_ap, kw = w_dram
    out_ap, kout = out
    res_ap, kres = res if res is not None else (None, None)
    nin = din // 128
    OW = min(dout, max(128, (8192 // nin) // 128 * 128))
    wgt = P.sb["wgt"]
    bfa, bfb = P.sb["bfa"], P.sb["bfb"]
    wka, wkb = P.sb["wka"], P.sb["wkb"]
    src_is_bf = src_ap.dtype == BF16
    xcols = nin * TC

    for olo in range(0, dout, OW):
        ow = min(OW, dout - olo)
        now = ow // 128
        wlo = olo + (w_slice[0] if w_slice else 0)
        for c in range(nin):
            S.dma("gpsimd", wgt[:, c * ow:(c + 1) * ow],
                  w_ap[128 * c:128 * (c + 1), wlo:wlo + ow],
                  r=[kw], w=sl("wgt", c * ow, ow), tag=f"{tag}.w{olo}_{c}")
        for i in range(NTC):
            pi = i & 1
            t0 = i * TC
            if din <= 1024:
                xbf = bfa[:, pi * xcols:(pi + 1) * xcols]
                kxb = sl("bfa", pi * xcols, xcols)
            else:
                xbf = bfa[:, 0:xcols]
                kxb = sl("bfa", 0, xcols)
            q = "sync" if src_is_bf else "gpsimd"
            S.dma(q, xbf.rearrange("p (c t) -> p c t", c=nin),
                  src_ap[:, t0:t0 + TC].rearrange("(c p) t -> p c t", p=128),
                  r=[ksrc], w=kxb, tag=f"{tag}.x{olo}_{i}")
            if res is not None:
                resb = wka[:, pi * 4096:pi * 4096 + now * TC]
                krb = sl("wka", pi * 4096, now * TC)
                S.dma("sync", resb.rearrange("p (c t) -> p c t", c=now),
                      res_ap[olo:olo + ow, t0:t0 + TC].rearrange("(c p) t -> p c t", p=128),
                      r=[kres], w=krb, tag=f"{tag}.r{olo}_{i}")
            for o in range(now):
                ps = P.sb[f"PS{o % 4}"]
                kps = f"PS{o % 4}"
                for c in range(nin):
                    S.mm(ps[:], wgt[:, (c * ow + o * 128):(c * ow + (o + 1) * 128)],
                         xbf[:, c * TC:(c + 1) * TC], c == 0, c == nin - 1,
                         r=sl("wgt", c * ow + o * 128, 128) + kxb, w=[kps],
                         tag=f"{tag}.mm{olo}_{i}_{o}")
                bcol = blob_col(P, bias, bias_off + (olo + o * 128) // 128) if bias else None
                og = olo + o * 128
                if out_f32 and res is None:
                    y32 = wkb[:, pi * 4096 + o * TC:pi * 4096 + (o + 1) * TC]
                    ky32 = sl("wkb", pi * 4096 + o * TC, TC)
                    if bias:
                        S.vop(lambda v, y=y32, p_=ps, b_=bcol:
                              v.tensor_scalar(y, p_[:], b_, None, ALU.add),
                              r=[kps, "blob"], w=ky32, tag=f"{tag}.ev{olo}_{i}_{o}")
                    else:
                        S.vop(lambda v, y=y32, p_=ps: v.tensor_copy(y, p_[:]),
                              r=[kps], w=ky32, tag=f"{tag}.ev{olo}_{i}_{o}")
                    S.dma("sync", out_ap[og:og + 128, t0:t0 + TC], y32,
                          r=ky32, w=[kout], tag=f"{tag}.st{olo}_{i}_{o}")
                elif res is not None:
                    y32 = wkb[:, pi * 4096 + o * TC:pi * 4096 + (o + 1) * TC]
                    ky32 = sl("wkb", pi * 4096 + o * TC, TC)
                    rslice = resb[:, o * TC:(o + 1) * TC]
                    if bias:
                        S.vop(lambda v, y=y32, p_=ps, b_=bcol, rr=rslice:
                              v.scalar_tensor_tensor(y, p_[:], b_, rr, ALU.add, ALU.add),
                              r=[kps, "blob"] + krb, w=ky32, tag=f"{tag}.ev{olo}_{i}_{o}")
                    else:
                        S.vop(lambda v, y=y32, p_=ps, rr=rslice:
                              v.tensor_tensor(y, p_[:], rr, ALU.add),
                              r=[kps] + krb, w=ky32, tag=f"{tag}.ev{olo}_{i}_{o}")
                    S.dma("sync", out_ap[og:og + 128, t0:t0 + TC], y32,
                          r=ky32, w=[kout], tag=f"{tag}.st{olo}_{i}_{o}")
                else:
                    ybf = bfb[:, (pi * 4 + o % 4) * TC:(pi * 4 + o % 4 + 1) * TC]
                    kyb = sl("bfb", (pi * 4 + o % 4) * TC, TC)
                    if func is None:
                        if scale_col is not None:
                            S.vop(lambda v, y=ybf, p_=ps, sc=scale_col:
                                  v.tensor_scalar(y, p_[:], sc, None, ALU.mult),
                                  r=[kps, "l1a_os_sb"], w=kyb, tag=f"{tag}.ev{olo}_{i}_{o}")
                        elif bias:
                            S.vop(lambda v, y=ybf, p_=ps, b_=bcol:
                                  v.tensor_scalar(y, p_[:], b_, None, ALU.add),
                                  r=[kps, "blob"], w=kyb, tag=f"{tag}.ev{olo}_{i}_{o}")
                        else:
                            S.vop(lambda v, y=ybf, p_=ps: v.tensor_copy(y, p_[:]),
                                  r=[kps], w=kyb, tag=f"{tag}.ev{olo}_{i}_{o}")
                    elif func == "gelu_tanh":
                        u = wkb[:, pi * 4096 + (o % 8) * TC:pi * 4096 + (o % 8 + 1) * TC]
                        ku = sl("wkb", pi * 4096 + (o % 8) * TC, TC)
                        sqv = wka[:, pi * 4096 + (o % 8) * TC:pi * 4096 + (o % 8 + 1) * TC]
                        ksq2 = sl("wka", pi * 4096 + (o % 8) * TC, TC)
                        if bias:
                            S.vop(lambda v, y=u, p_=ps, b_=bcol:
                                  v.tensor_scalar(y, p_[:], b_, None, ALU.add),
                                  r=[kps, "blob"], w=ku, tag=f"{tag}.gt0_{olo}_{i}_{o}")
                        else:
                            S.vop(lambda v, y=u, p_=ps: v.tensor_copy(y, p_[:]),
                                  r=[kps], w=ku, tag=f"{tag}.gt0_{olo}_{i}_{o}")
                        S.act(sqv, u, AF.Square, r=ku, w=ksq2, tag=f"{tag}.gt1_{olo}_{i}_{o}")
                        S.vop(lambda v, y=sqv: v.tensor_scalar(y, y, 0.044715, 1.0,
                                                               ALU.mult, ALU.add),
                              r=ksq2, w=ksq2, tag=f"{tag}.gt2_{olo}_{i}_{o}")
                        S.pop(lambda e, y=sqv, a=u: e.tensor_tensor(y, y, a, ALU.mult),
                              r=ksq2 + ku, w=ksq2, tag=f"{tag}.gt3_{olo}_{i}_{o}")
                        S.act(sqv, sqv, AF.Tanh, scale=GELU_C, r=ksq2, w=ksq2,
                              tag=f"{tag}.gt4_{olo}_{i}_{o}")
                        S.vop(lambda v, y=ybf, tq=sqv, a=u:
                              v.scalar_tensor_tensor(y, tq, 1.0, a, ALU.add, ALU.mult),
                              r=ksq2 + ku, w=kyb, tag=f"{tag}.gt5_{olo}_{i}_{o}")
                        S.vop(lambda v, y=ybf: v.tensor_scalar(y, y, 0.5, None, ALU.mult),
                              r=kyb, w=kyb, tag=f"{tag}.gt6_{olo}_{i}_{o}")
                    else:
                        S.act(ybf, ps[:], func, bias=(bcol if bias else 0.0),
                              r=[kps] + (["blob"] if bias else []), w=kyb,
                              tag=f"{tag}.ev{olo}_{i}_{o}")
                    S.dma("sync", out_ap[og:og + 128, t0:t0 + TC], ybf,
                          r=kyb, w=[kout], tag=f"{tag}.st{olo}_{i}_{o}")


def build_program(upto=None, taps=()):
    P = Prog()
    alloc_common(P)
    alloc_scratch(P)
    out = P.out("out", [BC, D, T])
    tap_outs = {}
    for tp in taps:
        shape = P.dram[tp].shape if hasattr(P.dram[tp], "shape") else None
        tap_outs[tp] = P.out("tap_" + tp, list(P.dram[tp].shape),
                             P.dram[tp].dtype)
    emit_clear_sems(P)
    S = Sched(P)
    emit_consts(S)

    stages = []

    def stage(name, fn):
        stages.append((name, fn))

    xT = P.input("xT")

    def s_ln0():
        for b in range(BC):
            emit_ln(S, f"ln0b{b}", src=(xT[b], f"xT{b}"),
                    dst_f32=(P.dram[f"H{b}"], f"H{b}"))
    stage("ln0", s_ln0)

    def s_l0a_ln1():
        for b in range(BC):
            emit_ln(S, f"l0a_ln1b{b}", src=(P.dram[f"H{b}"], f"H{b}"),
                    dst_bf=(P.dram[f"LN{b}"], f"LN{b}"), gname="l0a_ln1")
    stage("l0a_ln1", s_l0a_ln1)

    def s_l0a_qkv():
        for b in range(BC):
            emit_proj(S, f"l0a_q{b}", src=(P.dram[f"LN{b}"], f"LN{b}"),
                      w_dram=(P.input("l0a_c_attn_w"), "w_cattn"),
                      out=(P.dram[f"QF{b}"], f"QF{b}"), din=D, dout=D,
                      bias="l0a_c_attn_b", w_slice=(0, D))
            emit_proj(S, f"l0a_k{b}", src=(P.dram[f"LN{b}"], f"LN{b}"),
                      w_dram=(P.input("l0a_c_attn_w"), "w_cattn"),
                      out=(P.dram[f"KF{b}"], f"KF{b}"), din=D, dout=D,
                      bias="l0a_c_attn_b", bias_off=8, w_slice=(D, 2 * D))
            emit_proj_tm(S, f"l0a_v{b}", src=(P.dram[f"LN{b}"], f"LN{b}"),
                         w_dram=(P.input("l0a_c_attn_w"), "w_cattn"),
                         out_vt=(P.dram[f"VT{b}"], f"VT{b}"), din=D, dout=D,
                         bias_row=(P.sb["rowbias_sb"][0:1, 0:1024], "rowbias_sb"),
                         w_slice=(2 * D, 3 * D))
    stage("l0a_qkv", s_l0a_qkv)

    def s_l0a_self():
        for b in range(BC):
            emit_attn(S, f"l0a_at{b}", qf_src=(P.dram[f"QF{b}"], f"QF{b}"),
                      kf_src=(P.dram[f"KF{b}"], f"KF{b}"),
                      vt_src=(P.dram[f"VT{b}"], f"VT{b}"),
                      out_ao=(P.dram[f"AO{b}"], f"AO{b}"), mode="self")
    stage("l0a_self", s_l0a_self)

    def s_l0a_proj():
        for b in range(BC):
            emit_proj(S, f"l0a_pr{b}", src=(P.dram[f"AO{b}"], f"AO{b}"),
                      w_dram=(P.input("l0a_attn_proj_w"), "w_aproj"),
                      out=(P.dram[f"G{b}"], f"G{b}"), din=D, dout=D,
                      bias="l0a_attn_proj_b", res=(P.dram[f"H{b}"], f"H{b}"))
    stage("l0a_proj", s_l0a_proj)

    def s_l0a_lnx():
        for b in range(BC):
            emit_ln(S, f"l0a_lnx{b}", src=(P.dram[f"G{b}"], f"G{b}"),
                    dst_bf=(P.dram[f"LN{b}"], f"LN{b}"), gname="l0a_lnx")
    stage("l0a_lnx", s_l0a_lnx)

    def s_enc_kv():
        emit_proj(S, "enc_k", src=(P.input("encT"), "encT"),
                  w_dram=(P.input("l0a_x_kv_w"), "w_xkv"),
                  out=(P.dram["KENC"], "KENC"), din=D, dout=D,
                  bias="l0a_x_kv_b", w_slice=(0, D))
        emit_proj_tm(S, "enc_v", src=(P.input("encT"), "encT"),
                     w_dram=(P.input("l0a_x_kv_w"), "w_xkv"),
                     out_vt=(P.dram["VENC"], "VENC"), din=D, dout=D,
                     bias_row=(P.sb["rowbias_sb"][0:1, 1024:2048], "rowbias_sb"),
                     w_slice=(D, 2 * D))
        emit_kaug(S, "enc_ka", (P.dram["KENC"], "KENC"), (P.dram["KAUG"], "KAUG"))
    stage("enc_kv", s_enc_kv)

    def s_l0a_q2():
        for b in range(BC):
            emit_proj(S, f"l0a_q2{b}", src=(P.dram[f"LN{b}"], f"LN{b}"),
                      w_dram=(P.input("l0a_q_attn_w"), "w_qattn"),
                      out=(P.dram[f"QF{b}"], f"QF{b}"), din=D, dout=D,
                      bias="l0a_q_attn_b")
    stage("l0a_q2", s_l0a_q2)

    def s_l0a_cross():
        for b in range(BC):
            emit_attn(S, f"l0a_cx{b}", qf_src=(P.dram[f"QF{b}"], f"QF{b}"),
                      kf_src=(P.dram["KAUG"], "KAUG"),
                      vt_src=(P.dram["VENC"], "VENC"),
                      out_ao=(P.dram[f"AO{b}"], f"AO{b}"), mode="cross")
    stage("l0a_cross", s_l0a_cross)

    def s_l0a_xproj():
        for b in range(BC):
            emit_proj(S, f"l0a_xp{b}", src=(P.dram[f"AO{b}"], f"AO{b}"),
                      w_dram=(P.input("l0a_x_proj_w"), "w_xproj"),
                      out=(P.dram[f"H{b}"], f"H{b}"), din=D, dout=D,
                      bias="l0a_x_proj_b", res=(P.dram[f"G{b}"], f"G{b}"))
    stage("l0a_xproj", s_l0a_xproj)

    def s_l0a_ln2():
        for b in range(BC):
            emit_ln(S, f"l0a_ln2{b}", src=(P.dram[f"H{b}"], f"H{b}"),
                    dst_bf=(P.dram[f"LN{b}"], f"LN{b}"), gname="l0a_ln2")
    stage("l0a_ln2", s_l0a_ln2)

    def s_l0a_fc():
        for b in range(BC):
            emit_proj(S, f"l0a_fc{b}", src=(P.dram[f"LN{b}"], f"LN{b}"),
                      w_dram=(P.input("l0a_fc_w"), "w_fc"),
                      out=(P.dram[f"HH{b}"][0:4 * D, :], f"HH{b}"), din=D,
                      dout=4 * D, bias="l0a_fc_b", func="gelu_tanh")
    stage("l0a_fc", s_l0a_fc)

    def s_l0a_mlp():
        for b in range(BC):
            emit_proj(S, f"l0a_mp{b}", src=(P.dram[f"HH{b}"][0:4 * D, :], f"HH{b}"),
                      w_dram=(P.input("l0a_mlp_proj_w"), "w_mlpp"),
                      out=(P.dram[f"G{b}"], f"G{b}"), din=4 * D, dout=D,
                      bias="l0a_mlp_proj_b", res=(P.dram[f"H{b}"], f"H{b}"))
    stage("l0a_mlp", s_l0a_mlp)

    def s_sand1():
        for b in range(BC):
            emit_ln(S, f"sand1_{b}", src=(P.dram[f"G{b}"], f"G{b}"),
                    res=(xT[b], f"xT{b}"), dst_f32=(P.dram[f"X{b}"], f"X{b}"))
    stage("sand1", s_sand1)

    def s_ffn_base():
        for b in range(BC):
            emit_ln(S, f"ffnbase{b}", src=(P.dram[f"X{b}"], f"X{b}"),
                    dst_f32=(P.dram[f"H{b}"], f"H{b}"))
    stage("ffn_base", s_ffn_base)

    cur = {b: f"H{b}" for b in range(BC)}
    alt = {b: f"G{b}" for b in range(BC)}

    for i in range(FFN_DEPTH):
        def s_ffn_layer(i=i):
            for b in range(BC):
                for part, lo in [("q", 0), ("k", D), ("v", 2 * D)]:
                    emit_proj_tm(S, f"f{i}{part}{b}",
                                 src=(P.dram[cur[b]], cur[b]),
                                 w_dram=(P.input("ffn_wqkv")[i], "ffn_wqkv"),
                                 out_vt=(P.dram[f"QKVT{b}"][:, :, lo:lo + D], f"QKVT{b}"),
                                 din=D, dout=D, w_slice=(lo, lo + D), shift=True)
                emit_qk_prep(S, f"f{i}prep{b}", (P.dram[f"QKVT{b}"], f"QKVT{b}"),
                             (P.dram[f"QF{b}"], f"QF{b}"), (P.dram[f"KF{b}"], f"KF{b}"), i)
                emit_attn(S, f"f{i}att{b}", qf_src=(P.dram[f"QF{b}"], f"QF{b}"),
                          kf_src=(P.dram[f"KF{b}"], f"KF{b}"),
                          vt_src=(P.dram[f"QKVT{b}"][:, :, 2 * D:3 * D], f"QKVT{b}"),
                          out_ao=(P.dram[f"AO{b}"], f"AO{b}"), mode="local")
                emit_proj(S, f"f{i}wo{b}", src=(P.dram[f"AO{b}"], f"AO{b}"),
                          w_dram=(P.input("ffn_wo")[i], "ffn_wo"),
                          out=(P.dram[alt[b]], alt[b]), din=D, dout=D,
                          res=(P.dram[cur[b]], cur[b]))
                cur[b], alt[b] = alt[b], cur[b]
                emit_ln(S, f"f{i}ln{b}", src=(P.dram[cur[b]], cur[b]),
                        dst_bf=(P.dram[f"LN{b}"], f"LN{b}"), gname=f"ffn{i}_ln",
                        shift=True)
                emit_proj(S, f"f{i}w1a{b}", src=(P.dram[f"LN{b}"], f"LN{b}"),
                          w_dram=(P.input("ffn_w1")[i], "ffn_w1"),
                          out=(P.dram[f"HH{b}"][0:DIM_INNER_PAD, :], f"HH{b}"),
                          din=D, dout=DIM_INNER_PAD, bias=f"ffn{i}_b1",
                          w_slice=(0, DIM_INNER_PAD))
                emit_proj(S, f"f{i}w1g{b}", src=(P.dram[f"LN{b}"], f"LN{b}"),
                          w_dram=(P.input("ffn_w1")[i], "ffn_w1"),
                          out=(P.dram[f"HH{b}"][DIM_INNER_PAD:W1_COLS, :], f"HH{b}"),
                          din=D, dout=DIM_INNER_PAD, bias=f"ffn{i}_b1",
                          bias_off=DIM_INNER_PAD // 128, func=AF.Gelu,
                          w_slice=(DIM_INNER_PAD, W1_COLS))
                emit_glu_mul(S, f"f{i}glu{b}",
                             (P.dram[f"HH{b}"][0:DIM_INNER_PAD, :], f"HH{b}"),
                             (P.dram[f"HH{b}"][DIM_INNER_PAD:W1_COLS, :], f"HH{b}"),
                             (P.dram[f"HG{b}"], f"HG{b}"), DIM_INNER_PAD)
                emit_proj(S, f"f{i}w2{b}", src=(P.dram[f"HG{b}"], f"HG{b}"),
                          w_dram=(P.input("ffn_w2")[i], "ffn_w2"),
                          out=(P.dram[alt[b]], alt[b]), din=DIM_INNER_PAD, dout=D,
                          bias=f"ffn{i}_b2", res=(P.dram[cur[b]], cur[b]))
                cur[b], alt[b] = alt[b], cur[b]
        stage(f"ffn_l{i}", s_ffn_layer)

    def s_sand2():
        for b in range(BC):
            emit_ln(S, f"sand2_{b}", src=(P.dram[cur[b]], cur[b]),
                    res=(P.dram[f"X{b}"], f"X{b}"), dst_f32=(P.dram[f"Y{b}"], f"Y{b}"))
    stage("sand2", s_sand2)

    def s_l1_ln():
        for b in range(BC):
            emit_ln(S, f"l1ln{b}", src=(P.dram[f"Y{b}"], f"Y{b}"),
                    dst_bf=(P.dram[f"LN{b}"], f"LN{b}"))
    stage("l1_ln", s_l1_ln)

    def s_l1_qkv():
        for b in range(BC):
            emit_proj(S, f"l1q{b}", src=(P.dram[f"LN{b}"], f"LN{b}"),
                      w_dram=(P.input("l1a_wq"), "l1a_wq"),
                      out=(P.dram[f"QF{b}"], f"QF{b}"), din=D, dout=D)
            emit_proj(S, f"l1k{b}", src=(P.dram[f"LN{b}"], f"LN{b}"),
                      w_dram=(P.input("l1a_wk"), "l1a_wk"),
                      out=(P.dram[f"KF{b}"], f"KF{b}"), din=D, dout=D)
            emit_proj_tm(S, f"l1v{b}", src=(P.dram[f"LN{b}"], f"LN{b}"),
                         w_dram=(P.input("l1a_wv"), "l1a_wv"),
                         out_vt=(P.dram[f"VT{b}"], f"VT{b}"), din=D, dout=D)
    stage("l1_qkv", s_l1_qkv)

    def s_l1_core():
        for b in range(BC):
            emit_l1_decay(S, f"l1d{b}", (P.dram[f"LN{b}"], f"LN{b}"))
            emit_l1_core(S, f"l1c{b}", (P.dram[f"QF{b}"], f"QF{b}"),
                         (P.dram[f"KF{b}"], f"KF{b}"),
                         (P.dram[f"VT{b}"], f"VT{b}"),
                         (P.dram[f"BO{b}"], f"BO{b}"))
    stage("l1_core", s_l1_core)

    def s_l1_out():
        S.mm(P.sb["PS4"][:, 0:1],
             P.sb["ones_row_f32"][:], P.sb["l1a_os_sb"][0:1, 0:1], True, True,
             r=["l1a_os_sb", "ones_row_f32"], w=["PS4"], tag="l1ob")
        S.add("scalar", lambda e, o=P.sb["wkc"][:, 1032:1033], p_=P.sb["PS4"]:
              e.copy(o, p_[:, 0:1]), r=["PS4"], w=sl("wkc", 1024, 16), tag="l1oc")
        for b in range(BC):
            emit_proj(S, f"l1o{b}", src=(P.dram[f"BO{b}"], f"BO{b}"),
                      w_dram=(P.input("l1a_wo"), "l1a_wo"),
                      out=(P.dram[f"AO{b}"], f"AO{b}"), din=D, dout=D,
                      scale_col=P.sb["wkc"][:, 1032:1033])
    stage("l1_out", s_l1_out)

    def s_sand3():
        for b in range(BC):
            emit_ln(S, f"sand3_{b}", src=(P.dram[f"AO{b}"], f"AO{b}"),
                    res=(P.dram[f"Y{b}"], f"Y{b}"), dst_f32=(P.dram[f"X{b}"], f"X{b}"))
    stage("sand3", s_sand3)

    def s_moe_ln():
        for b in range(BC):
            emit_ln(S, f"moeln{b}", src=(P.dram[f"X{b}"], f"X{b}"),
                    dst_bf=(P.dram[f"LN{b}"], f"LN{b}"))
    stage("moe_ln", s_moe_ln)

    def s_moe_shared():
        for b in range(BC):
            emit_proj(S, f"msg{b}", src=(P.dram[f"LN{b}"], f"LN{b}"),
                      w_dram=(P.input("moe_sg"), "moe_sg"),
                      out=(P.dram[f"HH{b}"][0:D, :], f"HH{b}"), din=D, dout=D,
                      func=AF.Silu)
            emit_proj(S, f"msu{b}", src=(P.dram[f"LN{b}"], f"LN{b}"),
                      w_dram=(P.input("moe_su"), "moe_su"),
                      out=(P.dram[f"HH{b}"][D:2 * D, :], f"HH{b}"), din=D, dout=D)
            emit_glu_mul(S, f"msm{b}", (P.dram[f"HH{b}"][0:D, :], f"HH{b}"),
                         (P.dram[f"HH{b}"][D:2 * D, :], f"HH{b}"),
                         (P.dram[f"HG{b}"][0:D, :], f"HG{b}"), D)
            emit_proj(S, f"msd{b}", src=(P.dram[f"HG{b}"][0:D, :], f"HG{b}"),
                      w_dram=(P.input("moe_sd"), "moe_sd"),
                      out=(P.dram[f"BO{b}"], f"BO{b}"), din=D, dout=D,
                      out_f32=True)
    stage("moe_shared", s_moe_shared)

    def s_moe_router():
        for b in range(BC):
            emit_moe_router(S, f"mr{b}", (P.dram[f"LN{b}"], f"LN{b}"),
                            (P.dram["CWF"], "CWF"))
            emit_moe_experts(S, f"me{b}", (P.dram[f"LN{b}"], f"LN{b}"),
                             (P.dram[f"BO{b}"], f"BO{b}"))
    stage("moe_core", s_moe_router)

    def s_sand4():
        for b in range(BC):
            emit_ln(S, f"sand4_{b}", src=(P.dram[f"BO{b}"], f"BO{b}"),
                    res=(P.dram[f"X{b}"], f"X{b}"), dst_f32=(out[b], "out"))
    stage("sand4", s_sand4)

    for name, fn in stages:
        fn()
        if upto is not None and name == upto:
            break

    for tp, ap_out in tap_outs.items():
        src_ap = P.dram[tp]
        shp = list(src_ap.shape)
        total = 1
        for s_ in shp:
            total *= s_
        flat_in = src_ap.rearrange(
            " ".join(f"a{j}" for j in range(len(shp))) + " -> (" +
            " ".join(f"a{j}" for j in range(len(shp))) + ")") if False else src_ap
        esz = 2 if src_ap.dtype == BF16 else 4
        buf = P.sb["bfa"] if src_ap.dtype == BF16 else P.sb["wka"]
        cols = total // 128
        CW = 4096
        for lo in range(0, cols, CW):
            wdt = min(CW, cols - lo)
            iv = src_ap.rearrange(
                ("a b -> (a b)" if len(shp) == 2 else "a b c -> (a b c)"))
            iv = iv.rearrange("(p x) -> p x", p=128)[:, lo:lo + wdt]
            ov = ap_out.rearrange(
                ("a b -> (a b)" if len(shp) == 2 else "a b c -> (a b c)"))
            ov = ov.rearrange("(p x) -> p x", p=128)[:, lo:lo + wdt]
            S.dma("sync", buf[:, 0:wdt], iv, r=[tp], w=["tapbuf"],
                  tag=f"tap.{tp}.{lo}")
            S.dma("sync", ov, buf[:, 0:wdt], r=["tapbuf"], w=["tapout"],
                  tag=f"tapo.{tp}.{lo}")

    S.add("sync", lambda e: e.nop(),
          r=["out", "tapout"] + [f"X{b}" for b in range(BC)], tag="fin")
    S.plan_and_emit("prog")
    with P.nc.Block("fin2") as b:
        @b.sync
        def _(sync):
            for qn, s in P.sems.items():
                if qn.startswith("d") and qn[1:].isdigit():
                    sync.wait_ge(s.h, s.n)
    P.es.close()
    return P


_BUILD_CACHE = {}


def kernel(**inputs):
    consts = host_constants(inputs)
    x = np.asarray(inputs["L0_attn_hidden_states"], dtype=np.float32)
    in_maps = []
    for c in range(8):
        m = dict(consts)
        m["xT"] = np.ascontiguousarray(x[BC * c:BC * c + BC].transpose(0, 2, 1))
        in_maps.append(m)
    P = build_program()
    in_maps = [{k: v for k, v in m.items() if k in P.inputs} for m in in_maps]
    res = run_bass_kernel_spmd(P.nc, in_maps, core_ids=list(range(8)))
    outs = [r["out"] for r in res.results]
    y = np.concatenate(outs, axis=0)
    return np.ascontiguousarray(y.transpose(0, 2, 1)).astype(np.float32)


def emit_proj_tm(S, tag, src, w_dram, out_vt, din, dout, bias_row=None,
                 w_slice=None, shift=False):
    P = S.P
    src_ap, ksrc = src
    w_ap, kw = w_dram
    vt_ap, kvt = out_vt
    nin = din // 128
    wgt = P.sb["wgt"]
    bfa, bfb, wkc = P.sb["bfa"], P.sb["bfb"], P.sb["wkc"]
    assert nin * dout <= 8192
    wlo0 = w_slice[0] if w_slice else 0
    for c in range(nin):
        S.dma("gpsimd", wgt[:, c * dout:(c + 1) * dout],
              w_ap[128 * c:128 * (c + 1), wlo0:wlo0 + dout],
              r=[kw], w=sl("wgt", c * dout, dout), tag=f"{tag}.w{c}")
    nob = dout // 512
    if bias_row is not None:
        brow, kbr = bias_row
        for j in range(nob):
            S.mm(P.sb["PS4"][:], P.sb["ones_row_f32"][:], brow[0:1, j * 512:(j + 1) * 512],
                 True, True, r=[kbr, "ones_row_f32"], w=["PS4"], tag=f"{tag}.bb{j}")
            S.add("scalar", lambda e, o=wkc[:, j * 512:(j + 1) * 512], p_=P.sb["PS4"]:
                  e.copy(o, p_[:]), r=["PS4"], w=sl("wkc", j * 512, 512), tag=f"{tag}.bc{j}")
    src_is_bf = src_ap.dtype == BF16
    xcols = nin * TC
    half = nin // 2
    for i in range(NTC):
        pi = i & 1
        t0 = i * TC
        xbf = bfa[:, pi * xcols:(pi + 1) * xcols]
        kxb = sl("bfa", pi * xcols, xcols)
        q = "sync" if src_is_bf else "gpsimd"
        if not shift:
            S.dma(q, xbf.rearrange("p (c t) -> p c t", c=nin),
                  src_ap[:, t0:t0 + TC].rearrange("(c p) t -> p c t", p=128),
                  r=[ksrc], w=kxb, tag=f"{tag}.x{i}")
        else:
            S.dma(q, xbf[:, 0:half * TC].rearrange("p (c t) -> p c t", c=half),
                  src_ap[0:half * 128, t0:t0 + TC].rearrange("(c p) t -> p c t", p=128),
                  r=[ksrc], w=kxb, tag=f"{tag}.xa{i}")
            if t0 == 0:
                S.dma(q, xbf[:, half * TC:xcols].rearrange("p (c t) -> p c t", c=half)[:, :, 1:TC],
                      src_ap[half * 128:din, 0:TC - 1].rearrange("(c p) t -> p c t", p=128),
                      r=[ksrc], w=kxb, tag=f"{tag}.xs{i}")
                S.vop(lambda v, _x=xbf: [v.memset(_x[:, d * TC:d * TC + 1], 0.0)
                                         for d in range(half, nin)][-1],
                      w=kxb, tag=f"{tag}.z{i}")
            else:
                S.dma(q, xbf[:, half * TC:xcols].rearrange("p (c t) -> p c t", c=half),
                      src_ap[half * 128:din, t0 - 1:t0 + TC - 1]
                      .rearrange("(c p) t -> p c t", p=128),
                      r=[ksrc], w=kxb, tag=f"{tag}.xs{i}")
        for tt in range(4):
            tile = i * 4 + tt
            for oc in range(nob):
                ps = P.sb[f"PS{oc % 2 + 2 * (tt % 2)}"]
                kps = f"PS{oc % 2 + 2 * (tt % 2)}"
                for c in range(nin):
                    S.mm(ps[:], xbf[:, c * TC + tt * 128:c * TC + (tt + 1) * 128],
                         wgt[:, c * dout + oc * 512:c * dout + (oc + 1) * 512],
                         c == 0, c == nin - 1,
                         r=sl("wgt", c * dout + oc * 512, 512) + kxb, w=[kps],
                         tag=f"{tag}.mm{tile}_{oc}")
                yb = bfb[:, (2 * (tile % 2) + oc % 2) * TC:
                         (2 * (tile % 2) + oc % 2 + 1) * TC]
                kyb = sl("bfb", (2 * (tile % 2) + oc % 2) * TC, TC)
                if bias_row is not None:
                    S.vop(lambda v, o=yb, p_=ps, bb=wkc[:, oc * 512:(oc + 1) * 512]:
                          v.tensor_tensor(o, p_[:], bb, ALU.add),
                          r=[kps] + sl("wkc", oc * 512, 512), w=kyb,
                          tag=f"{tag}.ev{tile}_{oc}")
                else:
                    S.vop(lambda v, o=yb, p_=ps: v.tensor_copy(o, p_[:]),
                          r=[kps], w=kyb, tag=f"{tag}.ev{tile}_{oc}")
                S.dma("sync", vt_ap[tile, :, oc * 512:(oc + 1) * 512], yb,
                      r=kyb, w=[kvt], tag=f"{tag}.st{tile}_{oc}")


def emit_kaug(S, tag, kf_src, kaug_dst, scorer_scale=8.0):
    P = S.P
    kf_ap, kkf = kf_src
    ka_ap, kka = kaug_dst
    bfb = P.sb["bfb"]
    S.vop(lambda v: v.tensor_scalar(bfb[0:1, 0:T], P.sb["scorer_sb"][:],
                                    scorer_scale, None, ALU.mult),
          r=["scorer_sb"], w=sl("bfb", 0, T), tag=f"{tag}.s8")
    for h in range(16):
        S.dma("sync", ka_ap[65 * h:65 * h + 64, :], kf_ap[64 * h:64 * h + 64, :],
              r=[kkf], w=[kka], tag=f"{tag}.k{h}")
        S.dma("sync", ka_ap[65 * h + 64:65 * h + 65, :], bfb[0:1, 0:T],
              r=sl("bfb", 0, T), w=[kka], tag=f"{tag}.sc{h}")


def _att_window(mode, qt):
    if mode == "self":
        return 0, 128 * (qt + 1)
    if mode == "cross":
        return 0, min(T, 128 * (qt + 1) + 2)
    if qt == 0:
        return 0, 128
    if qt == 1:
        return 0, 256
    return 128 * (qt - 2), 384


def _att_masks(P, mode, qt, klo, kw):
    out = []
    if mode == "self":
        out.append((128 * qt - klo, 128, P.sb["m_causal"][:], "m_causal"))
    elif mode == "cross":
        out.append((128 * qt - klo, 128, P.sb["m_tril2"][:], "m_tril2"))
        if 128 * (qt + 1) + 2 <= T:
            out.append((128 * (qt + 1) - klo, 2, P.sb["m_edge2"][:, 0:2], "m_edge2"))
    else:
        if qt == 0:
            out.append((0, 128, P.sb["m_causal"][:], "m_causal"))
        elif qt == 1:
            out.append((128, 128, P.sb["m_causal"][:], "m_causal"))
        else:
            out.append((0, 128, P.sb["m_lower"][:], "m_lower"))
            out.append((256, 128, P.sb["m_causal"][:], "m_causal"))
    return out


def emit_attn(S, tag, qf_src, kf_src, vt_src, out_ao, mode, dbg=None):
    P = S.P
    qf_ap, kqf = qf_src
    kf_ap, kkf = kf_src
    vt_ap, kvt = vt_src
    ao_ap, kao = out_ao
    bfa, bfb = P.sb["bfa"], P.sb["bfb"]
    vab = P.sb["wka"][:].bitcast(BF16)
    att = P.sb["attst"]
    ident = P.sb["ident_bf"][:]
    scale = 0.125 if mode in ("self", "cross") else 8.0
    use_max = (mode == "local")

    for kc in range(16):
        S.dma("sync", vab[:, kc * 1024:(kc + 1) * 1024], vt_ap[kc, :, :],
              r=[kvt], w=sl("wka", kc * 512, 512), tag=f"{tag}.v{kc}")
    kv_all = sl("wka", 0, 8192)

    if mode == "cross":
        S.vop(lambda v: v.memset(vab[64:65, 16384:18432], 1.0),
              w=sl("wka", 8192, 1024), tag=f"{tag}.one0")
        S.vop(lambda v: v.memset(vab[64:65, 18432:20480], 1.0),
              w=sl("wka", 9216, 1024), tag=f"{tag}.one1")

    for pr in range(8):
        hA, hB = 2 * pr, 2 * pr + 1
        if mode == "cross":
            S.dma("sync", vab[0:64, 16384:18432], qf_ap[64 * hA:64 * hA + 64, :],
                  r=[kqf], w=sl("wka", 8192, 1024), tag=f"{tag}.q{pr}a")
            S.dma("sync", vab[0:64, 18432:20480], qf_ap[64 * hB:64 * hB + 64, :],
                  r=[kqf], w=sl("wka", 9216, 1024), tag=f"{tag}.q{pr}b")
            S.dma("sync", vab[0:65, 20480:22528], kf_ap[65 * hA:65 * hA + 65, :],
                  r=[kkf], w=sl("wka", 10240, 1024), tag=f"{tag}.k{pr}a")
            S.dma("sync", vab[0:65, 22528:24576], kf_ap[65 * hB:65 * hB + 65, :],
                  r=[kkf], w=sl("wka", 11264, 1024), tag=f"{tag}.k{pr}b")
            qslot = [vab[0:65, 16384:18432], vab[0:65, 18432:20480]]
            kslot = [vab[0:65, 20480:22528], vab[0:65, 22528:24576]]
            kq_keys = [sl("wka", 8192, 1024), sl("wka", 9216, 1024)]
            kk_keys = [sl("wka", 10240, 1024), sl("wka", 11264, 1024)]
        else:
            sb16 = 16384 + 4096 * (pr % 2)
            sw = 8192 + 2048 * (pr % 2)
            S.dma("sync", vab[:, sb16:sb16 + 2048], qf_ap[128 * pr:128 * (pr + 1), :],
                  r=[kqf], w=sl("wka", sw, 1024), tag=f"{tag}.q{pr}")
            S.dma("sync", vab[:, sb16 + 2048:sb16 + 4096], kf_ap[128 * pr:128 * (pr + 1), :],
                  r=[kkf], w=sl("wka", sw + 1024, 1024), tag=f"{tag}.k{pr}")
            qslot = [vab[0:64, sb16:sb16 + 2048], vab[64:128, sb16:sb16 + 2048]]
            kslot = [vab[0:64, sb16 + 2048:sb16 + 4096],
                     vab[64:128, sb16 + 2048:sb16 + 4096]]
            kq_keys = [sl("wka", sw, 1024)] * 2
            kk_keys = [sl("wka", sw + 1024, 1024)] * 2

        def phase1(qt):
            import os as _os2
            _par = _os2.environ.get("L1_PAR", "qwa")
            pi = (qt & 1)
            pq = pi if "q" in _par else 0
            pw = pi if "w" in _par else 0
            pa = pi if "a" in _par else 0
            q0 = 128 * qt
            klo, kw = _att_window(mode, qt)
            masks = _att_masks(P, mode, qt, klo, kw)
            nc_ = (kw + 511) // 512
            for hh in range(2):
                abuf = bfa[:, (2 * pi + hh) * 2048:(2 * pi + hh + 1) * 2048]
                kab = sl("bfa", (2 * pi + hh) * 2048, 2048)
                st0 = 32 * hh
                kst = f"attst{hh}"
                lhsT = qslot[hh][:, q0:q0 + 128]
                tp_ = (64 * hh, 0) if mode != "cross" else (0, 0)
                for c in range(nc_):
                    cw = min(512, kw - 512 * c)
                    bank = (2 * hh + (c + qt) % 2) if mode != "cross" else ((c + 2 * qt) % 4)
                    ps = P.sb[f"PS{bank}"]
                    kps = f"PS{bank}"
                    S.mm(ps[:, 0:cw], lhsT, kslot[hh][:, klo + 512 * c:klo + 512 * c + cw],
                         True, True, r=kq_keys[hh] + kk_keys[hh], w=[kps],
                         tag=f"{tag}.s{pr}_{qt}_{hh}_{c}", tile_position=tp_)
                    for moff, mwid, msk, mkey in masks:
                        lo, hi = moff, moff + mwid
                        if lo >= 512 * (c + 1) or hi <= 512 * c:
                            continue
                        ml, mh = max(lo, 512 * c), min(hi, 512 * (c + 1))
                        S.vop(lambda v, p_=ps[:, ml - 512 * c:mh - 512 * c],
                              m_=msk[:, ml - lo:mh - lo]:
                              v.tensor_tensor(p_, p_, m_, ALU.add),
                              r=[kps, mkey], w=[kps],
                              tag=f"{tag}.mk{pr}_{qt}_{hh}_{c}")
                    if use_max:
                        S.vop(lambda v, o=att[:, st0 + c:st0 + c + 1], p_=ps[:, 0:cw]:
                              v.reduce_max(o, p_, axis=AX.X),
                              r=[kps], w=[kst], hazard=True,
                              tag=f"{tag}.mx{pr}_{qt}_{hh}_{c}")
                        S.vop(lambda v, o=att[:, st0 + 12:st0 + 13],
                              a=att[:, st0 + c:st0 + c + 1]:
                              v.tensor_scalar(o, a, -scale, None, ALU.mult),
                              r=[kst], w=[kst], hazard=True, tag=f"{tag}.nm{pr}_{qt}_{hh}_{c}")
                        S.act(abuf[:, 512 * c:512 * c + cw], ps[:, 0:cw], AF.Exp,
                              bias=att[:, st0 + 12:st0 + 13], scale=scale,
                              accum_out=att[:, st0 + 4 + c:st0 + 5 + c],
                              r=[kps, kst], w=kab + [kst],
                              tag=f"{tag}.ex{pr}_{qt}_{hh}_{c}")
                    else:
                        S.act(abuf[:, 512 * c:512 * c + cw], ps[:, 0:cw], AF.Exp,
                              scale=scale,
                              accum_out=att[:, st0 + 4 + c:st0 + 5 + c],
                              r=[kps], w=kab + [kst],
                              tag=f"{tag}.ex{pr}_{qt}_{hh}_{c}")
                if nc_ == 1:
                    S.vop(lambda v, o=att[:, st0 + 8:st0 + 9],
                          a=att[:, st0 + 4:st0 + 5]: v.reciprocal(o, a),
                          r=[kst], w=[kst], hazard=True,
                          tag=f"{tag}.rc{pr}_{qt}_{hh}")
                else:
                    S.vop(lambda v, o=att[:, st0 + 14:st0 + 15],
                          a=att[:, st0 + 4:st0 + 4 + nc_]:
                          v.reduce_sum(o, a, axis=AX.X),
                          r=[kst], w=[kst], hazard=True, tag=f"{tag}.tt{pr}_{qt}_{hh}")
                    S.vop(lambda v, o=att[:, st0 + 15:st0 + 16],
                          a=att[:, st0 + 14:st0 + 15]: v.reciprocal(o, a),
                          r=[kst], w=[kst], hazard=True, tag=f"{tag}.rc{pr}_{qt}_{hh}")
                scol = st0 + 8 if nc_ == 1 else st0 + 15
                for c in range(nc_):
                    cw = min(512, kw - 512 * c)
                    S.vop(lambda v, o=abuf[:, 512 * c:512 * c + cw],
                          l_=att[:, scol:scol + 1]:
                          v.tensor_scalar(o, o, l_, None, ALU.mult),
                          r=kab + [kst], w=kab, tag=f"{tag}.sc{pr}_{qt}_{hh}_{c}")

        def phase2(qt):
            pi = qt & 1
            klo, kw = _att_window(mode, qt)
            nks = (kw + 127) // 128
            for hh in range(2):
                abuf = bfa[:, (2 * pi + hh) * 2048:(2 * pi + hh + 1) * 2048]
                kab = sl("bfa", (2 * pi + hh) * 2048, 2048)
                atb = bfa[:, 8192 + hh * 2048:8192 + (hh + 1) * 2048]
                kat = sl("bfa", 8192 + hh * 2048, 2048)
                for ks in range(nks):
                    ksw = min(128, kw - 128 * ks)
                    slot = ks % 8
                    S.trans(P.sb["PBF"][0:ksw, slot * 128:slot * 128 + 128],
                            abuf[:, 128 * ks:128 * ks + ksw], ident,
                            r=kab + ["ident_bf"], w=["PBF"],
                            tag=f"{tag}.tp{pr}_{qt}_{hh}_{ks}")
                    if (ks + hh) % 2 == 0:
                        S.add("scalar", lambda e, o=atb[0:ksw, 128 * ks:128 * (ks + 1)],
                              p_=P.sb["PBF"][0:ksw, slot * 128:slot * 128 + 128]:
                              e.copy(o, p_), r=["PBF"], w=kat,
                              tag=f"{tag}.tc{pr}_{qt}_{hh}_{ks}")
                    else:
                        S.vop(lambda v, o=atb[0:ksw, 128 * ks:128 * (ks + 1)],
                              p_=P.sb["PBF"][0:ksw, slot * 128:slot * 128 + 128]:
                              v.tensor_copy(o, p_), r=["PBF"], w=kat,
                              tag=f"{tag}.tc{pr}_{qt}_{hh}_{ks}")
            for ks in range(nks):
                ksw = min(128, kw - 128 * ks)
                kcabs = (klo + 128 * ks) // 128
                for hh in range(2):
                    h = hA + hh
                    atb = bfa[:, 8192 + hh * 2048:8192 + (hh + 1) * 2048]
                    kat = sl("bfa", 8192 + hh * 2048, 2048)
                    vsl = vab[0:ksw, kcabs * 1024 + 64 * h:kcabs * 1024 + 64 * h + 64]
                    S.mm(P.sb["PS4"][64 * hh:64 * hh + 64, 0:128], vsl,
                         atb[0:ksw, 128 * ks:128 * (ks + 1)],
                         ks == 0, ks == nks - 1,
                         r=kv_all + kat, w=["PS4"],
                         tag=f"{tag}.av{pr}_{qt}_{hh}_{ks}",
                         tile_position=(0, 64 * hh))
            if qt % 2 == 0:
                S.add("scalar", lambda e, o=bfb[:, qt * 128:(qt + 1) * 128],
                      p_=P.sb["PS4"]: e.copy(o, p_[:, 0:128]),
                      r=["PS4"], w=sl("bfb", qt * 128, 128), tag=f"{tag}.oe{pr}_{qt}")
            else:
                S.vop(lambda v, o=bfb[:, qt * 128:(qt + 1) * 128],
                      p_=P.sb["PS4"]: v.tensor_copy(o, p_[:, 0:128]),
                      r=["PS4"], w=sl("bfb", qt * 128, 128), tag=f"{tag}.oe{pr}_{qt}")

        phase1(0)
        for qt in range(1, 16):
            phase1(qt)
            phase2(qt - 1)
        phase2(15)
        S.dma("sync", ao_ap[128 * pr:128 * (pr + 1), :], bfb[:, 0:2048],
              r=sl("bfb", 0, 2048), w=[kao], tag=f"{tag}.ao{pr}")

def emit_qk_prep(S, tag, qkvt_src, qf_dst, kf_dst, layer):
    P = S.P
    qk_ap, kqk = qkvt_src
    dsts = [qf_dst, kf_dst]
    bfa, bfb = P.sb["bfa"], P.sb["bfb"]
    wka, wkb, wkc = P.sb["wka"], P.sb["wkb"], P.sb["wkc"]
    att = P.sb["attst"]
    ident = P.sb["ident_bf"][:]
    for j, rowname in enumerate(["ffn_qs_sb", "ffn_ks_sb"]):
        S.mm(P.sb["PS4"][:, 0:64], P.sb["ones_row_f32"][:],
             P.sb[rowname][0:1, 64 * layer:64 * layer + 64], True, True,
             r=[rowname, "ones_row_f32"], w=["PS4"], tag=f"{tag}.sb{j}")
        S.add("scalar", lambda e, o=wkc[:, 64 * j:64 * (j + 1)], p_=P.sb["PS4"]:
              e.copy(o, p_[:, 0:64]), r=["PS4"], w=sl("wkc", 0, 512),
              tag=f"{tag}.sc{j}")
    for tile in range(16):
        pi = tile & 1
        for j in range(2):
            x_tm = bfa[:, (2 * pi + j) * 1024:(2 * pi + j + 1) * 1024]
            kxt = sl("bfa", (2 * pi + j) * 1024, 1024)
            S.dma("sync", x_tm, qk_ap[tile, :, 1024 * j:1024 * (j + 1)],
                  r=[kqk], w=kxt, tag=f"{tag}.x{tile}_{j}")
            sq = wka[:, (2 * pi + j) * 1024:(2 * pi + j + 1) * 1024]
            ksq = sl("wka", (2 * pi + j) * 1024, 1024)
            t1 = wka[:, 4096 + (2 * pi + j) * 1024:4096 + (2 * pi + j + 1) * 1024]
            kt1 = sl("wka", 4096 + (2 * pi + j) * 1024, 1024)
            qsw = wka[:, 8192 + (2 * pi + j) * 1024:8192 + (2 * pi + j + 1) * 1024]
            kqs = sl("wka", 8192 + (2 * pi + j) * 1024, 1024)
            t2 = wkb[:, (2 * pi + j) * 1024:(2 * pi + j + 1) * 1024]
            kt2 = sl("wkb", (2 * pi + j) * 1024, 1024)
            xrot = bfb[:, (2 * pi + j) * 1024:(2 * pi + j + 1) * 1024]
            kxr = sl("bfb", (2 * pi + j) * 1024, 1024)
            stc = 16 * (2 * pi + j)
            kst = f"attst{2 * pi + j}"
            r3 = lambda a: a.rearrange("p (h d) -> p h d", h=16)
            S.act(sq, x_tm, AF.Square, r=kxt, w=ksq, tag=f"{tag}.sq{tile}_{j}")
            S.vop(lambda v, o=att[:, stc:stc + 16], a=r3(sq):
                  v.reduce_sum(o, a, axis=AX.X), r=ksq, w=[kst], hazard=True,
                  tag=f"{tag}.ss{tile}_{j}")
            S.act(att[:, stc:stc + 16], att[:, stc:stc + 16], AF.Sqrt,
                  r=[kst], w=[kst], hazard=True, tag=f"{tag}.nr{tile}_{j}")
            S.vop(lambda v, o=att[:, stc:stc + 16]:
                  v.tensor_scalar(o, o, 1e-12, None, ALU.max),
                  r=[kst], w=[kst], hazard=True, tag=f"{tag}.mx{tile}_{j}")
            S.vop(lambda v, o=att[:, stc:stc + 16]:
                  v.reciprocal(o, o), r=[kst], w=[kst], hazard=True,
                  tag=f"{tag}.rc{tile}_{j}")
            S.vop(lambda v, o=r3(t1), a=r3(x_tm),
                  b_=att[:, stc:stc + 16].unsqueeze(2).broadcast_to([128, 16, 64]):
                  v.tensor_tensor(o, a, b_, ALU.mult),
                  r=kxt + [kst], w=kt1, tag=f"{tag}.t1_{tile}_{j}")
            S.pop(lambda e, o=r3(t1), a=r3(t1),
                  b_=wkc[:, 64 * j:64 * (j + 1)].unsqueeze(1).broadcast_to([128, 16, 64]):
                  e.tensor_tensor(o, a, b_, ALU.mult),
                  r=kt1 + sl("wkc", 0, 512), w=kt1, tag=f"{tag}.t1b{tile}_{j}")
            S.vop(lambda v, o=r3(qsw)[:, :, 0:32], a=r3(t1)[:, :, 32:64]:
                  v.tensor_scalar(o, a, -1.0, None, ALU.mult),
                  r=kt1, w=kqs, tag=f"{tag}.rh0{tile}_{j}")
            S.vop(lambda v, o=r3(qsw)[:, :, 32:64], a=r3(t1)[:, :, 0:32]:
                  v.tensor_copy(o, a), r=kt1, w=kqs, tag=f"{tag}.rh1{tile}_{j}")
            ctab = P.sb[("rot_c1_sb" if j == 0 else "rot_c2_sb")]
            stab = P.sb[("rot_s1_sb" if j == 0 else "rot_s2_sb")]
            ckey = "rot_c1" if j == 0 else "rot_c2"
            skey = "rot_s1" if j == 0 else "rot_s2"
            S.pop(lambda e, o=r3(t2), a=r3(t1),
                  b_=ctab[:, 64 * tile:64 * (tile + 1)].unsqueeze(1)
                  .broadcast_to([128, 16, 64]):
                  e.tensor_tensor(o, a, b_, ALU.mult),
                  r=kt1 + [ckey], w=kt2, tag=f"{tag}.t2{tile}_{j}")
            S.vop(lambda v, o=r3(qsw), a=r3(qsw),
                  b_=stab[:, 64 * tile:64 * (tile + 1)].unsqueeze(1)
                  .broadcast_to([128, 16, 64]):
                  v.tensor_tensor(o, a, b_, ALU.mult),
                  r=kqs + [skey], w=kqs, tag=f"{tag}.t3{tile}_{j}")
            S.pop(lambda e, o=xrot, a=t2, b_=qsw:
                  e.tensor_tensor(o, a, b_, ALU.add),
                  r=kt2 + kqs, w=kxr, tag=f"{tag}.xr{tile}_{j}")
            dst_ap, kdst = dsts[j]
            for pc in range(8):
                slot = pc % 8
                S.trans(P.sb["PBF"][:, slot * 128:(slot + 1) * 128],
                        xrot[:, pc * 128:(pc + 1) * 128], ident,
                        r=kxr + ["ident_bf"], w=["PBF"],
                        tag=f"{tag}.tp{tile}_{j}_{pc}")
                stg = bfa[:, 4096 + ((2 * pi + j) * 2 + pc % 2) * 128:
                          4096 + ((2 * pi + j) * 2 + pc % 2 + 1) * 128]
                kstg = sl("bfa", 4096 + ((2 * pi + j) * 2 + pc % 2) * 128, 128)
                S.vop(lambda v, o=stg, p_=P.sb["PBF"][:, slot * 128:(slot + 1) * 128]:
                      v.tensor_copy(o, p_), r=["PBF"], w=kstg,
                      tag=f"{tag}.tc{tile}_{j}_{pc}")
                S.dma("sync", dst_ap[128 * pc:128 * (pc + 1), 128 * tile:128 * (tile + 1)],
                      stg, r=kstg, w=[kdst], tag=f"{tag}.st{tile}_{j}_{pc}")


def emit_glu_mul(S, tag, a_src, g_src, out, rows):
    P = S.P
    a_ap, ka = a_src
    g_ap, kg = g_src
    o_ap, ko = out
    bfa, bfb = P.sb["bfa"], P.sb["bfb"]
    GR = 512
    for t0 in range(0, T, TC):
        for r0 in range(0, rows, GR):
            rw = min(GR, rows - r0)
            nr = rw // 128
            pi = (t0 // TC) & 1
            a_sb = bfa[:, pi * 4096:pi * 4096 + nr * TC]
            g_sb = bfa[:, 8192 + pi * 4096:8192 + pi * 4096 + nr * TC]
            o_sb = bfb[:, pi * 2048:pi * 2048 + nr * TC]
            ka_ = sl("bfa", pi * 4096, nr * TC)
            kg_ = sl("bfa", 8192 + pi * 4096, nr * TC)
            ko_ = sl("bfb", pi * 2048, nr * TC)
            S.dma("sync", a_sb.rearrange("p (c t) -> p c t", c=nr),
                  a_ap[r0:r0 + rw, t0:t0 + TC].rearrange("(c p) t -> p c t", p=128),
                  r=[ka], w=ka_, tag=f"{tag}.a{t0}_{r0}")
            S.dma("sync", g_sb.rearrange("p (c t) -> p c t", c=nr),
                  g_ap[r0:r0 + rw, t0:t0 + TC].rearrange("(c p) t -> p c t", p=128),
                  r=[kg], w=kg_, tag=f"{tag}.g{t0}_{r0}")
            S.pop(lambda e, o=o_sb, a=a_sb, b_=g_sb:
                  e.tensor_tensor(o, a, b_, ALU.mult),
                  r=ka_ + kg_, w=ko_, tag=f"{tag}.m{t0}_{r0}")
            S.dma("gpsimd", o_ap[r0:r0 + rw, t0:t0 + TC].rearrange("(c p) t -> p c t", p=128),
                  o_sb.rearrange("p (c t) -> p c t", c=nr),
                  r=ko_, w=[ko], tag=f"{tag}.o{t0}_{r0}")


def emit_l1_decay(S, tag, x_src, ldrow_slot=0):
    P = S.P
    x_ap, kx = x_src
    bfa = P.sb["bfa"]
    pra = P.sb["PRA"]
    rows = P.sb["rows"]
    wd = P.sb["wkc"][:, 1024:1032].bitcast(BF16)
    S.dma("gpsimd", wd[:, 0:8], P.input("l1a_wd")[:, :],
          r=["l1a_wd"], w=sl("wkc", 1024, 8), tag=f"{tag}.wd")
    for i in range(NTC):
        pi = i & 1
        t0 = i * TC
        xbf = bfa[:, pi * 4096:(pi + 1) * 4096]
        kxb = sl("bfa", pi * 4096, 4096)
        S.dma("sync", xbf.rearrange("p (c t) -> p c t", c=ND),
              x_ap[:, t0:t0 + TC].rearrange("(c p) t -> p c t", p=128),
              r=[kx], w=kxb, tag=f"{tag}.x{i}")
        for c in range(ND):
            S.mm(pra[:, :], wd[:, c:c + 1], xbf[:, c * TC:(c + 1) * TC],
                 c == 0, c == ND - 1, r=sl("wkc", 1024, 8) + kxb, w=["PRA"],
                 tag=f"{tag}.mm{i}")
        S.act(rows[0:1, 0:TC], pra[:, :], AF.Sigmoid,
              bias=P.sb["l1a_bd_sb"][0:1, :],
              r=["PRA", "l1a_bd_sb"], w=["rows"], hazard=True, tag=f"{tag}.sg{i}")
        S.act(rows[0:1, TC:2 * TC], rows[0:1, 0:TC], AF.Ln,
              bias=P.sb["eps8_col"][0:1, :],
              r=["rows", "eps8_col"], w=["rows"], hazard=True, tag=f"{tag}.ln{i}")
        S.dma("sync", P.dram["LDROW"][0:1, t0:t0 + TC], rows[0:1, TC:2 * TC],
              r=["rows"], w=["LDROW"], tag=f"{tag}.st{i}")


def emit_l1_core(S, tag, qf_src, kf_src, vt_src, out_o):
    P = S.P
    qf_ap, kqf = qf_src
    kf_ap, kkf = kf_src
    vt_ap, kvt = vt_src
    o_ap, ko = out_o
    bfa, wkb, wkc = P.sb["bfa"], P.sb["wkb"], P.sb["wkc"]
    vab = P.sb["wka"][:].bitcast(BF16)
    ident = P.sb["ident_bf"][:]
    rows = P.sb["rows"]
    S.pop(lambda e: e.iota(wkb[:, 0:2432], pattern=[[1, 2432]], base=-385,
                           channel_multiplier=-1,
                           allow_small_or_imprecise_dtypes=True),
          w=sl("wkb", 0, 2432), tag=f"{tag}.iom")
    kiom = sl("wkb", 0, 2432)
    S.dma("sync", rows[0:1, 0:T], P.dram["LDROW"][0:1, :], r=["LDROW"],
          w=["rows"], tag=f"{tag}.ld")
    for kc in range(16):
        S.dma("sync", vab[:, kc * 1024:(kc + 1) * 1024], vt_ap[kc, :, :],
              r=[kvt], w=sl("wka", kc * 512, 512), tag=f"{tag}.v{kc}")
    kv_all = sl("wka", 0, 8192)

    for kc in range(4):
        kbase = 512 * kc
        S.dma("sync", bfa[:, 0:4096].rearrange("p (c t) -> p c t", c=ND),
              kf_ap[:, kbase:kbase + 512].rearrange("(c p) t -> p c t", p=128),
              r=[kkf], w=sl("bfa", 0, 4096), tag=f"{tag}.k{kc}")
        kkc = sl("bfa", 0, 4096)
        S.mm(P.sb["PS4"][:], P.sb["ones_row_f32"][:], rows[0:1, kbase:kbase + 512],
             True, True, r=["rows", "ones_row_f32"], w=["PS4"], tag=f"{tag}.lb{kc}")
        S.add("scalar", lambda e, o=wkb[:, 2560:3072], p_=P.sb["PS4"]:
              e.copy(o, p_[:]), r=["PS4"], w=sl("wkb", 2560, 512),
              tag=f"{tag}.lc{kc}")
        kldb = sl("wkb", 2560, 512)
        def phase1(qt):
            import os as _os2
            _par = _os2.environ.get("L1_PAR", "qwa")
            pi = (qt & 1)
            pq = pi if "q" in _par else 0
            pw = pi if "w" in _par else 0
            pa = pi if "a" in _par else 0
            q0 = 128 * qt
            jlo = max(0, q0 - kbase)
            cw = 512 - jlo
            qt_sb = bfa[:, 4096 + pq * 1024:5120 + pq * 1024]
            kqt = sl("bfa", 4096 + pq * 1024, 1024)
            S.dma("sync", qt_sb.rearrange("p (c t) -> p c t", c=ND),
                  qf_ap[:, q0:q0 + 128].rearrange("(c p) t -> p c t", p=128),
                  r=[kqf], w=kqt, tag=f"{tag}.q{kc}_{qt}")
            ps = P.sb[f"PS{qt % 2}"]
            kps = f"PS{qt % 2}"
            for c in range(ND):
                S.mm(ps[:, 0:cw], qt_sb[:, c * 128:(c + 1) * 128],
                     bfa[:, c * 512 + jlo:c * 512 + 512], c == 0, c == ND - 1,
                     r=kqt + kkc, w=[kps], tag=f"{tag}.s{kc}_{qt}")

        def phase2(qt):
            import os as _os2
            _par = _os2.environ.get("L1_PAR", "qwa")
            pi = (qt & 1)
            pq = pi if "q" in _par else 0
            pw = pi if "w" in _par else 0
            pa = pi if "a" in _par else 0
            q0 = 128 * qt
            jlo = max(0, q0 - kbase)
            cw = 512 - jlo
            ps = P.sb[f"PS{qt % 2}"]
            kps = f"PS{qt % 2}"
            ioff = kbase + jlo - q0 + 384
            wt = wkb[:, 3072 + pw * 512:3072 + pw * 512 + cw]
            kwt = sl("wkb", 3072 + pw * 512, cw)
            S.vop(lambda v, o=wt, a=wkb[:, ioff:ioff + cw], b_=wkb[:, 2560 + jlo:3072]:
                  v.scalar_tensor_tensor(o, a, 0.0, b_, ALU.max, ALU.mult),
                  r=kiom + kldb, w=kwt, tag=f"{tag}.w{kc}_{qt}")
            S.act(wt, wt, AF.Exp, r=kwt, w=kwt, tag=f"{tag}.e{kc}_{qt}")
            import os as _os
            _d = _os.environ.get("L1_DBG", "")
            if _d and qt == 0 and kc == 3:
                if "w" in _d:
                    S.dma("sync", P.dram["DBGW"][:, 0:cw], wt, r=kwt, w=["DBGW"],
                          tag=f"{tag}.dbgw")
                if "l" in _d:
                    S.dma("sync", P.dram["DBGL"][:, 0:512], wkb[:, 2560:3072],
                          r=kldb, w=["DBGW"], tag=f"{tag}.dbgl")
                if "i" in _d:
                    S.dma("sync", P.dram["DBGI"][:, 0:cw], wkb[:, ioff:ioff + cw],
                          r=kiom, w=["DBGW"], tag=f"{tag}.dbgi")
            abf = P.sb["bfb"][:, pa * 512:pa * 512 + cw]
            kab = sl("bfb", pa * 512, cw)
            S.vop(lambda v, o=abf, p_=ps[:, 0:cw], b_=wt:
                  v.tensor_tensor(o, p_, b_, ALU.mult),
                  r=[kps] + kwt, w=kab, tag=f"{tag}.a{kc}_{qt}")
            if kbase <= q0 < kbase + 512:
                S.pop(lambda e, o=abf[:, 0:128], m=P.sb["m01_upper"][:]:
                      e.tensor_tensor(o, o, m, ALU.mult),
                      r=kab + ["m01_upper"], w=kab, tag=f"{tag}.dm{kc}_{qt}")
            nks = cw // 128
            for ks in range(nks):
                slot = ks % 8
                S.trans(P.sb["PBF"][:, slot * 128:(slot + 1) * 128],
                        abf[:, ks * 128:(ks + 1) * 128], ident,
                        r=kab + ["ident_bf"], w=["PBF"], tag=f"{tag}.tp{kc}_{qt}_{ks}")
                S.vop(lambda v, o=P.sb["bfb"][:, 2048 + ks * 128:2048 + (ks + 1) * 128],
                      p_=P.sb["PBF"][:, slot * 128:(slot + 1) * 128]:
                      v.tensor_copy(o, p_), r=["PBF"], w=sl("bfb", 2048 + ks * 128, 128),
                      tag=f"{tag}.tc{kc}_{qt}_{ks}")
            kat = sl("bfb", 2048, cw)
            for o in range(ND):
                pso = P.sb[f"PS{2 + o % 2}"]
                kpso = f"PS{2 + o % 2}"
                for ks in range(nks):
                    kt_abs = (kbase + jlo) // 128 + ks
                    vsl = vab[:, kt_abs * 1024 + o * 128:kt_abs * 1024 + (o + 1) * 128]
                    S.mm(pso[:, 0:128], vsl,
                         P.sb["bfb"][:, 2048 + ks * 128:2048 + (ks + 1) * 128],
                         ks == 0, ks == nks - 1, r=kv_all + kat, w=[kpso],
                         tag=f"{tag}.av{kc}_{qt}_{o}")
                S.vop(lambda v, oo=wkb[:, 4096 + o * 128:4096 + (o + 1) * 128],
                      p_=pso: v.tensor_copy(oo, p_[:, 0:128]),
                      r=[kpso], w=sl("wkb", 4096 + o * 128, 128),
                      tag=f"{tag}.oe{kc}_{qt}_{o}")
            first = (kc == (q0 // 512))
            S.dma("gpsimd", o_ap[:, q0:q0 + 128].rearrange("(c p) t -> p c t", p=128),
                  wkb[:, 4096:5120].rearrange("p (c t) -> p c t", c=ND),
                  r=sl("wkb", 4096, 1024), w=[ko], tag=f"{tag}.oa{kc}_{qt}",
                  **({} if first else {"accum_op": ALU.add}))

        qts = [q for q in range(4 * kc + 4) if q < 16]
        import os as _os
        if _os.environ.get("L1_NOPIPE"):
            for q in qts:
                phase1(q)
                phase2(q)
        elif qts:
            phase1(qts[0])
            for idx in range(1, len(qts)):
                phase1(qts[idx])
                phase2(qts[idx - 1])
            phase2(qts[-1])


def emit_moe_router(S, tag, x_src, cwf_dst):
    P = S.P
    x_ap, kx = x_src
    cw_ap, kcw = cwf_dst
    bfa, wkc = P.sb["bfa"], P.sb["wkc"]
    att = P.sb["attst"]
    wr = wkc[:, 1024:1088].bitcast(BF16)
    S.dma("gpsimd", wr[:, 0:128].rearrange("p (c o) -> p c o", c=ND),
          P.input("moe_wr").rearrange("(c p) o -> p c o", p=128),
          r=["moe_wr"], w=sl("wkc", 1024, 64), tag=f"{tag}.wr")
    kwr = sl("wkc", 1024, 64)
    for i in range(NTC):
        pi = i & 1
        t0 = i * TC
        xbf = bfa[:, pi * 4096:(pi + 1) * 4096]
        kxb = sl("bfa", pi * 4096, 4096)
        S.dma("sync", xbf.rearrange("p (c t) -> p c t", c=ND),
              x_ap[:, t0:t0 + TC].rearrange("(c p) t -> p c t", p=128),
              r=[kx], w=kxb, tag=f"{tag}.x{i}")
        for tt in range(4):
            ps = P.sb[f"PS{tt % 2}"]
            kps = f"PS{tt % 2}"
            for c in range(ND):
                S.mm(ps[:, 0:16], xbf[:, c * TC + tt * 128:c * TC + (tt + 1) * 128],
                     wr[:, c * 16:(c + 1) * 16], c == 0, c == ND - 1,
                     r=kxb + kwr, w=[kps], tag=f"{tag}.mm{i}_{tt}")
            st0 = 16 * (tt % 2)
            kst = f"attst{tt % 2}"
            probs = wkc[:, (tt % 2) * 512:(tt % 2) * 512 + 16]
            kpb = sl("wkc", (tt % 2) * 512, 16)
            S.vop(lambda v, o=att[:, st0:st0 + 1], p_=ps[:, 0:16]:
                  v.reduce_max(o, p_, axis=AX.X), r=[kps], w=[kst], hazard=True,
                  tag=f"{tag}.m1{i}_{tt}")
            S.vop(lambda v, o=att[:, st0 + 1:st0 + 2], a=att[:, st0:st0 + 1]:
                  v.tensor_scalar(o, a, -1.0, None, ALU.mult),
                  r=[kst], w=[kst], hazard=True, tag=f"{tag}.ng{i}_{tt}")
            S.act(probs, ps[:, 0:16], AF.Exp, bias=att[:, st0 + 1:st0 + 2],
                  accum_out=att[:, st0 + 2:st0 + 3],
                  r=[kps, kst], w=kpb + [kst], tag=f"{tag}.ex{i}_{tt}")
            S.vop(lambda v, o=att[:, st0 + 3:st0 + 4], a=att[:, st0 + 2:st0 + 3]:
                  v.reciprocal(o, a), r=[kst], w=[kst], hazard=True,
                  tag=f"{tag}.rc{i}_{tt}")
            S.vop(lambda v, o=probs, r_=att[:, st0 + 3:st0 + 4]:
                  v.tensor_scalar(o, o, r_, None, ALU.mult),
                  r=kpb + [kst], w=kpb, hazard=True, tag=f"{tag}.nm{i}_{tt}")
            S.vop(lambda v, o=att[:, st0 + 4:st0 + 5], p_=probs:
                  v.reduce_max(o, p_, axis=AX.X), r=kpb, w=[kst], hazard=True,
                  tag=f"{tag}.p1{i}_{tt}")
            tmp = wkc[:, (tt % 2) * 512 + 16:(tt % 2) * 512 + 32]
            ktm = kpb
            S.vop(lambda v, o=tmp, p_=probs, m_=att[:, st0 + 4:st0 + 5]:
                  v.tensor_scalar(o, p_, m_, None, ALU.is_ge),
                  r=kpb + [kst], w=ktm, hazard=True, tag=f"{tag}.ge{i}_{tt}")
            S.vop(lambda v, o=tmp, p_=probs:
                  v.scalar_tensor_tensor(o, o, -2.0, p_, ALU.mult, ALU.add),
                  r=ktm + kpb, w=ktm, hazard=True, tag=f"{tag}.tm{i}_{tt}")
            S.vop(lambda v, o=att[:, st0 + 5:st0 + 6], p_=tmp:
                  v.reduce_max(o, p_, axis=AX.X), r=ktm, w=[kst], hazard=True,
                  tag=f"{tag}.p2{i}_{tt}")
            S.vop(lambda v, o=tmp, p_=probs, m_=att[:, st0 + 5:st0 + 6]:
                  v.tensor_scalar(o, p_, m_, None, ALU.is_ge),
                  r=kpb + [kst], w=ktm, hazard=True, tag=f"{tag}.g2{i}_{tt}")
            S.vop(lambda v, o=probs, m_=tmp:
                  v.tensor_tensor(o, o, m_, ALU.mult),
                  r=kpb + ktm, w=kpb, hazard=True, tag=f"{tag}.cw{i}_{tt}")
            S.trans(P.sb["PS2"][0:16, 0:128], probs, P.sb["ident_f32"][:],
                    r=kpb + ["ident_f32"], w=["PS2"], tag=f"{tag}.tp{i}_{tt}")
            S.add("scalar", lambda e, o=wkc[0:16, 1536:1664],
                  p_=P.sb["PS2"][0:16, 0:128]: e.copy(o, p_),
                  r=["PS2"], w=sl("wkc", 1536, 128), tag=f"{tag}.cp{i}_{tt}")
            S.dma("sync", cw_ap[:, t0 + tt * 128:t0 + (tt + 1) * 128],
                  wkc[0:16, 1536:1664], r=sl("wkc", 1536, 128), w=[kcw],
                  tag=f"{tag}.st{i}_{tt}")


def emit_moe_experts(S, tag, x_src, acc_dram):
    P = S.P
    x_ap, kx = x_src
    a_ap, kacc = acc_dram
    bfa, bfb, wka, wkb, wkc, wgt = (P.sb[n] for n in
                                    ["bfa", "bfb", "wka", "wkb", "wkc", "wgt"])
    vab = wka[:].bitcast(BF16)
    for tci in range(NTC):
        t0 = tci * TC
        S.dma("sync", wkb[:, 0:4096].rearrange("p (c t) -> p c t", c=ND),
              a_ap[:, t0:t0 + TC].rearrange("(c p) t -> p c t", p=128),
              r=[kacc], w=sl("wkb", 0, 4096), tag=f"{tag}.acc{tci}")
        kya = sl("wkb", 0, 4096)
        S.dma("sync", bfa[:, 0:4096].rearrange("p (c t) -> p c t", c=ND),
              x_ap[:, t0:t0 + TC].rearrange("(c p) t -> p c t", p=128),
              r=[kx], w=sl("bfa", 0, 4096), tag=f"{tag}.x{tci}")
        kxb = sl("bfa", 0, 4096)
        for e in range(16):
            S.dma("gpsimd", wgt[:, 0:4096].rearrange("p (c o) -> p c o", c=ND),
                  P.input("moe_wg")[e].rearrange("(c p) o -> p c o", p=128),
                  r=["moe_wg"], w=sl("wgt", 0, 4096), tag=f"{tag}.wg{tci}_{e}")
            S.dma("gpsimd", wgt[:, 4096:8192].rearrange("p (c o) -> p c o", c=ND),
                  P.input("moe_wu")[e].rearrange("(c p) o -> p c o", p=128),
                  r=["moe_wu"], w=sl("wgt", 4096, 4096), tag=f"{tag}.wu{tci}_{e}")
            S.dma("gpsimd", vab[:, 8192:12288].rearrange("p (c o) -> p c o", c=4),
                  P.input("moe_wd")[e].rearrange("(c p) o -> p c o", p=128),
                  r=["moe_wd"], w=sl("wka", 4096, 2048), tag=f"{tag}.wd{tci}_{e}")
            kwg = sl("wgt", 0, 4096)
            kwu = sl("wgt", 4096, 4096)
            kwd = sl("wka", 4096, 2048)
            S.dma("sync", P.sb["rows"][0:1, 0:TC], P.dram["CWF"][e:e + 1, t0:t0 + TC],
                  r=["CWF"], w=["rows"], tag=f"{tag}.cw{tci}_{e}")
            S.mm(P.sb["PS4"][:], P.sb["ones_row_f32"][:], P.sb["rows"][0:1, 0:TC],
                 True, True, r=["rows", "ones_row_f32"], w=["PS4"],
                 tag=f"{tag}.cb{tci}_{e}")
            S.add("scalar", lambda e2, o=wkc[:, 0:512], p_=P.sb["PS4"]:
                  e2.copy(o, p_[:]), r=["PS4"], w=sl("wkc", 0, 512),
                  tag=f"{tag}.cc{tci}_{e}")
            kcwb = sl("wkc", 0, 512)
            for f in range(4):
                psg = P.sb["PS0"]
                psu = P.sb["PS1"]
                for c in range(ND):
                    S.mm(psg[:], wgt[:, c * 512 + f * 128:c * 512 + (f + 1) * 128],
                         bfa[:, c * TC:(c + 1) * TC], c == 0, c == ND - 1,
                         r=kwg + kxb, w=["PS0"], tag=f"{tag}.hg{tci}_{e}_{f}")
                for c in range(ND):
                    S.mm(psu[:], wgt[:, 4096 + c * 512 + f * 128:4096 + c * 512 + (f + 1) * 128],
                         bfa[:, c * TC:(c + 1) * TC], c == 0, c == ND - 1,
                         r=kwu + kxb, w=["PS1"], tag=f"{tag}.hu{tci}_{e}_{f}")
                ssb = bfb[:, 2048 + f * 512:2048 + (f + 1) * 512]
                kss = sl("bfb", 2048 + f * 512, 512)
                S.act(ssb, psg[:], AF.Silu, r=["PS0"], w=kss,
                      tag=f"{tag}.si{tci}_{e}_{f}")
                t1b = wkb[:, 4096 + f * 512:4096 + (f + 1) * 512]
                kt1 = sl("wkb", 4096 + f * 512, 512)
                S.vop(lambda v, o=t1b, a=ssb, p_=psu:
                      v.tensor_tensor(o, a, p_[:], ALU.mult),
                      r=kss + ["PS1"], w=kt1, tag=f"{tag}.pr{tci}_{e}_{f}")
                hb = bfb[:, f * 512:(f + 1) * 512]
                khb = sl("bfb", f * 512, 512)
                S.pop(lambda e2, o=hb, a=t1b, b_=wkc[:, 0:512]:
                      e2.tensor_tensor(o, a, b_, ALU.mult),
                      r=kt1 + kcwb, w=khb, tag=f"{tag}.hc{tci}_{e}_{f}")
            kh = sl("bfb", 0, 2048)
            for o in range(ND):
                pso = P.sb[f"PS{2 + o % 2}"]
                kpso = f"PS{2 + o % 2}"
                for f in range(4):
                    S.mm(pso[:], vab[:, 8192 + f * 1024 + o * 128:8192 + f * 1024 + (o + 1) * 128],
                         bfb[:, f * 512:(f + 1) * 512], f == 0, f == 3,
                         r=kwd + kh, w=[kpso], tag=f"{tag}.y{tci}_{e}_{o}")
                S.vop(lambda v, oo=wkb[:, o * 512:(o + 1) * 512], p_=pso:
                      v.tensor_tensor(oo, oo, p_[:], ALU.add),
                      r=kya + [kpso], w=kya, tag=f"{tag}.ya{tci}_{e}_{o}")
        S.dma("sync", a_ap[:, t0:t0 + TC].rearrange("(c p) t -> p c t", p=128),
              wkb[:, 0:4096].rearrange("p (c t) -> p c t", c=ND),
              r=kya, w=[kacc], tag=f"{tag}.st{tci}")


# version 04 (final)
# speedup vs baseline: 1.0936x; 1.0936x over previous
import numpy as np
import ml_dtypes
from contextlib import ExitStack

import concourse.bass as bass
import concourse.mybir as mybir
from concourse.bass_utils import run_bass_kernel_spmd

F32 = mybir.dt.float32
BF16 = mybir.dt.bfloat16
AF = mybir.ActivationFunctionType
ALU = mybir.AluOpType
AX = mybir.AxisListType

D = 1024
T = 2048
BC = 2
TC = 512
NTC = T // TC
ND = D // 128
EPS = 1e-5
NEG = -1.0e30
GELU_C = 0.7978845608028654

FFN_DEPTH = 6
WINDOW = 256
DIM_INNER_PAD = 2816
W1_COLS = 2 * DIM_INNER_PAD


def _blob_entries():
    e = [("l0a_ln1_g", 8), ("l0a_ln1_b", 8), ("l0a_c_attn_b", 24),
         ("l0a_attn_proj_b", 8), ("l0a_lnx_g", 8), ("l0a_lnx_b", 8),
         ("l0a_q_attn_b", 8), ("l0a_x_kv_b", 16), ("l0a_x_proj_b", 8),
         ("l0a_ln2_g", 8), ("l0a_ln2_b", 8), ("l0a_fc_b", 32),
         ("l0a_mlp_proj_b", 8)]
    for i in range(FFN_DEPTH):
        e += [(f"ffn{i}_ln_g", 8), (f"ffn{i}_ln_b", 8),
              (f"ffn{i}_b1", W1_COLS // 128), (f"ffn{i}_b2", 8)]
    return e

BLOB_ENTRIES = _blob_entries()
BLOB_INDEX = {}
_c = 0
for _n, _k in BLOB_ENTRIES:
    BLOB_INDEX[_n] = _c
    _c += _k
BLOB_COLS = _c

ROWG_ENTRIES = (["l0a_ln1_g", "l0a_lnx_g", "l0a_ln2_g"]
                + [f"ffn{i}_ln_g" for i in range(FFN_DEPTH)])
ROWG_INDEX = {n: i * D for i, n in enumerate(ROWG_ENTRIES)}
ROWG_COLS = D * len(ROWG_ENTRIES)


def host_constants(inp):
    f32 = np.float32
    bf = ml_dtypes.bfloat16
    c = {}
    c["encT"] = np.ascontiguousarray(
        np.asarray(inp["L0_attn_encoder_hidden_states"], dtype=f32)[0].T)
    c["scorer"] = np.ascontiguousarray(
        np.asarray(inp["L0_attn_scorer_logits"], dtype=f32).reshape(1, T))

    blob = np.zeros((128, BLOB_COLS), dtype=f32)

    def put(name, vec):
        k = BLOB_INDEX[name]
        v = np.asarray(vec, dtype=f32).reshape(-1)
        blob[:, k:k + v.size // 128] = v.reshape(-1, 128).T

    put("l0a_ln1_g", inp["L0_attn_ln1_g"]); put("l0a_ln1_b", inp["L0_attn_ln1_b"])
    put("l0a_c_attn_b", inp["L0_attn_c_attn_b"])
    put("l0a_attn_proj_b", inp["L0_attn_attn_proj_b"])
    put("l0a_lnx_g", inp["L0_attn_lnx_g"]); put("l0a_lnx_b", inp["L0_attn_lnx_b"])
    put("l0a_q_attn_b", inp["L0_attn_q_attn_b"])
    put("l0a_x_kv_b", inp["L0_attn_x_kv_b"])
    put("l0a_x_proj_b", inp["L0_attn_x_proj_b"])
    put("l0a_ln2_g", inp["L0_attn_ln2_g"]); put("l0a_ln2_b", inp["L0_attn_ln2_b"])
    put("l0a_fc_b", inp["L0_attn_fc_b"])
    put("l0a_mlp_proj_b", inp["L0_attn_mlp_proj_b"])
    b1 = np.asarray(inp["L0_ffn_b1"], dtype=f32)
    for i in range(FFN_DEPTH):
        put(f"ffn{i}_ln_g", inp["L0_ffn_ln_g"][i])
        put(f"ffn{i}_ln_b", inp["L0_ffn_ln_b"][i])
        b1p = np.zeros(W1_COLS, dtype=f32)
        b1p[0:2730] = b1[i, 0:2730]
        b1p[DIM_INNER_PAD:DIM_INNER_PAD + 2730] = b1[i, 2730:5460]
        put(f"ffn{i}_b1", b1p)
        put(f"ffn{i}_b2", inp["L0_ffn_b2"][i])
    c["blob"] = blob

    for nm, key in [("l0a_c_attn_w", "L0_attn_c_attn_w"),
                    ("l0a_attn_proj_w", "L0_attn_attn_proj_w"),
                    ("l0a_q_attn_w", "L0_attn_q_attn_w"),
                    ("l0a_x_kv_w", "L0_attn_x_kv_w"),
                    ("l0a_x_proj_w", "L0_attn_x_proj_w"),
                    ("l0a_fc_w", "L0_attn_fc_w"),
                    ("l0a_mlp_proj_w", "L0_attn_mlp_proj_w"),
                    ("ffn_wqkv", "L0_ffn_Wqkv"), ("ffn_wo", "L0_ffn_Wo"),
                    ("moe_wr", "L1_ffn_Wr"), ("moe_wg", "L1_ffn_Wg"),
                    ("moe_wu", "L1_ffn_Wu"), ("moe_wd", "L1_ffn_Wd"),
                    ("moe_sg", "L1_ffn_Sg"), ("moe_su", "L1_ffn_Su"),
                    ("moe_sd", "L1_ffn_Sd")]:
        c[nm] = np.asarray(inp[key], dtype=f32)

    w1 = np.asarray(inp["L0_ffn_W1"], dtype=f32)
    w1p = np.zeros((FFN_DEPTH, D, W1_COLS), dtype=f32)
    w1p[:, :, 0:2730] = w1[:, :, 0:2730]
    w1p[:, :, DIM_INNER_PAD:DIM_INNER_PAD + 2730] = w1[:, :, 2730:5460]
    c["ffn_w1"] = w1p
    w2 = np.asarray(inp["L0_ffn_W2"], dtype=f32)
    w2p = np.zeros((FFN_DEPTH, DIM_INNER_PAD, D), dtype=f32)
    w2p[:, 0:2730, :] = w2
    c["ffn_w2"] = w2p
    c["ffn_qs"] = np.ascontiguousarray(
        np.asarray(inp["L0_ffn_q_scale"], dtype=f32).reshape(1, -1))
    c["ffn_ks"] = np.ascontiguousarray(
        np.asarray(inp["L0_ffn_k_scale"], dtype=f32).reshape(1, -1))

    c["l1a_wq"] = np.ascontiguousarray(np.asarray(inp["L1_attn_Wq"], dtype=f32).T)
    c["l1a_wk"] = np.ascontiguousarray(np.asarray(inp["L1_attn_Wk"], dtype=f32).T)
    c["l1a_wv"] = np.ascontiguousarray(np.asarray(inp["L1_attn_Wv"], dtype=f32).T)
    c["l1a_wo"] = np.ascontiguousarray(np.asarray(inp["L1_attn_Wo"], dtype=f32).T)
    c["l1a_wd"] = np.ascontiguousarray(
        np.asarray(inp["L1_attn_Wd"], dtype=f32).reshape(8, 128).T)
    c["l1a_bd"] = np.asarray(inp["L1_attn_bd"], dtype=f32).reshape(1, 1)
    c["l1a_os"] = np.asarray(inp["L1_attn_out_scale"], dtype=f32).reshape(1, 1)

    c["ident_bf"] = np.eye(128, dtype=bf)
    c["ident_f32"] = np.eye(128, dtype=f32)
    ii = np.arange(128)[:, None]; jj = np.arange(128)[None, :]
    c["m_causal"] = np.where(jj > ii, NEG, 0.0).astype(f32)
    c["m_tril2"] = np.where(jj > ii + 2, NEG, 0.0).astype(f32)
    c["m_lower"] = np.where(jj < ii, NEG, 0.0).astype(f32)
    c["m01_upper"] = (jj > ii).astype(bf)
    me = np.zeros((128, 2), dtype=f32)
    me[0:126, 0] = NEG
    me[0:127, 1] = NEG
    c["m_edge2"] = me
    rb = np.zeros((1, 2048), dtype=f32)
    rb[0, 0:1024] = np.asarray(inp["L0_attn_c_attn_b"], dtype=f32)[2048:3072]
    rb[0, 1024:2048] = np.asarray(inp["L0_attn_x_kv_b"], dtype=f32)[1024:2048]
    c["rowbias"] = rb

    half = np.arange(0, 64, 2, dtype=np.float64)
    inv_freq = 1.0 / (10000.0 ** (half / 64.0))
    t = np.arange(T, dtype=np.float64)
    freqs = np.concatenate([t[:, None] * inv_freq[None, :]] * 2, -1)
    base_scale = (half + 0.4 * 64) / (1.4 * 64)
    power = (t - T // 2) / (WINDOW // 2)
    scale = np.concatenate([base_scale[None, :] ** power[:, None]] * 2, -1)
    c["rot_c1"] = (np.cos(freqs) * scale).astype(f32)
    c["rot_s1"] = (np.sin(freqs) * scale).astype(f32)
    c["rot_c2"] = (np.cos(freqs) / scale).astype(f32)
    c["rot_s2"] = (np.sin(freqs) / scale).astype(f32)
    return c


class Sem:
    def __init__(self, es, nc, name):
        self.h = es.enter_context(nc.semaphore(name))
        self.n = 0

    def inc(self, instr, k=1):
        instr.then_inc(self.h, k)
        self.n += k
        return self.n


class Prog:
    def __init__(self, taps=(), upto=None):
        self.nc = bass.Bass("TRN2", target_bir_lowering=False)
        self.es = ExitStack()
        self.taps = list(taps)
        self.upto = upto
        self.inputs = {}
        self.sb = {}
        self.dram = {}
        self.sems = {}
        self.uid = 0

    def inp(self, name, shape, dt=F32):
        ap = self.nc.dram_tensor(name, list(shape), dt, kind="ExternalInput").ap()
        self.inputs[name] = ap
        return ap

    def input(self, name):
        if name not in self.inputs:
            shape, dt = INPUT_SHAPES[name]
            self.inp(name, shape, dt)
        return self.inputs[name]

    def scratch(self, name, shape, dt=F32):
        ap = self.nc.dram_tensor(name, list(shape), dt, kind="Internal").ap()
        self.dram[name] = ap
        return ap

    def out(self, name, shape, dt=F32):
        return self.nc.dram_tensor(name, list(shape), dt, kind="ExternalOutput").ap()

    def sbuf(self, name, shape, dt=F32):
        t = self.es.enter_context(self.nc.sbuf_tensor("sb_" + name, list(shape), dt))
        self.sb[name] = t
        return t

    def psum(self, name, shape, dt=F32):
        t = self.es.enter_context(self.nc.psum_tensor(name, list(shape), dt))
        self.sb[name] = t
        return t

    def sem(self, name):
        if name not in self.sems:
            self.sems[name] = Sem(self.es, self.nc, name)
        return self.sems[name]

    def sems_list(self, *names):
        return [self.sem(n) for n in names]


def blob_col(P, name, k=0):
    cidx = BLOB_INDEX[name] + k
    return P.sb["blob"][:, cidx:cidx + 1]


def fmv(ap, t0, tw=TC):
    return ap[:, t0:t0 + tw].rearrange("(c p) t -> p c t", p=128)



def alloc_common(P):
    P.sbuf("blob", [128, BLOB_COLS], F32)
    P.sbuf("ident_bf", [128, 128], BF16)
    P.sbuf("ident_f32", [128, 128], F32)
    P.sbuf("m_causal", [128, 128], F32)
    P.sbuf("m_tril2", [128, 128], F32)
    P.sbuf("m_lower", [128, 128], F32)
    P.sbuf("m01_upper", [128, 128], BF16)
    P.sbuf("m_edge2", [128, 2], F32)
    P.sbuf("rowbias_sb", [1, 2048], F32)
    P.sbuf("attst", [128, 64], F32)
    P.sbuf("scorer_sb", [1, T], F32)
    P.sbuf("ffn_qs_sb", [1, FFN_DEPTH * 64], F32)
    P.sbuf("ffn_ks_sb", [1, FFN_DEPTH * 64], F32)
    P.sbuf("l1a_bd_sb", [1, 1], F32)
    P.sbuf("l1a_os_sb", [1, 1], F32)
    for nm in ["rot_c1", "rot_s1", "rot_c2", "rot_s2"]:
        P.sbuf(nm + "_sb", [128, 16 * 64], BF16)
    P.sbuf("ones_row_bf", [1, 128], BF16)
    P.sbuf("ones_row_f32", [1, 128], F32)
    P.sbuf("ones_col_bf", [128, 1], BF16)
    P.sbuf("eps_col", [128, 1], F32)
    P.sbuf("eps8_col", [128, 1], F32)
    P.sbuf("rows", [1, 4 * TC], F32)
    P.sbuf("wka", [128, 12288], F32)
    P.sbuf("wkb", [128, 8192], F32)
    P.sbuf("wkc", [128, 2048], F32)
    P.sbuf("bfa", [128, 16384], BF16)
    P.sbuf("bfb", [128, 4096], BF16)
    P.sbuf("wgt", [128, 8192], BF16)
    P.psum("PS0", [128, 512], F32)
    P.psum("PS1", [128, 512], F32)
    P.psum("PS2", [128, 512], F32)
    P.psum("PS3", [128, 512], F32)
    P.psum("PS4", [128, 512], F32)
    P.psum("PBF", [128, 1024], BF16)
    P.psum("PRA", [1, 512], F32)
    P.psum("PRB", [1, 512], F32)


INPUT_SHAPES = {
    "xT": ([BC, D, T], F32), "encT": ([D, T], F32), "scorer": ([1, T], F32),
    "blob": ([128, BLOB_COLS], F32),
    "ident_bf": ([128, 128], BF16), "ident_f32": ([128, 128], F32),
    "m_causal": ([128, 128], F32), "m_tril2": ([128, 128], F32),
    "m_lower": ([128, 128], F32), "m01_upper": ([128, 128], BF16),
    "m_edge2": ([128, 2], F32), "rowbias": ([1, 2048], F32),
    "rot_c1": ([T, 64], F32), "rot_s1": ([T, 64], F32),
    "rot_c2": ([T, 64], F32), "rot_s2": ([T, 64], F32),
    "ffn_qs": ([1, FFN_DEPTH * 64], F32), "ffn_ks": ([1, FFN_DEPTH * 64], F32),
    "l0a_c_attn_w": ([D, 3 * D], F32), "l0a_attn_proj_w": ([D, D], F32),
    "l0a_q_attn_w": ([D, D], F32), "l0a_x_kv_w": ([D, 2 * D], F32),
    "l0a_x_proj_w": ([D, D], F32), "l0a_fc_w": ([D, 4 * D], F32),
    "l0a_mlp_proj_w": ([4 * D, D], F32),
    "ffn_wqkv": ([FFN_DEPTH, D, 3 * D], F32), "ffn_wo": ([FFN_DEPTH, D, D], F32),
    "ffn_w1": ([FFN_DEPTH, D, W1_COLS], F32),
    "ffn_w2": ([FFN_DEPTH, DIM_INNER_PAD, D], F32),
    "l1a_wq": ([D, D], F32), "l1a_wk": ([D, D], F32), "l1a_wv": ([D, D], F32),
    "l1a_wo": ([D, D], F32), "l1a_wd": ([128, 8], F32),
    "l1a_bd": ([1, 1], F32), "l1a_os": ([1, 1], F32),
    "moe_wr": ([D, 16], F32), "moe_wg": ([16, D, 512], F32),
    "moe_wu": ([16, D, 512], F32), "moe_wd": ([16, 512, D], F32),
    "moe_sg": ([D, D], F32), "moe_su": ([D, D], F32), "moe_sd": ([D, D], F32),
}


def _unused_declare_inputs(P):
    P.inp("xT", [BC, D, T])
    P.inp("encT", [D, T])
    P.inp("scorer", [1, T])
    P.inp("blob", [128, BLOB_COLS])
    P.inp("ident_bf", [128, 128], BF16)
    P.inp("ident_f32", [128, 128])
    P.inp("m_causal", [128, 128]); P.inp("m_tril2", [128, 128]); P.inp("m_lower", [128, 128])
    P.inp("m01_upper", [128, 128], BF16)
    P.inp("rot_c1", [T, 64]); P.inp("rot_s1", [T, 64])
    P.inp("rot_c2", [T, 64]); P.inp("rot_s2", [T, 64])
    P.inp("ffn_qs", [1, FFN_DEPTH * 64]); P.inp("ffn_ks", [1, FFN_DEPTH * 64])
    P.inp("l0a_c_attn_w", [D, 3 * D]); P.inp("l0a_attn_proj_w", [D, D])
    P.inp("l0a_q_attn_w", [D, D]); P.inp("l0a_x_kv_w", [D, 2 * D])
    P.inp("l0a_x_proj_w", [D, D]); P.inp("l0a_fc_w", [D, 4 * D])
    P.inp("l0a_mlp_proj_w", [4 * D, D])
    P.inp("ffn_wqkv", [FFN_DEPTH, D, 3 * D]); P.inp("ffn_wo", [FFN_DEPTH, D, D])
    P.inp("ffn_w1", [FFN_DEPTH, D, W1_COLS]); P.inp("ffn_w2", [FFN_DEPTH, DIM_INNER_PAD, D])
    P.inp("l1a_wq", [D, D]); P.inp("l1a_wk", [D, D]); P.inp("l1a_wv", [D, D])
    P.inp("l1a_wo", [D, D]); P.inp("l1a_wd", [D, 1])
    P.inp("l1a_bd", [1, 1]); P.inp("l1a_os", [1, 1])
    P.inp("moe_wr", [D, 16]); P.inp("moe_wg", [16, D, 512]); P.inp("moe_wu", [16, D, 512])
    P.inp("moe_wd", [16, 512, D])
    P.inp("moe_sg", [D, D]); P.inp("moe_su", [D, D]); P.inp("moe_sd", [D, D])


def alloc_scratch(P):
    for b in range(BC):
        P.scratch(f"H{b}", [D, T])
        P.scratch(f"G{b}", [D, T])
        P.scratch(f"X{b}", [D, T])
        P.scratch(f"Y{b}", [D, T])
        P.scratch(f"LN{b}", [D, T], BF16)
        P.scratch(f"QF{b}", [D, T], BF16)
        P.scratch(f"KF{b}", [D, T], BF16)
        P.scratch(f"VT{b}", [16, 128, D], BF16)
        P.scratch(f"AO{b}", [D, T], BF16)
        P.scratch(f"HH{b}", [W1_COLS, T], BF16)
        P.scratch(f"HG{b}", [DIM_INNER_PAD, T], BF16)
        P.scratch(f"BO{b}", [D, T])
    P.scratch("KENC", [D, T], BF16)
    P.scratch("VENC", [16, 128, D], BF16)
    P.scratch("KAUG", [16 * 65, T], BF16)
    for b in range(BC):
        P.scratch(f"QKVT{b}", [16, 128, 3 * D], BF16)
    P.scratch("LDROW", [1, T])
    P.scratch("CWF", [16, T])



ENGINES = ("sync", "gpsimd", "scalar", "vector", "tensor")


class Op:
    __slots__ = ("eng", "fn", "reads", "writes", "hazard", "dma",
                 "deps", "needed", "sem", "tok", "tag", "gate")

    def __init__(self, eng, fn, reads, writes, hazard, dma, tag=""):
        self.eng = eng
        self.fn = fn
        self.reads = tuple(reads)
        self.writes = tuple(writes)
        self.hazard = hazard
        self.dma = dma
        self.deps = []
        self.needed = False
        self.sem = None
        self.tok = 0
        self.tag = tag
        self.gate = None


class Sched:
    def __init__(self, P):
        self.P = P
        self.ops = []

    def add(self, eng, fn, r=(), w=(), hazard=False, dma=False, tag=""):
        op = Op(eng, fn, r, w, hazard, dma, tag)
        self.ops.append(op)
        return op

    def dma(self, q, out, in_, r=(), w=(), tag="", **kw):
        assert q in ("sync", "gpsimd", "scalar")
        return self.add(q, lambda e: e.dma_start(out=out, in_=in_, **kw),
                        r=r, w=w, dma=True, tag=tag)

    def mm(self, out, lhsT, rhs, start, stop, r=(), w=(), tag="", **kw):
        return self.add("tensor",
                        lambda e: e.matmul(out, lhsT, rhs, start=start, stop=stop, **kw),
                        r=r, w=w, tag=tag)

    def trans(self, out, in_, ident, r=(), w=(), tag=""):
        return self.add("tensor", lambda e: e.transpose(out, in_, ident),
                        r=r, w=w, tag=tag)

    def act(self, out, in_, func, r=(), w=(), bias=0.0, scale=1.0,
            accum_out=None, hazard=False, tag=""):
        if accum_out is not None:
            return self.add("scalar",
                            lambda e: e.activation(out, in_, func, bias=bias,
                                                   scale=scale, accum_out=accum_out),
                            r=r, w=w, hazard=True, tag=tag)
        return self.add("scalar",
                        lambda e: e.activation(out, in_, func, bias=bias, scale=scale),
                        r=r, w=w, hazard=hazard, tag=tag)

    def vop(self, fn, r=(), w=(), hazard=False, tag=""):
        return self.add("vector", fn, r=r, w=w, hazard=hazard, tag=tag)

    def pop(self, fn, r=(), w=(), tag=""):
        return self.add("gpsimd", fn, r=r, w=w, tag=tag)

    def plan_and_emit(self, block_name="prog"):
        P = self.P
        nc = P.nc
        last_w = {}
        readers = {}
        for op in self.ops:
            dep = {}
            for k in op.reads:
                lw = last_w.get(k)
                if lw is not None:
                    dep[id(lw)] = lw
            for k in op.writes:
                lw = last_w.get(k)
                if lw is not None:
                    dep[id(lw)] = lw
                for rd in readers.get(k, ()):
                    dep[id(rd)] = rd
            keep = []
            for d in dep.values():
                if d is op:
                    continue
                if (d.eng == op.eng and not d.dma and not d.hazard
                        and not op.hazard):
                    continue
                keep.append(d)
            op.deps = keep
            for d in keep:
                d.needed = True
            for k in op.reads:
                readers.setdefault(k, []).append(op)
            for k in op.writes:
                last_w[k] = op
                readers[k] = [op] if k in op.reads else []
        NDSEM = 24
        csem = {e: P.sem("c_" + e) for e in ENGINES}
        dpool = [P.sem(f"d{j}") for j in range(NDSEM)]
        di = 0
        for op in self.ops:
            if op.dma:
                s = dpool[di % NDSEM]
                di += 1
                op.gate = s.n
                s.n += 16
                op.sem, op.tok = s, s.n
            elif op.needed or op.hazard:
                s = csem[op.eng]
                s.n += 1
                op.sem, op.tok = s, s.n
        per_eng = {e: [o for o in self.ops if o.eng == e] for e in ENGINES}

        def body(eng_name):
            def f(handle):
                seen = {}
                for op in per_eng[eng_name]:
                    for d in op.deps:
                        assert d.sem is not None
                        key = d.sem.h
                        if seen.get(key, -1) < d.tok:
                            handle.wait_ge(d.sem.h, d.tok)
                            seen[key] = d.tok
                    if op.dma and op.gate and seen.get(op.sem.h, -1) < op.gate:
                        handle.wait_ge(op.sem.h, op.gate)
                        seen[op.sem.h] = op.gate
                    ins = op.fn(handle)
                    if op.sem is not None:
                        if ins is None:
                            ins = handle.engine_nop() if not op.dma else None
                        assert ins is not None, f"op {op.tag} returned no instruction"
                        ins.then_inc(op.sem.h, 16 if op.dma else 1)
            return f

        with nc.Block(block_name) as b:
            b.sync(body("sync"))
            b.gpsimd(body("gpsimd"))
            b.scalar(body("scalar"))
            b.vector(body("vector"))
            b.tensor(body("tensor"))
        self.ops = []


def emit_clear_sems(P):
    nc = P.nc
    for e in ENGINES:
        P.sem("c_" + e)
    for e in ("sync", "gpsimd", "scalar"):
        P.sem("q_" + e)
    with nc.Block("init") as b:
        @b.sync
        def _(sync):
            for s in P.sems.values():
                sync.sem_clear(s.h)


def emit_consts(S):
    P = S.P
    pairs = [("blob", "blob"), ("ident_bf", "ident_bf"),
             ("ident_f32", "ident_f32"), ("m_causal", "m_causal"),
             ("m_tril2", "m_tril2"), ("m_lower", "m_lower"),
             ("m01_upper", "m01_upper"), ("m_edge2", "m_edge2"),
             ("rowbias_sb", "rowbias"), ("scorer_sb", "scorer"),
             ("ffn_qs_sb", "ffn_qs"), ("ffn_ks_sb", "ffn_ks"),
             ("l1a_bd_sb", "l1a_bd"), ("l1a_os_sb", "l1a_os")]
    for sb_name, in_name in pairs:
        S.dma("sync", P.sb[sb_name][:], P.input(in_name)[:, :], w=[sb_name])
    for nm in ["rot_c1", "rot_s1", "rot_c2", "rot_s2"]:
        S.dma("gpsimd", P.sb[nm + "_sb"][:],
              P.input(nm).rearrange("(c p) d -> p c d", p=128), w=[nm])
    S.vop(lambda v: v.memset(P.sb["ones_row_bf"][:], 1.0), w=["ones_row_bf"])
    S.vop(lambda v: v.memset(P.sb["ones_row_f32"][:], 1.0), w=["ones_row_f32"])
    S.vop(lambda v: v.memset(P.sb["ones_col_bf"][:], 1.0), w=["ones_col_bf"])
    S.vop(lambda v: v.memset(P.sb["eps_col"][:], EPS), w=["eps_col"])
    S.vop(lambda v: v.memset(P.sb["eps8_col"][:], 1e-8), w=["eps8_col"])


def sl(arena, lo_col, width):
    lo = lo_col // 512
    hi = (lo_col + width + 511) // 512
    return [f"{arena}.s{j}" for j in range(lo, hi)]


def emit_ln(S, tag, src, dst_bf=None, dst_f32=None, res=None, gname=None,
            shift=False):
    P = S.P
    src, ksrc = src
    res, kres = res if res is not None else (None, None)
    dst_bf, kdbf = dst_bf if dst_bf is not None else (None, None)
    dst_f32, kd32 = dst_f32 if dst_f32 is not None else (None, None)
    wka, wkb, wkc = P.sb["wka"], P.sb["wkb"], P.sb["wkc"]
    bfa, bfb = P.sb["bfa"], P.sb["bfb"]
    rows = P.sb["rows"]
    half = ND // 2
    HW = ND * TC

    for i in range(NTC):
        pi = i & 1
        t0 = i * TC
        x32 = wka[:, pi * HW:(pi + 1) * HW]
        resb = wka[:, 8192:8192 + HW]
        t1 = wkb[:, pi * HW:(pi + 1) * HW]
        asb = wkc[:, pi * 512:(pi + 1) * 512]
        csb = wkc[:, 1024 + pi * 512:1024 + (pi + 1) * 512]
        xbf = bfa[:, pi * HW:(pi + 1) * HW]
        sq = bfa[:, 8192 + pi * HW:8192 + (pi + 1) * HW]
        ybf = bfb[:, 0:HW]
        kx = sl("wka", pi * HW, HW)
        kr = sl("wka", 8192, HW)
        kt = sl("wkb", pi * HW, HW)
        kab = sl("wkc", pi * 512, 512)
        kcb = sl("wkc", 1024 + pi * 512, 512)
        kxb = sl("bfa", pi * HW, HW)
        ksq = sl("bfa", 8192 + pi * HW, HW)
        ky = sl("bfb", 0, HW)
        ka, kc = ("PS0", "PS1") if pi == 0 else ("PS2", "PS3")
        ps_a = P.sb["PS0"] if pi == 0 else P.sb["PS2"]
        ps_c = P.sb["PS1"] if pi == 0 else P.sb["PS3"]
        pra, prb = P.sb["PRA"], P.sb["PRB"]
        krow = ["rows"]
        rs = lambda k: rows[0:1, k * TC:(k + 1) * TC]

        lq = "gpsimd" if src.dtype == BF16 else "sync"
        if not shift:
            S.dma(lq, x32.rearrange("p (c t) -> p c t", c=ND), fmv(src, t0),
                  r=[ksrc], w=kx, tag=f"{tag}.x{i}")
        else:
            S.dma(lq, x32[:, 0:half * TC].rearrange("p (c t) -> p c t", c=half),
                  src[0:half * 128, t0:t0 + TC].rearrange("(c p) t -> p c t", p=128),
                  r=[ksrc], w=kx, tag=f"{tag}.xa{i}")
            if t0 == 0:
                S.dma(lq,
                      x32[:, half * TC:HW].rearrange("p (c t) -> p c t", c=half)[:, :, 1:TC],
                      src[half * 128:D, 0:TC - 1].rearrange("(c p) t -> p c t", p=128),
                      r=[ksrc], w=kx, tag=f"{tag}.xs{i}")
                S.vop(lambda v, _x=x32: [v.memset(_x[:, d * TC:d * TC + 1], 0.0)
                                         for d in range(half, ND)][-1],
                      w=kx, tag=f"{tag}.z{i}")
            else:
                S.dma(lq,
                      x32[:, half * TC:HW].rearrange("p (c t) -> p c t", c=half),
                      src[half * 128:D, t0 - 1:t0 + TC - 1]
                      .rearrange("(c p) t -> p c t", p=128),
                      r=[ksrc], w=kx, tag=f"{tag}.xs{i}")
        if res is not None:
            S.dma("sync", resb.rearrange("p (c t) -> p c t", c=ND), fmv(res, t0),
                  r=[kres], w=kr, tag=f"{tag}.res{i}")
        S.dma("gpsimd", xbf, x32, r=kx, w=kxb, tag=f"{tag}.cast{i}")
        S.act(sq, x32, AF.Square, r=kx, w=ksq, tag=f"{tag}.sqr{i}")
        for d in range(ND):
            S.mm(pra[:, :], P.sb["ones_col_bf"][:], xbf[:, d * TC:(d + 1) * TC],
                 d == 0, d == ND - 1, r=kxb + ["ones_col_bf"], w=["PRA"],
                 tag=f"{tag}.sm{i}")
        for d in range(ND):
            S.mm(prb[:, :], P.sb["ones_col_bf"][:], sq[:, d * TC:(d + 1) * TC],
                 d == 0, d == ND - 1, r=ksq + ["ones_col_bf"], w=["PRB"],
                 tag=f"{tag}.sv{i}")
        S.vop(lambda v, a=rs(0), b_=pra: v.tensor_scalar(a, b_[:, :], -1.0 / D, None, ALU.mult),
              r=["PRA"], w=krow, tag=f"{tag}.r0_{i}")
        S.vop(lambda v, a=rs(1), b_=prb: v.tensor_scalar(a, b_[:, :], 1.0 / D, None, ALU.mult),
              r=["PRB"], w=krow, tag=f"{tag}.r1_{i}")
        S.vop(lambda v, a=rs(2), b_=rs(0): v.tensor_tensor(a, b_, b_, ALU.mult),
              r=krow, w=krow, tag=f"{tag}.r2_{i}")
        S.vop(lambda v, a=rs(1), b_=rs(2): v.tensor_tensor(a, a, b_, ALU.subtract),
              r=krow, w=krow, tag=f"{tag}.r3_{i}")
        S.act(rs(2), rs(1), AF.Sqrt, bias=P.sb["eps_col"][0:1, :],
              r=krow + ["eps_col"], w=krow, tag=f"{tag}.sqrt{i}")
        S.vop(lambda v, a=rs(3), b_=rs(2): v.reciprocal(a, b_),
              r=krow, w=krow, hazard=True, tag=f"{tag}.recip{i}")
        S.vop(lambda v, a=rs(2), b_=rs(0), c_=rs(3): v.tensor_tensor(a, b_, c_, ALU.mult),
              r=krow, w=krow, tag=f"{tag}.c{i}")
        S.mm(ps_a[:], P.sb["ones_row_f32"][:], rs(3), True, True,
             r=krow + ["ones_row_f32"], w=[ka], tag=f"{tag}.ba{i}")
        S.mm(ps_c[:], P.sb["ones_row_f32"][:], rs(2), True, True,
             r=krow + ["ones_row_f32"], w=[kc], tag=f"{tag}.bc{i}")
        S.add("scalar", lambda e, a=asb, b_=ps_a: e.copy(a, b_[:]),
              r=[ka], w=kab, tag=f"{tag}.cpa{i}")
        S.add("scalar", lambda e, a=csb, b_=ps_c: e.copy(a, b_[:]),
              r=[kc], w=kcb, tag=f"{tag}.cpc{i}")
        S.pop(lambda e, o=t1, a=x32, b_=asb:
              e.tensor_tensor(o.rearrange("p (c t) -> p c t", c=ND),
                              a.rearrange("p (c t) -> p c t", c=ND),
                              b_.unsqueeze(1).broadcast_to([128, ND, TC]),
                              ALU.mult),
              r=kx + kab, w=kt, tag=f"{tag}.t1_{i}")
        S.vop(lambda v, o=t1, b_=csb:
              v.tensor_tensor(o.rearrange("p (c t) -> p c t", c=ND),
                              o.rearrange("p (c t) -> p c t", c=ND),
                              b_.unsqueeze(1).broadcast_to([128, ND, TC]),
                              ALU.add),
              r=kt + kcb, w=kt, tag=f"{tag}.t2_{i}")
        if gname is not None:
            for d in range(ND):
                S.vop(lambda v, o=ybf[:, d * TC:(d + 1) * TC],
                      a=t1[:, d * TC:(d + 1) * TC],
                      g=blob_col(P, gname + "_g", d),
                      bb=blob_col(P, gname + "_b", d).broadcast_to([128, TC]):
                      v.scalar_tensor_tensor(o, a, g, bb, ALU.mult, ALU.add),
                      r=kt + ["blob"], w=ky, tag=f"{tag}.gb{i}")
        elif dst_bf is not None:
            S.vop(lambda v, o=ybf, a=t1: v.tensor_copy(o, a),
                  r=kt, w=ky, tag=f"{tag}.cpy{i}")
        if res is not None:
            S.pop(lambda e, o=t1, b_=resb: e.tensor_tensor(o, o, b_, ALU.add),
                  r=kt + kr, w=kt, tag=f"{tag}.radd{i}")
        if dst_f32 is not None:
            S.dma("sync", fmv(dst_f32, t0), t1.rearrange("p (c t) -> p c t", c=ND),
                  r=kt, w=[kd32], tag=f"{tag}.st32_{i}")
        if dst_bf is not None:
            S.dma("sync", fmv(dst_bf, t0), ybf.rearrange("p (c t) -> p c t", c=ND),
                  r=ky, w=[kdbf], tag=f"{tag}.stbf{i}")


def emit_proj(S, tag, src, w_dram, out, din, dout, bias=None, func=None,
              res=None, scale_col=None, w_slice=None, bias_off=0, out_f32=False):
    P = S.P
    src_ap, ksrc = src
    w_ap, kw = w_dram
    out_ap, kout = out
    res_ap, kres = res if res is not None else (None, None)
    nin = din // 128
    OW = min(dout, max(128, (8192 // nin) // 128 * 128))
    wgt = P.sb["wgt"]
    bfa, bfb = P.sb["bfa"], P.sb["bfb"]
    wka, wkb = P.sb["wka"], P.sb["wkb"]
    src_is_bf = src_ap.dtype == BF16
    xcols = nin * TC

    for olo in range(0, dout, OW):
        ow = min(OW, dout - olo)
        now = ow // 128
        wlo = olo + (w_slice[0] if w_slice else 0)
        for c in range(nin):
            S.dma("gpsimd", wgt[:, c * ow:(c + 1) * ow],
                  w_ap[128 * c:128 * (c + 1), wlo:wlo + ow],
                  r=[kw], w=sl("wgt", c * ow, ow), tag=f"{tag}.w{olo}_{c}")
        for i in range(NTC):
            pi = i & 1
            t0 = i * TC
            if din <= 1024:
                xbf = bfa[:, pi * xcols:(pi + 1) * xcols]
                kxb = sl("bfa", pi * xcols, xcols)
            else:
                xbf = bfa[:, 0:xcols]
                kxb = sl("bfa", 0, xcols)
            q = "sync" if src_is_bf else "gpsimd"
            S.dma(q, xbf.rearrange("p (c t) -> p c t", c=nin),
                  src_ap[:, t0:t0 + TC].rearrange("(c p) t -> p c t", p=128),
                  r=[ksrc], w=kxb, tag=f"{tag}.x{olo}_{i}")
            if res is not None:
                resb = wka[:, pi * 4096:pi * 4096 + now * TC]
                krb = sl("wka", pi * 4096, now * TC)
                S.dma("sync", resb.rearrange("p (c t) -> p c t", c=now),
                      res_ap[olo:olo + ow, t0:t0 + TC].rearrange("(c p) t -> p c t", p=128),
                      r=[kres], w=krb, tag=f"{tag}.r{olo}_{i}")
            for o in range(now):
                ps = P.sb[f"PS{o % 4}"]
                kps = f"PS{o % 4}"
                for c in range(nin):
                    S.mm(ps[:], wgt[:, (c * ow + o * 128):(c * ow + (o + 1) * 128)],
                         xbf[:, c * TC:(c + 1) * TC], c == 0, c == nin - 1,
                         r=sl("wgt", c * ow + o * 128, 128) + kxb, w=[kps],
                         tag=f"{tag}.mm{olo}_{i}_{o}")
                bcol = blob_col(P, bias, bias_off + (olo + o * 128) // 128) if bias else None
                og = olo + o * 128
                if out_f32 and res is None:
                    y32 = wkb[:, pi * 4096 + o * TC:pi * 4096 + (o + 1) * TC]
                    ky32 = sl("wkb", pi * 4096 + o * TC, TC)
                    if bias:
                        S.vop(lambda v, y=y32, p_=ps, b_=bcol:
                              v.tensor_scalar(y, p_[:], b_, None, ALU.add),
                              r=[kps, "blob"], w=ky32, tag=f"{tag}.ev{olo}_{i}_{o}")
                    else:
                        S.vop(lambda v, y=y32, p_=ps: v.tensor_copy(y, p_[:]),
                              r=[kps], w=ky32, tag=f"{tag}.ev{olo}_{i}_{o}")
                    S.dma("sync", out_ap[og:og + 128, t0:t0 + TC], y32,
                          r=ky32, w=[kout], tag=f"{tag}.st{olo}_{i}_{o}")
                elif res is not None:
                    y32 = wkb[:, pi * 4096 + o * TC:pi * 4096 + (o + 1) * TC]
                    ky32 = sl("wkb", pi * 4096 + o * TC, TC)
                    rslice = resb[:, o * TC:(o + 1) * TC]
                    if bias:
                        S.vop(lambda v, y=y32, p_=ps, b_=bcol, rr=rslice:
                              v.scalar_tensor_tensor(y, p_[:], b_, rr, ALU.add, ALU.add),
                              r=[kps, "blob"] + krb, w=ky32, tag=f"{tag}.ev{olo}_{i}_{o}")
                    else:
                        S.vop(lambda v, y=y32, p_=ps, rr=rslice:
                              v.tensor_tensor(y, p_[:], rr, ALU.add),
                              r=[kps] + krb, w=ky32, tag=f"{tag}.ev{olo}_{i}_{o}")
                    S.dma("sync", out_ap[og:og + 128, t0:t0 + TC], y32,
                          r=ky32, w=[kout], tag=f"{tag}.st{olo}_{i}_{o}")
                else:
                    ybf = bfb[:, (pi * 4 + o % 4) * TC:(pi * 4 + o % 4 + 1) * TC]
                    kyb = sl("bfb", (pi * 4 + o % 4) * TC, TC)
                    if func is None:
                        if scale_col is not None:
                            S.vop(lambda v, y=ybf, p_=ps, sc=scale_col:
                                  v.tensor_scalar(y, p_[:], sc, None, ALU.mult),
                                  r=[kps, "l1a_os_sb"], w=kyb, tag=f"{tag}.ev{olo}_{i}_{o}")
                        elif bias:
                            S.vop(lambda v, y=ybf, p_=ps, b_=bcol:
                                  v.tensor_scalar(y, p_[:], b_, None, ALU.add),
                                  r=[kps, "blob"], w=kyb, tag=f"{tag}.ev{olo}_{i}_{o}")
                        else:
                            S.vop(lambda v, y=ybf, p_=ps: v.tensor_copy(y, p_[:]),
                                  r=[kps], w=kyb, tag=f"{tag}.ev{olo}_{i}_{o}")
                    elif func == "gelu_tanh":
                        u = wkb[:, pi * 4096 + (o % 8) * TC:pi * 4096 + (o % 8 + 1) * TC]
                        ku = sl("wkb", pi * 4096 + (o % 8) * TC, TC)
                        sqv = wka[:, pi * 4096 + (o % 8) * TC:pi * 4096 + (o % 8 + 1) * TC]
                        ksq2 = sl("wka", pi * 4096 + (o % 8) * TC, TC)
                        if bias:
                            S.vop(lambda v, y=u, p_=ps, b_=bcol:
                                  v.tensor_scalar(y, p_[:], b_, None, ALU.add),
                                  r=[kps, "blob"], w=ku, tag=f"{tag}.gt0_{olo}_{i}_{o}")
                        else:
                            S.vop(lambda v, y=u, p_=ps: v.tensor_copy(y, p_[:]),
                                  r=[kps], w=ku, tag=f"{tag}.gt0_{olo}_{i}_{o}")
                        S.act(sqv, u, AF.Square, r=ku, w=ksq2, tag=f"{tag}.gt1_{olo}_{i}_{o}")
                        S.vop(lambda v, y=sqv: v.tensor_scalar(y, y, 0.044715, 1.0,
                                                               ALU.mult, ALU.add),
                              r=ksq2, w=ksq2, tag=f"{tag}.gt2_{olo}_{i}_{o}")
                        S.pop(lambda e, y=sqv, a=u: e.tensor_tensor(y, y, a, ALU.mult),
                              r=ksq2 + ku, w=ksq2, tag=f"{tag}.gt3_{olo}_{i}_{o}")
                        S.act(sqv, sqv, AF.Tanh, scale=GELU_C, r=ksq2, w=ksq2,
                              tag=f"{tag}.gt4_{olo}_{i}_{o}")
                        S.vop(lambda v, y=ybf, tq=sqv, a=u:
                              v.scalar_tensor_tensor(y, tq, 1.0, a, ALU.add, ALU.mult),
                              r=ksq2 + ku, w=kyb, tag=f"{tag}.gt5_{olo}_{i}_{o}")
                        S.vop(lambda v, y=ybf: v.tensor_scalar(y, y, 0.5, None, ALU.mult),
                              r=kyb, w=kyb, tag=f"{tag}.gt6_{olo}_{i}_{o}")
                    else:
                        S.act(ybf, ps[:], func, bias=(bcol if bias else 0.0),
                              r=[kps] + (["blob"] if bias else []), w=kyb,
                              tag=f"{tag}.ev{olo}_{i}_{o}")
                    S.dma("sync", out_ap[og:og + 128, t0:t0 + TC], ybf,
                          r=kyb, w=[kout], tag=f"{tag}.st{olo}_{i}_{o}")


def build_program(upto=None, taps=()):
    P = Prog()
    alloc_common(P)
    alloc_scratch(P)
    out = P.out("out", [BC, D, T])
    tap_outs = {}
    for tp in taps:
        shape = P.dram[tp].shape if hasattr(P.dram[tp], "shape") else None
        tap_outs[tp] = P.out("tap_" + tp, list(P.dram[tp].shape),
                             P.dram[tp].dtype)
    emit_clear_sems(P)
    S = Sched(P)
    emit_consts(S)

    stages = []

    def stage(name, fn):
        stages.append((name, fn))

    xT = P.input("xT")

    def s_ln0():
        for b in range(BC):
            emit_ln(S, f"ln0b{b}", src=(xT[b], f"xT{b}"),
                    dst_f32=(P.dram[f"H{b}"], f"H{b}"))
    stage("ln0", s_ln0)

    def s_l0a_ln1():
        for b in range(BC):
            emit_ln(S, f"l0a_ln1b{b}", src=(P.dram[f"H{b}"], f"H{b}"),
                    dst_bf=(P.dram[f"LN{b}"], f"LN{b}"), gname="l0a_ln1")
    stage("l0a_ln1", s_l0a_ln1)

    def s_l0a_qkv():
        for b in range(BC):
            emit_proj(S, f"l0a_q{b}", src=(P.dram[f"LN{b}"], f"LN{b}"),
                      w_dram=(P.input("l0a_c_attn_w"), "w_cattn"),
                      out=(P.dram[f"QF{b}"], f"QF{b}"), din=D, dout=D,
                      bias="l0a_c_attn_b", w_slice=(0, D))
            emit_proj(S, f"l0a_k{b}", src=(P.dram[f"LN{b}"], f"LN{b}"),
                      w_dram=(P.input("l0a_c_attn_w"), "w_cattn"),
                      out=(P.dram[f"KF{b}"], f"KF{b}"), din=D, dout=D,
                      bias="l0a_c_attn_b", bias_off=8, w_slice=(D, 2 * D))
            emit_proj_tm(S, f"l0a_v{b}", src=(P.dram[f"LN{b}"], f"LN{b}"),
                         w_dram=(P.input("l0a_c_attn_w"), "w_cattn"),
                         out_vt=(P.dram[f"VT{b}"], f"VT{b}"), din=D, dout=D,
                         bias_row=(P.sb["rowbias_sb"][0:1, 0:1024], "rowbias_sb"),
                         w_slice=(2 * D, 3 * D))
    stage("l0a_qkv", s_l0a_qkv)

    def s_l0a_self():
        for b in range(BC):
            emit_attn(S, f"l0a_at{b}", qf_src=(P.dram[f"QF{b}"], f"QF{b}"),
                      kf_src=(P.dram[f"KF{b}"], f"KF{b}"),
                      vt_src=(P.dram[f"VT{b}"], f"VT{b}"),
                      out_ao=(P.dram[f"AO{b}"], f"AO{b}"), mode="self")
    stage("l0a_self", s_l0a_self)

    def s_l0a_proj():
        for b in range(BC):
            emit_proj(S, f"l0a_pr{b}", src=(P.dram[f"AO{b}"], f"AO{b}"),
                      w_dram=(P.input("l0a_attn_proj_w"), "w_aproj"),
                      out=(P.dram[f"G{b}"], f"G{b}"), din=D, dout=D,
                      bias="l0a_attn_proj_b", res=(P.dram[f"H{b}"], f"H{b}"))
    stage("l0a_proj", s_l0a_proj)

    def s_l0a_lnx():
        for b in range(BC):
            emit_ln(S, f"l0a_lnx{b}", src=(P.dram[f"G{b}"], f"G{b}"),
                    dst_bf=(P.dram[f"LN{b}"], f"LN{b}"), gname="l0a_lnx")
    stage("l0a_lnx", s_l0a_lnx)

    def s_enc_kv():
        emit_proj(S, "enc_k", src=(P.input("encT"), "encT"),
                  w_dram=(P.input("l0a_x_kv_w"), "w_xkv"),
                  out=(P.dram["KENC"], "KENC"), din=D, dout=D,
                  bias="l0a_x_kv_b", w_slice=(0, D))
        emit_proj_tm(S, "enc_v", src=(P.input("encT"), "encT"),
                     w_dram=(P.input("l0a_x_kv_w"), "w_xkv"),
                     out_vt=(P.dram["VENC"], "VENC"), din=D, dout=D,
                     bias_row=(P.sb["rowbias_sb"][0:1, 1024:2048], "rowbias_sb"),
                     w_slice=(D, 2 * D))
        emit_kaug(S, "enc_ka", (P.dram["KENC"], "KENC"), (P.dram["KAUG"], "KAUG"))
    stage("enc_kv", s_enc_kv)

    def s_l0a_q2():
        for b in range(BC):
            emit_proj(S, f"l0a_q2{b}", src=(P.dram[f"LN{b}"], f"LN{b}"),
                      w_dram=(P.input("l0a_q_attn_w"), "w_qattn"),
                      out=(P.dram[f"QF{b}"], f"QF{b}"), din=D, dout=D,
                      bias="l0a_q_attn_b")
    stage("l0a_q2", s_l0a_q2)

    def s_l0a_cross():
        for b in range(BC):
            emit_attn(S, f"l0a_cx{b}", qf_src=(P.dram[f"QF{b}"], f"QF{b}"),
                      kf_src=(P.dram["KAUG"], "KAUG"),
                      vt_src=(P.dram["VENC"], "VENC"),
                      out_ao=(P.dram[f"AO{b}"], f"AO{b}"), mode="cross")
    stage("l0a_cross", s_l0a_cross)

    def s_l0a_xproj():
        for b in range(BC):
            emit_proj(S, f"l0a_xp{b}", src=(P.dram[f"AO{b}"], f"AO{b}"),
                      w_dram=(P.input("l0a_x_proj_w"), "w_xproj"),
                      out=(P.dram[f"H{b}"], f"H{b}"), din=D, dout=D,
                      bias="l0a_x_proj_b", res=(P.dram[f"G{b}"], f"G{b}"))
    stage("l0a_xproj", s_l0a_xproj)

    def s_l0a_ln2():
        for b in range(BC):
            emit_ln(S, f"l0a_ln2{b}", src=(P.dram[f"H{b}"], f"H{b}"),
                    dst_bf=(P.dram[f"LN{b}"], f"LN{b}"), gname="l0a_ln2")
    stage("l0a_ln2", s_l0a_ln2)

    def s_l0a_fc():
        for b in range(BC):
            emit_proj(S, f"l0a_fc{b}", src=(P.dram[f"LN{b}"], f"LN{b}"),
                      w_dram=(P.input("l0a_fc_w"), "w_fc"),
                      out=(P.dram[f"HH{b}"][0:4 * D, :], f"HH{b}"), din=D,
                      dout=4 * D, bias="l0a_fc_b", func="gelu_tanh")
    stage("l0a_fc", s_l0a_fc)

    def s_l0a_mlp():
        for b in range(BC):
            emit_proj(S, f"l0a_mp{b}", src=(P.dram[f"HH{b}"][0:4 * D, :], f"HH{b}"),
                      w_dram=(P.input("l0a_mlp_proj_w"), "w_mlpp"),
                      out=(P.dram[f"G{b}"], f"G{b}"), din=4 * D, dout=D,
                      bias="l0a_mlp_proj_b", res=(P.dram[f"H{b}"], f"H{b}"))
    stage("l0a_mlp", s_l0a_mlp)

    def s_sand1():
        for b in range(BC):
            emit_ln(S, f"sand1_{b}", src=(P.dram[f"G{b}"], f"G{b}"),
                    res=(xT[b], f"xT{b}"), dst_f32=(P.dram[f"X{b}"], f"X{b}"))
    stage("sand1", s_sand1)

    def s_ffn_base():
        for b in range(BC):
            emit_ln(S, f"ffnbase{b}", src=(P.dram[f"X{b}"], f"X{b}"),
                    dst_f32=(P.dram[f"H{b}"], f"H{b}"))
    stage("ffn_base", s_ffn_base)

    cur = {b: f"H{b}" for b in range(BC)}
    alt = {b: f"G{b}" for b in range(BC)}

    for i in range(FFN_DEPTH):
        def s_ffn_layer(i=i):
            for b in range(BC):
                for part, lo in [("q", 0), ("k", D), ("v", 2 * D)]:
                    emit_proj_tm(S, f"f{i}{part}{b}",
                                 src=(P.dram[cur[b]], cur[b]),
                                 w_dram=(P.input("ffn_wqkv")[i], "ffn_wqkv"),
                                 out_vt=(P.dram[f"QKVT{b}"][:, :, lo:lo + D], f"QKVT{b}"),
                                 din=D, dout=D, w_slice=(lo, lo + D), shift=True)
                emit_qk_prep(S, f"f{i}prep{b}", (P.dram[f"QKVT{b}"], f"QKVT{b}"),
                             (P.dram[f"QF{b}"], f"QF{b}"), (P.dram[f"KF{b}"], f"KF{b}"), i)
                emit_attn(S, f"f{i}att{b}", qf_src=(P.dram[f"QF{b}"], f"QF{b}"),
                          kf_src=(P.dram[f"KF{b}"], f"KF{b}"),
                          vt_src=(P.dram[f"QKVT{b}"][:, :, 2 * D:3 * D], f"QKVT{b}"),
                          out_ao=(P.dram[f"AO{b}"], f"AO{b}"), mode="local")
                emit_proj(S, f"f{i}wo{b}", src=(P.dram[f"AO{b}"], f"AO{b}"),
                          w_dram=(P.input("ffn_wo")[i], "ffn_wo"),
                          out=(P.dram[alt[b]], alt[b]), din=D, dout=D,
                          res=(P.dram[cur[b]], cur[b]))
                cur[b], alt[b] = alt[b], cur[b]
                emit_ln(S, f"f{i}ln{b}", src=(P.dram[cur[b]], cur[b]),
                        dst_bf=(P.dram[f"LN{b}"], f"LN{b}"), gname=f"ffn{i}_ln",
                        shift=True)
                emit_w1_fused(S, f"f{i}w1{b}", src=(P.dram[f"LN{b}"], f"LN{b}"),
                              w_dram=(P.input("ffn_w1")[i], "ffn_w1"),
                              out=(P.dram[f"HG{b}"], f"HG{b}"),
                              bias=f"ffn{i}_b1")
                emit_proj(S, f"f{i}w2{b}", src=(P.dram[f"HG{b}"], f"HG{b}"),
                          w_dram=(P.input("ffn_w2")[i], "ffn_w2"),
                          out=(P.dram[alt[b]], alt[b]), din=DIM_INNER_PAD, dout=D,
                          bias=f"ffn{i}_b2", res=(P.dram[cur[b]], cur[b]))
                cur[b], alt[b] = alt[b], cur[b]
        stage(f"ffn_l{i}", s_ffn_layer)

    def s_sand2():
        for b in range(BC):
            emit_ln(S, f"sand2_{b}", src=(P.dram[cur[b]], cur[b]),
                    res=(P.dram[f"X{b}"], f"X{b}"), dst_f32=(P.dram[f"Y{b}"], f"Y{b}"))
    stage("sand2", s_sand2)

    def s_l1_ln():
        for b in range(BC):
            emit_ln(S, f"l1ln{b}", src=(P.dram[f"Y{b}"], f"Y{b}"),
                    dst_bf=(P.dram[f"LN{b}"], f"LN{b}"))
    stage("l1_ln", s_l1_ln)

    def s_l1_qkv():
        for b in range(BC):
            emit_proj(S, f"l1q{b}", src=(P.dram[f"LN{b}"], f"LN{b}"),
                      w_dram=(P.input("l1a_wq"), "l1a_wq"),
                      out=(P.dram[f"QF{b}"], f"QF{b}"), din=D, dout=D)
            emit_proj(S, f"l1k{b}", src=(P.dram[f"LN{b}"], f"LN{b}"),
                      w_dram=(P.input("l1a_wk"), "l1a_wk"),
                      out=(P.dram[f"KF{b}"], f"KF{b}"), din=D, dout=D)
            emit_proj_tm(S, f"l1v{b}", src=(P.dram[f"LN{b}"], f"LN{b}"),
                         w_dram=(P.input("l1a_wv"), "l1a_wv"),
                         out_vt=(P.dram[f"VT{b}"], f"VT{b}"), din=D, dout=D)
    stage("l1_qkv", s_l1_qkv)

    def s_l1_core():
        for b in range(BC):
            emit_l1_decay(S, f"l1d{b}", (P.dram[f"LN{b}"], f"LN{b}"))
            emit_l1_core(S, f"l1c{b}", (P.dram[f"QF{b}"], f"QF{b}"),
                         (P.dram[f"KF{b}"], f"KF{b}"),
                         (P.dram[f"VT{b}"], f"VT{b}"),
                         (P.dram[f"BO{b}"], f"BO{b}"))
    stage("l1_core", s_l1_core)

    def s_l1_out():
        S.mm(P.sb["PS4"][:, 0:1],
             P.sb["ones_row_f32"][:], P.sb["l1a_os_sb"][0:1, 0:1], True, True,
             r=["l1a_os_sb", "ones_row_f32"], w=["PS4"], tag="l1ob")
        S.add("scalar", lambda e, o=P.sb["wkc"][:, 1032:1033], p_=P.sb["PS4"]:
              e.copy(o, p_[:, 0:1]), r=["PS4"], w=sl("wkc", 1024, 16), tag="l1oc")
        for b in range(BC):
            emit_proj(S, f"l1o{b}", src=(P.dram[f"BO{b}"], f"BO{b}"),
                      w_dram=(P.input("l1a_wo"), "l1a_wo"),
                      out=(P.dram[f"AO{b}"], f"AO{b}"), din=D, dout=D,
                      scale_col=P.sb["wkc"][:, 1032:1033])
    stage("l1_out", s_l1_out)

    def s_sand3():
        for b in range(BC):
            emit_ln(S, f"sand3_{b}", src=(P.dram[f"AO{b}"], f"AO{b}"),
                    res=(P.dram[f"Y{b}"], f"Y{b}"), dst_f32=(P.dram[f"X{b}"], f"X{b}"))
    stage("sand3", s_sand3)

    def s_moe_ln():
        for b in range(BC):
            emit_ln(S, f"moeln{b}", src=(P.dram[f"X{b}"], f"X{b}"),
                    dst_bf=(P.dram[f"LN{b}"], f"LN{b}"))
    stage("moe_ln", s_moe_ln)

    def s_moe_shared():
        for b in range(BC):
            emit_proj(S, f"msg{b}", src=(P.dram[f"LN{b}"], f"LN{b}"),
                      w_dram=(P.input("moe_sg"), "moe_sg"),
                      out=(P.dram[f"HH{b}"][0:D, :], f"HH{b}"), din=D, dout=D,
                      func=AF.Silu)
            emit_proj(S, f"msu{b}", src=(P.dram[f"LN{b}"], f"LN{b}"),
                      w_dram=(P.input("moe_su"), "moe_su"),
                      out=(P.dram[f"HH{b}"][D:2 * D, :], f"HH{b}"), din=D, dout=D)
            emit_glu_mul(S, f"msm{b}", (P.dram[f"HH{b}"][0:D, :], f"HH{b}"),
                         (P.dram[f"HH{b}"][D:2 * D, :], f"HH{b}"),
                         (P.dram[f"HG{b}"][0:D, :], f"HG{b}"), D)
            emit_proj(S, f"msd{b}", src=(P.dram[f"HG{b}"][0:D, :], f"HG{b}"),
                      w_dram=(P.input("moe_sd"), "moe_sd"),
                      out=(P.dram[f"BO{b}"], f"BO{b}"), din=D, dout=D,
                      out_f32=True)
    stage("moe_shared", s_moe_shared)

    def s_moe_router():
        for b in range(BC):
            emit_moe_router(S, f"mr{b}", (P.dram[f"LN{b}"], f"LN{b}"),
                            (P.dram["CWF"], "CWF"))
            emit_moe_experts(S, f"me{b}", (P.dram[f"LN{b}"], f"LN{b}"),
                             (P.dram[f"BO{b}"], f"BO{b}"))
    stage("moe_core", s_moe_router)

    def s_sand4():
        for b in range(BC):
            emit_ln(S, f"sand4_{b}", src=(P.dram[f"BO{b}"], f"BO{b}"),
                    res=(P.dram[f"X{b}"], f"X{b}"), dst_f32=(out[b], "out"))
    stage("sand4", s_sand4)

    for name, fn in stages:
        fn()
        if upto is not None and name == upto:
            break

    for tp, ap_out in tap_outs.items():
        src_ap = P.dram[tp]
        shp = list(src_ap.shape)
        total = 1
        for s_ in shp:
            total *= s_
        flat_in = src_ap.rearrange(
            " ".join(f"a{j}" for j in range(len(shp))) + " -> (" +
            " ".join(f"a{j}" for j in range(len(shp))) + ")") if False else src_ap
        esz = 2 if src_ap.dtype == BF16 else 4
        buf = P.sb["bfa"] if src_ap.dtype == BF16 else P.sb["wka"]
        cols = total // 128
        CW = 4096
        for lo in range(0, cols, CW):
            wdt = min(CW, cols - lo)
            iv = src_ap.rearrange(
                ("a b -> (a b)" if len(shp) == 2 else "a b c -> (a b c)"))
            iv = iv.rearrange("(p x) -> p x", p=128)[:, lo:lo + wdt]
            ov = ap_out.rearrange(
                ("a b -> (a b)" if len(shp) == 2 else "a b c -> (a b c)"))
            ov = ov.rearrange("(p x) -> p x", p=128)[:, lo:lo + wdt]
            S.dma("sync", buf[:, 0:wdt], iv, r=[tp], w=["tapbuf"],
                  tag=f"tap.{tp}.{lo}")
            S.dma("sync", ov, buf[:, 0:wdt], r=["tapbuf"], w=["tapout"],
                  tag=f"tapo.{tp}.{lo}")

    S.add("sync", lambda e: e.nop(),
          r=["out", "tapout"] + [f"X{b}" for b in range(BC)], tag="fin")
    S.plan_and_emit("prog")
    with P.nc.Block("fin2") as b:
        @b.sync
        def _(sync):
            for qn, s in P.sems.items():
                if qn.startswith("d") and qn[1:].isdigit():
                    sync.wait_ge(s.h, s.n)
    P.es.close()
    return P


_BUILD_CACHE = {}


def kernel(**inputs):
    consts = host_constants(inputs)
    x = np.asarray(inputs["L0_attn_hidden_states"], dtype=np.float32)
    in_maps = []
    for c in range(8):
        m = dict(consts)
        m["xT"] = np.ascontiguousarray(x[BC * c:BC * c + BC].transpose(0, 2, 1))
        in_maps.append(m)
    P = build_program()
    in_maps = [{k: v for k, v in m.items() if k in P.inputs} for m in in_maps]
    res = run_bass_kernel_spmd(P.nc, in_maps, core_ids=list(range(8)))
    outs = [r["out"] for r in res.results]
    y = np.concatenate(outs, axis=0)
    return np.ascontiguousarray(y.transpose(0, 2, 1)).astype(np.float32)


def emit_proj_tm(S, tag, src, w_dram, out_vt, din, dout, bias_row=None,
                 w_slice=None, shift=False):
    P = S.P
    src_ap, ksrc = src
    w_ap, kw = w_dram
    vt_ap, kvt = out_vt
    nin = din // 128
    wgt = P.sb["wgt"]
    bfa, bfb, wkc = P.sb["bfa"], P.sb["bfb"], P.sb["wkc"]
    assert nin * dout <= 8192
    wlo0 = w_slice[0] if w_slice else 0
    for c in range(nin):
        S.dma("gpsimd", wgt[:, c * dout:(c + 1) * dout],
              w_ap[128 * c:128 * (c + 1), wlo0:wlo0 + dout],
              r=[kw], w=sl("wgt", c * dout, dout), tag=f"{tag}.w{c}")
    nob = dout // 512
    if bias_row is not None:
        brow, kbr = bias_row
        for j in range(nob):
            S.mm(P.sb["PS4"][:], P.sb["ones_row_f32"][:], brow[0:1, j * 512:(j + 1) * 512],
                 True, True, r=[kbr, "ones_row_f32"], w=["PS4"], tag=f"{tag}.bb{j}")
            S.add("scalar", lambda e, o=wkc[:, j * 512:(j + 1) * 512], p_=P.sb["PS4"]:
                  e.copy(o, p_[:]), r=["PS4"], w=sl("wkc", j * 512, 512), tag=f"{tag}.bc{j}")
    src_is_bf = src_ap.dtype == BF16
    xcols = nin * TC
    half = nin // 2
    for i in range(NTC):
        pi = i & 1
        t0 = i * TC
        xbf = bfa[:, pi * xcols:(pi + 1) * xcols]
        kxb = sl("bfa", pi * xcols, xcols)
        q = "sync" if src_is_bf else "gpsimd"
        if not shift:
            S.dma(q, xbf.rearrange("p (c t) -> p c t", c=nin),
                  src_ap[:, t0:t0 + TC].rearrange("(c p) t -> p c t", p=128),
                  r=[ksrc], w=kxb, tag=f"{tag}.x{i}")
        else:
            S.dma(q, xbf[:, 0:half * TC].rearrange("p (c t) -> p c t", c=half),
                  src_ap[0:half * 128, t0:t0 + TC].rearrange("(c p) t -> p c t", p=128),
                  r=[ksrc], w=kxb, tag=f"{tag}.xa{i}")
            if t0 == 0:
                S.dma(q, xbf[:, half * TC:xcols].rearrange("p (c t) -> p c t", c=half)[:, :, 1:TC],
                      src_ap[half * 128:din, 0:TC - 1].rearrange("(c p) t -> p c t", p=128),
                      r=[ksrc], w=kxb, tag=f"{tag}.xs{i}")
                S.vop(lambda v, _x=xbf: [v.memset(_x[:, d * TC:d * TC + 1], 0.0)
                                         for d in range(half, nin)][-1],
                      w=kxb, tag=f"{tag}.z{i}")
            else:
                S.dma(q, xbf[:, half * TC:xcols].rearrange("p (c t) -> p c t", c=half),
                      src_ap[half * 128:din, t0 - 1:t0 + TC - 1]
                      .rearrange("(c p) t -> p c t", p=128),
                      r=[ksrc], w=kxb, tag=f"{tag}.xs{i}")
        for tt in range(4):
            tile = i * 4 + tt
            for oc in range(nob):
                ps = P.sb[f"PS{oc % 2 + 2 * (tt % 2)}"]
                kps = f"PS{oc % 2 + 2 * (tt % 2)}"
                for c in range(nin):
                    S.mm(ps[:], xbf[:, c * TC + tt * 128:c * TC + (tt + 1) * 128],
                         wgt[:, c * dout + oc * 512:c * dout + (oc + 1) * 512],
                         c == 0, c == nin - 1,
                         r=sl("wgt", c * dout + oc * 512, 512) + kxb, w=[kps],
                         tag=f"{tag}.mm{tile}_{oc}")
                yb = bfb[:, (2 * (tile % 2) + oc % 2) * TC:
                         (2 * (tile % 2) + oc % 2 + 1) * TC]
                kyb = sl("bfb", (2 * (tile % 2) + oc % 2) * TC, TC)
                if bias_row is not None:
                    S.vop(lambda v, o=yb, p_=ps, bb=wkc[:, oc * 512:(oc + 1) * 512]:
                          v.tensor_tensor(o, p_[:], bb, ALU.add),
                          r=[kps] + sl("wkc", oc * 512, 512), w=kyb,
                          tag=f"{tag}.ev{tile}_{oc}")
                elif tile % 2 == 0:
                    S.vop(lambda v, o=yb, p_=ps: v.tensor_copy(o, p_[:]),
                          r=[kps], w=kyb, tag=f"{tag}.ev{tile}_{oc}")
                else:
                    S.add("scalar", lambda e, o=yb, p_=ps: e.copy(o, p_[:]),
                          r=[kps], w=kyb, tag=f"{tag}.ev{tile}_{oc}")
                S.dma("sync", vt_ap[tile, :, oc * 512:(oc + 1) * 512], yb,
                      r=kyb, w=[kvt], tag=f"{tag}.st{tile}_{oc}")


def emit_kaug(S, tag, kf_src, kaug_dst, scorer_scale=8.0):
    P = S.P
    kf_ap, kkf = kf_src
    ka_ap, kka = kaug_dst
    bfb = P.sb["bfb"]
    S.vop(lambda v: v.tensor_scalar(bfb[0:1, 0:T], P.sb["scorer_sb"][:],
                                    scorer_scale, None, ALU.mult),
          r=["scorer_sb"], w=sl("bfb", 0, T), tag=f"{tag}.s8")
    for h in range(16):
        S.dma("sync", ka_ap[65 * h:65 * h + 64, :], kf_ap[64 * h:64 * h + 64, :],
              r=[kkf], w=[kka], tag=f"{tag}.k{h}")
        S.dma("sync", ka_ap[65 * h + 64:65 * h + 65, :], bfb[0:1, 0:T],
              r=sl("bfb", 0, T), w=[kka], tag=f"{tag}.sc{h}")


def _att_window(mode, qt):
    if mode == "self":
        return 0, 128 * (qt + 1)
    if mode == "cross":
        return 0, min(T, 128 * (qt + 1) + 2)
    if qt == 0:
        return 0, 128
    if qt == 1:
        return 0, 256
    return 128 * (qt - 2), 384


def _att_masks(P, mode, qt, klo, kw):
    out = []
    if mode == "self":
        out.append((128 * qt - klo, 128, P.sb["m_causal"][:], "m_causal"))
    elif mode == "cross":
        out.append((128 * qt - klo, 128, P.sb["m_tril2"][:], "m_tril2"))
        if 128 * (qt + 1) + 2 <= T:
            out.append((128 * (qt + 1) - klo, 2, P.sb["m_edge2"][:, 0:2], "m_edge2"))
    else:
        if qt == 0:
            out.append((0, 128, P.sb["m_causal"][:], "m_causal"))
        elif qt == 1:
            out.append((128, 128, P.sb["m_causal"][:], "m_causal"))
        else:
            out.append((0, 128, P.sb["m_lower"][:], "m_lower"))
            out.append((256, 128, P.sb["m_causal"][:], "m_causal"))
    return out


def emit_attn(S, tag, qf_src, kf_src, vt_src, out_ao, mode, dbg=None):
    P = S.P
    qf_ap, kqf = qf_src
    kf_ap, kkf = kf_src
    vt_ap, kvt = vt_src
    ao_ap, kao = out_ao
    bfa, bfb = P.sb["bfa"], P.sb["bfb"]
    vab = P.sb["wka"][:].bitcast(BF16)
    att = P.sb["attst"]
    ident = P.sb["ident_bf"][:]
    scale = 0.125 if mode in ("self", "cross") else 8.0
    use_max = False

    for kc in range(16):
        S.dma("sync", vab[:, kc * 1024:(kc + 1) * 1024], vt_ap[kc, :, :],
              r=[kvt], w=sl("wka", kc * 512, 512), tag=f"{tag}.v{kc}")
    kv_all = sl("wka", 0, 8192)

    if mode == "cross":
        S.vop(lambda v: v.memset(vab[64:65, 16384:18432], 1.0),
              w=sl("wka", 8192, 1024), tag=f"{tag}.one0")
        S.vop(lambda v: v.memset(vab[64:65, 18432:20480], 1.0),
              w=sl("wka", 9216, 1024), tag=f"{tag}.one1")

    for pr in range(8):
        hA, hB = 2 * pr, 2 * pr + 1
        if mode == "cross":
            S.dma("sync", vab[0:64, 16384:18432], qf_ap[64 * hA:64 * hA + 64, :],
                  r=[kqf], w=sl("wka", 8192, 1024), tag=f"{tag}.q{pr}a")
            S.dma("sync", vab[0:64, 18432:20480], qf_ap[64 * hB:64 * hB + 64, :],
                  r=[kqf], w=sl("wka", 9216, 1024), tag=f"{tag}.q{pr}b")
            S.dma("sync", vab[0:65, 20480:22528], kf_ap[65 * hA:65 * hA + 65, :],
                  r=[kkf], w=sl("wka", 10240, 1024), tag=f"{tag}.k{pr}a")
            S.dma("sync", vab[0:65, 22528:24576], kf_ap[65 * hB:65 * hB + 65, :],
                  r=[kkf], w=sl("wka", 11264, 1024), tag=f"{tag}.k{pr}b")
            qslot = [vab[0:65, 16384:18432], vab[0:65, 18432:20480]]
            kslot = [vab[0:65, 20480:22528], vab[0:65, 22528:24576]]
            kq_keys = [sl("wka", 8192, 1024), sl("wka", 9216, 1024)]
            kk_keys = [sl("wka", 10240, 1024), sl("wka", 11264, 1024)]
        else:
            sb16 = 16384 + 4096 * (pr % 2)
            sw = 8192 + 2048 * (pr % 2)
            S.dma("sync", vab[:, sb16:sb16 + 2048], qf_ap[128 * pr:128 * (pr + 1), :],
                  r=[kqf], w=sl("wka", sw, 1024), tag=f"{tag}.q{pr}")
            S.dma("sync", vab[:, sb16 + 2048:sb16 + 4096], kf_ap[128 * pr:128 * (pr + 1), :],
                  r=[kkf], w=sl("wka", sw + 1024, 1024), tag=f"{tag}.k{pr}")
            qslot = [vab[0:64, sb16:sb16 + 2048], vab[64:128, sb16:sb16 + 2048]]
            kslot = [vab[0:64, sb16 + 2048:sb16 + 4096],
                     vab[64:128, sb16 + 2048:sb16 + 4096]]
            kq_keys = [sl("wka", sw, 1024)] * 2
            kk_keys = [sl("wka", sw + 1024, 1024)] * 2

        def phase1(qt):
            import os as _os2
            _par = _os2.environ.get("L1_PAR", "qwa")
            pi = (qt & 1)
            pq = pi if "q" in _par else 0
            pw = pi if "w" in _par else 0
            pa = pi if "a" in _par else 0
            q0 = 128 * qt
            klo, kw = _att_window(mode, qt)
            masks = _att_masks(P, mode, qt, klo, kw)
            nc_ = (kw + 511) // 512
            for hh in range(2):
                abuf = bfa[:, (2 * pi + hh) * 2048:(2 * pi + hh + 1) * 2048]
                kab = sl("bfa", (2 * pi + hh) * 2048, 2048)
                st0 = 32 * hh
                kst = f"attst{hh}"
                lhsT = qslot[hh][:, q0:q0 + 128]
                tp_ = (64 * hh, 0) if mode != "cross" else (0, 0)
                for c in range(nc_):
                    cw = min(512, kw - 512 * c)
                    bank = (2 * hh + (c + qt) % 2) if mode != "cross" else ((c + 2 * qt) % 4)
                    ps = P.sb[f"PS{bank}"]
                    kps = f"PS{bank}"
                    S.mm(ps[:, 0:cw], lhsT, kslot[hh][:, klo + 512 * c:klo + 512 * c + cw],
                         True, True, r=kq_keys[hh] + kk_keys[hh], w=[kps],
                         tag=f"{tag}.s{pr}_{qt}_{hh}_{c}", tile_position=tp_)
                    for moff, mwid, msk, mkey in masks:
                        lo, hi = moff, moff + mwid
                        if lo >= 512 * (c + 1) or hi <= 512 * c:
                            continue
                        ml, mh = max(lo, 512 * c), min(hi, 512 * (c + 1))
                        S.vop(lambda v, p_=ps[:, ml - 512 * c:mh - 512 * c],
                              m_=msk[:, ml - lo:mh - lo]:
                              v.tensor_tensor(p_, p_, m_, ALU.add),
                              r=[kps, mkey], w=[kps],
                              tag=f"{tag}.mk{pr}_{qt}_{hh}_{c}")
                    if use_max:
                        S.vop(lambda v, o=att[:, st0 + c:st0 + c + 1], p_=ps[:, 0:cw]:
                              v.reduce_max(o, p_, axis=AX.X),
                              r=[kps], w=[kst], hazard=True,
                              tag=f"{tag}.mx{pr}_{qt}_{hh}_{c}")
                        S.vop(lambda v, o=att[:, st0 + 12:st0 + 13],
                              a=att[:, st0 + c:st0 + c + 1]:
                              v.tensor_scalar(o, a, -scale, None, ALU.mult),
                              r=[kst], w=[kst], hazard=True, tag=f"{tag}.nm{pr}_{qt}_{hh}_{c}")
                        S.act(abuf[:, 512 * c:512 * c + cw], ps[:, 0:cw], AF.Exp,
                              bias=att[:, st0 + 12:st0 + 13], scale=scale,
                              accum_out=att[:, st0 + 4 + c:st0 + 5 + c],
                              r=[kps, kst], w=kab + [kst],
                              tag=f"{tag}.ex{pr}_{qt}_{hh}_{c}")
                    else:
                        S.act(abuf[:, 512 * c:512 * c + cw], ps[:, 0:cw], AF.Exp,
                              scale=scale,
                              accum_out=att[:, st0 + 4 + c:st0 + 5 + c],
                              r=[kps], w=kab + [kst],
                              tag=f"{tag}.ex{pr}_{qt}_{hh}_{c}")
                if nc_ == 1:
                    S.vop(lambda v, o=att[:, st0 + 8:st0 + 9],
                          a=att[:, st0 + 4:st0 + 5]: v.reciprocal(o, a),
                          r=[kst], w=[kst], hazard=True,
                          tag=f"{tag}.rc{pr}_{qt}_{hh}")
                else:
                    S.vop(lambda v, o=att[:, st0 + 14:st0 + 15],
                          a=att[:, st0 + 4:st0 + 4 + nc_]:
                          v.reduce_sum(o, a, axis=AX.X),
                          r=[kst], w=[kst], hazard=True, tag=f"{tag}.tt{pr}_{qt}_{hh}")
                    S.vop(lambda v, o=att[:, st0 + 15:st0 + 16],
                          a=att[:, st0 + 14:st0 + 15]: v.reciprocal(o, a),
                          r=[kst], w=[kst], hazard=True, tag=f"{tag}.rc{pr}_{qt}_{hh}")
                scol = st0 + 8 if nc_ == 1 else st0 + 15
                for c in range(nc_):
                    cw = min(512, kw - 512 * c)
                    S.vop(lambda v, o=abuf[:, 512 * c:512 * c + cw],
                          l_=att[:, scol:scol + 1]:
                          v.tensor_scalar(o, o, l_, None, ALU.mult),
                          r=kab + [kst], w=kab, tag=f"{tag}.sc{pr}_{qt}_{hh}_{c}")

        def phase2(qt):
            pi = qt & 1
            klo, kw = _att_window(mode, qt)
            nks = (kw + 127) // 128
            for hh in range(2):
                abuf = bfa[:, (2 * pi + hh) * 2048:(2 * pi + hh + 1) * 2048]
                kab = sl("bfa", (2 * pi + hh) * 2048, 2048)
                atb = bfa[:, 8192 + hh * 2048:8192 + (hh + 1) * 2048]
                kat = sl("bfa", 8192 + hh * 2048, 2048)
                for ks in range(nks):
                    ksw = min(128, kw - 128 * ks)
                    slot = ks % 8
                    S.trans(P.sb["PBF"][0:ksw, slot * 128:slot * 128 + 128],
                            abuf[:, 128 * ks:128 * ks + ksw], ident,
                            r=kab + ["ident_bf"], w=["PBF"],
                            tag=f"{tag}.tp{pr}_{qt}_{hh}_{ks}")
                    if (ks + hh) % 2 == 0:
                        S.add("scalar", lambda e, o=atb[0:ksw, 128 * ks:128 * (ks + 1)],
                              p_=P.sb["PBF"][0:ksw, slot * 128:slot * 128 + 128]:
                              e.copy(o, p_), r=["PBF"], w=kat,
                              tag=f"{tag}.tc{pr}_{qt}_{hh}_{ks}")
                    else:
                        S.vop(lambda v, o=atb[0:ksw, 128 * ks:128 * (ks + 1)],
                              p_=P.sb["PBF"][0:ksw, slot * 128:slot * 128 + 128]:
                              v.tensor_copy(o, p_), r=["PBF"], w=kat,
                              tag=f"{tag}.tc{pr}_{qt}_{hh}_{ks}")
            for ks in range(nks):
                ksw = min(128, kw - 128 * ks)
                kcabs = (klo + 128 * ks) // 128
                for hh in range(2):
                    h = hA + hh
                    atb = bfa[:, 8192 + hh * 2048:8192 + (hh + 1) * 2048]
                    kat = sl("bfa", 8192 + hh * 2048, 2048)
                    vsl = vab[0:ksw, kcabs * 1024 + 64 * h:kcabs * 1024 + 64 * h + 64]
                    S.mm(P.sb["PS4"][64 * hh:64 * hh + 64, 0:128], vsl,
                         atb[0:ksw, 128 * ks:128 * (ks + 1)],
                         ks == 0, ks == nks - 1,
                         r=kv_all + kat, w=["PS4"],
                         tag=f"{tag}.av{pr}_{qt}_{hh}_{ks}",
                         tile_position=(0, 64 * hh))
            if qt % 2 == 0:
                S.add("scalar", lambda e, o=bfb[:, qt * 128:(qt + 1) * 128],
                      p_=P.sb["PS4"]: e.copy(o, p_[:, 0:128]),
                      r=["PS4"], w=sl("bfb", qt * 128, 128), tag=f"{tag}.oe{pr}_{qt}")
            else:
                S.vop(lambda v, o=bfb[:, qt * 128:(qt + 1) * 128],
                      p_=P.sb["PS4"]: v.tensor_copy(o, p_[:, 0:128]),
                      r=["PS4"], w=sl("bfb", qt * 128, 128), tag=f"{tag}.oe{pr}_{qt}")

        phase1(0)
        for qt in range(1, 16):
            phase1(qt)
            phase2(qt - 1)
        phase2(15)
        S.dma("sync", ao_ap[128 * pr:128 * (pr + 1), :], bfb[:, 0:2048],
              r=sl("bfb", 0, 2048), w=[kao], tag=f"{tag}.ao{pr}")

def emit_qk_prep(S, tag, qkvt_src, qf_dst, kf_dst, layer):
    P = S.P
    qk_ap, kqk = qkvt_src
    dsts = [qf_dst, kf_dst]
    bfa, bfb = P.sb["bfa"], P.sb["bfb"]
    wka, wkb, wkc = P.sb["wka"], P.sb["wkb"], P.sb["wkc"]
    att = P.sb["attst"]
    ident = P.sb["ident_bf"][:]
    for j, rowname in enumerate(["ffn_qs_sb", "ffn_ks_sb"]):
        S.mm(P.sb["PS4"][:, 0:64], P.sb["ones_row_f32"][:],
             P.sb[rowname][0:1, 64 * layer:64 * layer + 64], True, True,
             r=[rowname, "ones_row_f32"], w=["PS4"], tag=f"{tag}.sb{j}")
        S.add("scalar", lambda e, o=wkc[:, 64 * j:64 * (j + 1)], p_=P.sb["PS4"]:
              e.copy(o, p_[:, 0:64]), r=["PS4"], w=sl("wkc", 0, 512),
              tag=f"{tag}.sc{j}")
    for tile in range(16):
        pi = tile & 1
        for j in range(2):
            x_tm = bfa[:, (2 * pi + j) * 1024:(2 * pi + j + 1) * 1024]
            kxt = sl("bfa", (2 * pi + j) * 1024, 1024)
            S.dma("sync", x_tm, qk_ap[tile, :, 1024 * j:1024 * (j + 1)],
                  r=[kqk], w=kxt, tag=f"{tag}.x{tile}_{j}")
            sq = wka[:, (2 * pi + j) * 1024:(2 * pi + j + 1) * 1024]
            ksq = sl("wka", (2 * pi + j) * 1024, 1024)
            t1 = wka[:, 4096 + (2 * pi + j) * 1024:4096 + (2 * pi + j + 1) * 1024]
            kt1 = sl("wka", 4096 + (2 * pi + j) * 1024, 1024)
            qsw = wka[:, 8192 + (2 * pi + j) * 1024:8192 + (2 * pi + j + 1) * 1024]
            kqs = sl("wka", 8192 + (2 * pi + j) * 1024, 1024)
            t2 = wkb[:, (2 * pi + j) * 1024:(2 * pi + j + 1) * 1024]
            kt2 = sl("wkb", (2 * pi + j) * 1024, 1024)
            xrot = bfb[:, (2 * pi + j) * 1024:(2 * pi + j + 1) * 1024]
            kxr = sl("bfb", (2 * pi + j) * 1024, 1024)
            stc = 16 * (2 * pi + j)
            kst = f"attst{2 * pi + j}"
            r3 = lambda a: a.rearrange("p (h d) -> p h d", h=16)
            S.act(sq, x_tm, AF.Square, r=kxt, w=ksq, tag=f"{tag}.sq{tile}_{j}")
            S.vop(lambda v, o=att[:, stc:stc + 16], a=r3(sq):
                  v.reduce_sum(o, a, axis=AX.X), r=ksq, w=[kst], hazard=True,
                  tag=f"{tag}.ss{tile}_{j}")
            S.act(att[:, stc:stc + 16], att[:, stc:stc + 16], AF.Sqrt,
                  r=[kst], w=[kst], hazard=True, tag=f"{tag}.nr{tile}_{j}")
            S.vop(lambda v, o=att[:, stc:stc + 16]:
                  v.tensor_scalar(o, o, 1e-12, None, ALU.max),
                  r=[kst], w=[kst], hazard=True, tag=f"{tag}.mx{tile}_{j}")
            S.vop(lambda v, o=att[:, stc:stc + 16]:
                  v.reciprocal(o, o), r=[kst], w=[kst], hazard=True,
                  tag=f"{tag}.rc{tile}_{j}")
            S.vop(lambda v, o=r3(t1), a=r3(x_tm),
                  b_=att[:, stc:stc + 16].unsqueeze(2).broadcast_to([128, 16, 64]):
                  v.tensor_tensor(o, a, b_, ALU.mult),
                  r=kxt + [kst], w=kt1, tag=f"{tag}.t1_{tile}_{j}")
            S.pop(lambda e, o=r3(t1), a=r3(t1),
                  b_=wkc[:, 64 * j:64 * (j + 1)].unsqueeze(1).broadcast_to([128, 16, 64]):
                  e.tensor_tensor(o, a, b_, ALU.mult),
                  r=kt1 + sl("wkc", 0, 512), w=kt1, tag=f"{tag}.t1b{tile}_{j}")
            S.vop(lambda v, o=r3(qsw)[:, :, 0:32], a=r3(t1)[:, :, 32:64]:
                  v.tensor_scalar(o, a, -1.0, None, ALU.mult),
                  r=kt1, w=kqs, tag=f"{tag}.rh0{tile}_{j}")
            S.vop(lambda v, o=r3(qsw)[:, :, 32:64], a=r3(t1)[:, :, 0:32]:
                  v.tensor_copy(o, a), r=kt1, w=kqs, tag=f"{tag}.rh1{tile}_{j}")
            ctab = P.sb[("rot_c1_sb" if j == 0 else "rot_c2_sb")]
            stab = P.sb[("rot_s1_sb" if j == 0 else "rot_s2_sb")]
            ckey = "rot_c1" if j == 0 else "rot_c2"
            skey = "rot_s1" if j == 0 else "rot_s2"
            S.pop(lambda e, o=r3(t2), a=r3(t1),
                  b_=ctab[:, 64 * tile:64 * (tile + 1)].unsqueeze(1)
                  .broadcast_to([128, 16, 64]):
                  e.tensor_tensor(o, a, b_, ALU.mult),
                  r=kt1 + [ckey], w=kt2, tag=f"{tag}.t2{tile}_{j}")
            S.vop(lambda v, o=r3(qsw), a=r3(qsw),
                  b_=stab[:, 64 * tile:64 * (tile + 1)].unsqueeze(1)
                  .broadcast_to([128, 16, 64]):
                  v.tensor_tensor(o, a, b_, ALU.mult),
                  r=kqs + [skey], w=kqs, tag=f"{tag}.t3{tile}_{j}")
            S.pop(lambda e, o=xrot, a=t2, b_=qsw:
                  e.tensor_tensor(o, a, b_, ALU.add),
                  r=kt2 + kqs, w=kxr, tag=f"{tag}.xr{tile}_{j}")
            dst_ap, kdst = dsts[j]
            for pc in range(8):
                slot = pc % 8
                S.trans(P.sb["PBF"][:, slot * 128:(slot + 1) * 128],
                        xrot[:, pc * 128:(pc + 1) * 128], ident,
                        r=kxr + ["ident_bf"], w=["PBF"],
                        tag=f"{tag}.tp{tile}_{j}_{pc}")
                stg = bfa[:, 4096 + ((2 * pi + j) * 2 + pc % 2) * 128:
                          4096 + ((2 * pi + j) * 2 + pc % 2 + 1) * 128]
                kstg = sl("bfa", 4096 + ((2 * pi + j) * 2 + pc % 2) * 128, 128)
                if pc % 2 == 0:
                    S.vop(lambda v, o=stg, p_=P.sb["PBF"][:, slot * 128:(slot + 1) * 128]:
                          v.tensor_copy(o, p_), r=["PBF"], w=kstg,
                          tag=f"{tag}.tc{tile}_{j}_{pc}")
                else:
                    S.add("scalar", lambda e, o=stg,
                          p_=P.sb["PBF"][:, slot * 128:(slot + 1) * 128]:
                          e.copy(o, p_), r=["PBF"], w=kstg,
                          tag=f"{tag}.tc{tile}_{j}_{pc}")
                S.dma("sync", dst_ap[128 * pc:128 * (pc + 1), 128 * tile:128 * (tile + 1)],
                      stg, r=kstg, w=[kdst], tag=f"{tag}.st{tile}_{j}_{pc}")


def emit_glu_mul(S, tag, a_src, g_src, out, rows):
    P = S.P
    a_ap, ka = a_src
    g_ap, kg = g_src
    o_ap, ko = out
    bfa, bfb = P.sb["bfa"], P.sb["bfb"]
    GR = 512
    for t0 in range(0, T, TC):
        for r0 in range(0, rows, GR):
            rw = min(GR, rows - r0)
            nr = rw // 128
            pi = (t0 // TC) & 1
            a_sb = bfa[:, pi * 4096:pi * 4096 + nr * TC]
            g_sb = bfa[:, 8192 + pi * 4096:8192 + pi * 4096 + nr * TC]
            o_sb = bfb[:, pi * 2048:pi * 2048 + nr * TC]
            ka_ = sl("bfa", pi * 4096, nr * TC)
            kg_ = sl("bfa", 8192 + pi * 4096, nr * TC)
            ko_ = sl("bfb", pi * 2048, nr * TC)
            S.dma("sync", a_sb.rearrange("p (c t) -> p c t", c=nr),
                  a_ap[r0:r0 + rw, t0:t0 + TC].rearrange("(c p) t -> p c t", p=128),
                  r=[ka], w=ka_, tag=f"{tag}.a{t0}_{r0}")
            S.dma("sync", g_sb.rearrange("p (c t) -> p c t", c=nr),
                  g_ap[r0:r0 + rw, t0:t0 + TC].rearrange("(c p) t -> p c t", p=128),
                  r=[kg], w=kg_, tag=f"{tag}.g{t0}_{r0}")
            S.pop(lambda e, o=o_sb, a=a_sb, b_=g_sb:
                  e.tensor_tensor(o, a, b_, ALU.mult),
                  r=ka_ + kg_, w=ko_, tag=f"{tag}.m{t0}_{r0}")
            S.dma("gpsimd", o_ap[r0:r0 + rw, t0:t0 + TC].rearrange("(c p) t -> p c t", p=128),
                  o_sb.rearrange("p (c t) -> p c t", c=nr),
                  r=ko_, w=[ko], tag=f"{tag}.o{t0}_{r0}")


def emit_w1_fused(S, tag, src, w_dram, out, bias, layer_pad=DIM_INNER_PAD):
    P = S.P
    src_ap, ksrc = src
    w_ap, kw = w_dram
    out_ap, kout = out
    wgt, bfa, bfb, wkb = P.sb["wgt"], P.sb["bfa"], P.sb["bfb"], P.sb["wkb"]
    hb16 = wkb[:].bitcast(BF16)
    OW = 512
    for olo in range(0, layer_pad, OW):
        ow = min(OW, layer_pad - olo)
        now = ow // 128
        for c in range(8):
            S.dma("gpsimd", wgt[:, c * ow:(c + 1) * ow],
                  w_ap[128 * c:128 * (c + 1), olo:olo + ow],
                  r=[kw], w=sl("wgt", c * ow, ow), tag=f"{tag}.wa{olo}_{c}")
            S.dma("gpsimd", wgt[:, 4096 + c * ow:4096 + (c + 1) * ow],
                  w_ap[128 * c:128 * (c + 1), layer_pad + olo:layer_pad + olo + ow],
                  r=[kw], w=sl("wgt", 4096 + c * ow, ow), tag=f"{tag}.wg{olo}_{c}")
        for i in range(NTC):
            pi = i & 1
            t0 = i * TC
            xbf = bfa[:, pi * 4096:(pi + 1) * 4096]
            kxb = sl("bfa", pi * 4096, 4096)
            S.dma("sync", xbf.rearrange("p (c t) -> p c t", c=ND),
                  src_ap[:, t0:t0 + TC].rearrange("(c p) t -> p c t", p=128),
                  r=[ksrc], w=kxb, tag=f"{tag}.x{olo}_{i}")
            for o in range(now):
                psa = P.sb[f"PS{o % 2}"]
                kpa = f"PS{o % 2}"
                psg = P.sb[f"PS{2 + o % 2}"]
                kpg = f"PS{2 + o % 2}"
                for c in range(8):
                    S.mm(psa[:], wgt[:, c * ow + o * 128:c * ow + (o + 1) * 128],
                         xbf[:, c * TC:(c + 1) * TC], c == 0, c == 7,
                         r=sl("wgt", c * ow + o * 128, 128) + kxb, w=[kpa],
                         tag=f"{tag}.ma{olo}_{i}_{o}")
                for c in range(8):
                    S.mm(psg[:], wgt[:, 4096 + c * ow + o * 128:4096 + c * ow + (o + 1) * 128],
                         xbf[:, c * TC:(c + 1) * TC], c == 0, c == 7,
                         r=sl("wgt", 4096 + c * ow + o * 128, 128) + kxb, w=[kpg],
                         tag=f"{tag}.mg{olo}_{i}_{o}")
                ocol = (olo + o * 128) // 128
                ba = blob_col(P, bias, ocol)
                bg = blob_col(P, bias, layer_pad // 128 + ocol)
                slot = pi * 2 + o % 2
                a_bf = bfb[:, slot * 512:(slot + 1) * 512]
                ka = sl("bfb", slot * 512, 512)
                g_bf = bfa[:, 8192 + slot * 512:8192 + (slot + 1) * 512]
                kg = sl("bfa", 8192 + slot * 512, 512)
                h_bf = hb16[:, slot * 512:(slot + 1) * 512]
                kh = sl("wkb", slot * 256, 256)
                S.vop(lambda v, y=a_bf, p_=psa, b_=ba:
                      v.tensor_scalar(y, p_[:], b_, None, ALU.add),
                      r=[kpa, "blob"], w=ka, tag=f"{tag}.ea{olo}_{i}_{o}")
                S.act(g_bf, psg[:], AF.Gelu, bias=bg,
                      r=[kpg, "blob"], w=kg, tag=f"{tag}.eg{olo}_{i}_{o}")
                S.pop(lambda e, y=h_bf, a=a_bf, g=g_bf:
                      e.tensor_tensor(y, a, g, ALU.mult),
                      r=ka + kg, w=kh, tag=f"{tag}.h{olo}_{i}_{o}")
                S.dma("sync", out_ap[olo + o * 128:olo + (o + 1) * 128, t0:t0 + TC],
                      h_bf, r=kh, w=[kout], tag=f"{tag}.st{olo}_{i}_{o}")


def emit_l1_decay(S, tag, x_src, ldrow_slot=0):
    P = S.P
    x_ap, kx = x_src
    bfa = P.sb["bfa"]
    pra = P.sb["PRA"]
    rows = P.sb["rows"]
    wd = P.sb["wkc"][:, 1024:1032].bitcast(BF16)
    S.dma("gpsimd", wd[:, 0:8], P.input("l1a_wd")[:, :],
          r=["l1a_wd"], w=sl("wkc", 1024, 8), tag=f"{tag}.wd")
    for i in range(NTC):
        pi = i & 1
        t0 = i * TC
        xbf = bfa[:, pi * 4096:(pi + 1) * 4096]
        kxb = sl("bfa", pi * 4096, 4096)
        S.dma("sync", xbf.rearrange("p (c t) -> p c t", c=ND),
              x_ap[:, t0:t0 + TC].rearrange("(c p) t -> p c t", p=128),
              r=[kx], w=kxb, tag=f"{tag}.x{i}")
        for c in range(ND):
            S.mm(pra[:, :], wd[:, c:c + 1], xbf[:, c * TC:(c + 1) * TC],
                 c == 0, c == ND - 1, r=sl("wkc", 1024, 8) + kxb, w=["PRA"],
                 tag=f"{tag}.mm{i}")
        S.act(rows[0:1, 0:TC], pra[:, :], AF.Sigmoid,
              bias=P.sb["l1a_bd_sb"][0:1, :],
              r=["PRA", "l1a_bd_sb"], w=["rows"], hazard=True, tag=f"{tag}.sg{i}")
        S.act(rows[0:1, TC:2 * TC], rows[0:1, 0:TC], AF.Ln,
              bias=P.sb["eps8_col"][0:1, :],
              r=["rows", "eps8_col"], w=["rows"], hazard=True, tag=f"{tag}.ln{i}")
        S.dma("sync", P.dram["LDROW"][0:1, t0:t0 + TC], rows[0:1, TC:2 * TC],
              r=["rows"], w=["LDROW"], tag=f"{tag}.st{i}")


def emit_l1_core(S, tag, qf_src, kf_src, vt_src, out_o):
    P = S.P
    qf_ap, kqf = qf_src
    kf_ap, kkf = kf_src
    vt_ap, kvt = vt_src
    o_ap, ko = out_o
    bfa, wkb, wkc = P.sb["bfa"], P.sb["wkb"], P.sb["wkc"]
    vab = P.sb["wka"][:].bitcast(BF16)
    ident = P.sb["ident_bf"][:]
    rows = P.sb["rows"]
    S.pop(lambda e: e.iota(wkb[:, 0:2432], pattern=[[1, 2432]], base=-385,
                           channel_multiplier=-1,
                           allow_small_or_imprecise_dtypes=True),
          w=sl("wkb", 0, 2432), tag=f"{tag}.iom")
    kiom = sl("wkb", 0, 2432)
    S.dma("sync", rows[0:1, 0:T], P.dram["LDROW"][0:1, :], r=["LDROW"],
          w=["rows"], tag=f"{tag}.ld")
    for kc in range(16):
        S.dma("sync", vab[:, kc * 1024:(kc + 1) * 1024], vt_ap[kc, :, :],
              r=[kvt], w=sl("wka", kc * 512, 512), tag=f"{tag}.v{kc}")
    kv_all = sl("wka", 0, 8192)

    for kc in range(4):
        kbase = 512 * kc
        S.dma("sync", bfa[:, 0:4096].rearrange("p (c t) -> p c t", c=ND),
              kf_ap[:, kbase:kbase + 512].rearrange("(c p) t -> p c t", p=128),
              r=[kkf], w=sl("bfa", 0, 4096), tag=f"{tag}.k{kc}")
        kkc = sl("bfa", 0, 4096)
        S.mm(P.sb["PS4"][:], P.sb["ones_row_f32"][:], rows[0:1, kbase:kbase + 512],
             True, True, r=["rows", "ones_row_f32"], w=["PS4"], tag=f"{tag}.lb{kc}")
        S.add("scalar", lambda e, o=wkb[:, 2560:3072], p_=P.sb["PS4"]:
              e.copy(o, p_[:]), r=["PS4"], w=sl("wkb", 2560, 512),
              tag=f"{tag}.lc{kc}")
        kldb = sl("wkb", 2560, 512)
        def phase1(qt):
            import os as _os2
            _par = _os2.environ.get("L1_PAR", "qwa")
            pi = (qt & 1)
            pq = pi if "q" in _par else 0
            pw = pi if "w" in _par else 0
            pa = pi if "a" in _par else 0
            q0 = 128 * qt
            jlo = max(0, q0 - kbase)
            cw = 512 - jlo
            qt_sb = bfa[:, 4096 + pq * 1024:5120 + pq * 1024]
            kqt = sl("bfa", 4096 + pq * 1024, 1024)
            S.dma("sync", qt_sb.rearrange("p (c t) -> p c t", c=ND),
                  qf_ap[:, q0:q0 + 128].rearrange("(c p) t -> p c t", p=128),
                  r=[kqf], w=kqt, tag=f"{tag}.q{kc}_{qt}")
            ps = P.sb[f"PS{qt % 2}"]
            kps = f"PS{qt % 2}"
            for c in range(ND):
                S.mm(ps[:, 0:cw], qt_sb[:, c * 128:(c + 1) * 128],
                     bfa[:, c * 512 + jlo:c * 512 + 512], c == 0, c == ND - 1,
                     r=kqt + kkc, w=[kps], tag=f"{tag}.s{kc}_{qt}")

        def phase2(qt):
            import os as _os2
            _par = _os2.environ.get("L1_PAR", "qwa")
            pi = (qt & 1)
            pq = pi if "q" in _par else 0
            pw = pi if "w" in _par else 0
            pa = pi if "a" in _par else 0
            q0 = 128 * qt
            jlo = max(0, q0 - kbase)
            cw = 512 - jlo
            ps = P.sb[f"PS{qt % 2}"]
            kps = f"PS{qt % 2}"
            ioff = kbase + jlo - q0 + 384
            wt = wkb[:, 3072 + pw * 512:3072 + pw * 512 + cw]
            kwt = sl("wkb", 3072 + pw * 512, cw)
            S.vop(lambda v, o=wt, a=wkb[:, ioff:ioff + cw], b_=wkb[:, 2560 + jlo:3072]:
                  v.scalar_tensor_tensor(o, a, 0.0, b_, ALU.max, ALU.mult),
                  r=kiom + kldb, w=kwt, tag=f"{tag}.w{kc}_{qt}")
            S.act(wt, wt, AF.Exp, r=kwt, w=kwt, tag=f"{tag}.e{kc}_{qt}")
            import os as _os
            _d = _os.environ.get("L1_DBG", "")
            if _d and qt == 0 and kc == 3:
                if "w" in _d:
                    S.dma("sync", P.dram["DBGW"][:, 0:cw], wt, r=kwt, w=["DBGW"],
                          tag=f"{tag}.dbgw")
                if "l" in _d:
                    S.dma("sync", P.dram["DBGL"][:, 0:512], wkb[:, 2560:3072],
                          r=kldb, w=["DBGW"], tag=f"{tag}.dbgl")
                if "i" in _d:
                    S.dma("sync", P.dram["DBGI"][:, 0:cw], wkb[:, ioff:ioff + cw],
                          r=kiom, w=["DBGW"], tag=f"{tag}.dbgi")
            abf = P.sb["bfb"][:, pa * 512:pa * 512 + cw]
            kab = sl("bfb", pa * 512, cw)
            S.vop(lambda v, o=abf, p_=ps[:, 0:cw], b_=wt:
                  v.tensor_tensor(o, p_, b_, ALU.mult),
                  r=[kps] + kwt, w=kab, tag=f"{tag}.a{kc}_{qt}")
            if kbase <= q0 < kbase + 512:
                S.pop(lambda e, o=abf[:, 0:128], m=P.sb["m01_upper"][:]:
                      e.tensor_tensor(o, o, m, ALU.mult),
                      r=kab + ["m01_upper"], w=kab, tag=f"{tag}.dm{kc}_{qt}")
            nks = cw // 128
            for ks in range(nks):
                slot = ks % 8
                S.trans(P.sb["PBF"][:, slot * 128:(slot + 1) * 128],
                        abf[:, ks * 128:(ks + 1) * 128], ident,
                        r=kab + ["ident_bf"], w=["PBF"], tag=f"{tag}.tp{kc}_{qt}_{ks}")
                S.vop(lambda v, o=P.sb["bfb"][:, 2048 + ks * 128:2048 + (ks + 1) * 128],
                      p_=P.sb["PBF"][:, slot * 128:(slot + 1) * 128]:
                      v.tensor_copy(o, p_), r=["PBF"], w=sl("bfb", 2048 + ks * 128, 128),
                      tag=f"{tag}.tc{kc}_{qt}_{ks}")
            kat = sl("bfb", 2048, cw)
            for o in range(ND):
                pso = P.sb[f"PS{2 + o % 2}"]
                kpso = f"PS{2 + o % 2}"
                for ks in range(nks):
                    kt_abs = (kbase + jlo) // 128 + ks
                    vsl = vab[:, kt_abs * 1024 + o * 128:kt_abs * 1024 + (o + 1) * 128]
                    S.mm(pso[:, 0:128], vsl,
                         P.sb["bfb"][:, 2048 + ks * 128:2048 + (ks + 1) * 128],
                         ks == 0, ks == nks - 1, r=kv_all + kat, w=[kpso],
                         tag=f"{tag}.av{kc}_{qt}_{o}")
                S.vop(lambda v, oo=wkb[:, 4096 + o * 128:4096 + (o + 1) * 128],
                      p_=pso: v.tensor_copy(oo, p_[:, 0:128]),
                      r=[kpso], w=sl("wkb", 4096 + o * 128, 128),
                      tag=f"{tag}.oe{kc}_{qt}_{o}")
            first = (kc == (q0 // 512))
            S.dma("gpsimd", o_ap[:, q0:q0 + 128].rearrange("(c p) t -> p c t", p=128),
                  wkb[:, 4096:5120].rearrange("p (c t) -> p c t", c=ND),
                  r=sl("wkb", 4096, 1024), w=[ko], tag=f"{tag}.oa{kc}_{qt}",
                  **({} if first else {"accum_op": ALU.add}))

        qts = [q for q in range(4 * kc + 4) if q < 16]
        import os as _os
        if _os.environ.get("L1_NOPIPE"):
            for q in qts:
                phase1(q)
                phase2(q)
        elif qts:
            phase1(qts[0])
            for idx in range(1, len(qts)):
                phase1(qts[idx])
                phase2(qts[idx - 1])
            phase2(qts[-1])


def emit_moe_router(S, tag, x_src, cwf_dst):
    P = S.P
    x_ap, kx = x_src
    cw_ap, kcw = cwf_dst
    bfa, wkc = P.sb["bfa"], P.sb["wkc"]
    att = P.sb["attst"]
    wr = wkc[:, 1024:1088].bitcast(BF16)
    S.dma("gpsimd", wr[:, 0:128].rearrange("p (c o) -> p c o", c=ND),
          P.input("moe_wr").rearrange("(c p) o -> p c o", p=128),
          r=["moe_wr"], w=sl("wkc", 1024, 64), tag=f"{tag}.wr")
    kwr = sl("wkc", 1024, 64)
    for i in range(NTC):
        pi = i & 1
        t0 = i * TC
        xbf = bfa[:, pi * 4096:(pi + 1) * 4096]
        kxb = sl("bfa", pi * 4096, 4096)
        S.dma("sync", xbf.rearrange("p (c t) -> p c t", c=ND),
              x_ap[:, t0:t0 + TC].rearrange("(c p) t -> p c t", p=128),
              r=[kx], w=kxb, tag=f"{tag}.x{i}")
        for tt in range(4):
            ps = P.sb[f"PS{tt % 2}"]
            kps = f"PS{tt % 2}"
            for c in range(ND):
                S.mm(ps[:, 0:16], xbf[:, c * TC + tt * 128:c * TC + (tt + 1) * 128],
                     wr[:, c * 16:(c + 1) * 16], c == 0, c == ND - 1,
                     r=kxb + kwr, w=[kps], tag=f"{tag}.mm{i}_{tt}")
            st0 = 16 * (tt % 2)
            kst = f"attst{tt % 2}"
            probs = wkc[:, (tt % 2) * 512:(tt % 2) * 512 + 16]
            kpb = sl("wkc", (tt % 2) * 512, 16)
            S.vop(lambda v, o=att[:, st0:st0 + 1], p_=ps[:, 0:16]:
                  v.reduce_max(o, p_, axis=AX.X), r=[kps], w=[kst], hazard=True,
                  tag=f"{tag}.m1{i}_{tt}")
            S.vop(lambda v, o=att[:, st0 + 1:st0 + 2], a=att[:, st0:st0 + 1]:
                  v.tensor_scalar(o, a, -1.0, None, ALU.mult),
                  r=[kst], w=[kst], hazard=True, tag=f"{tag}.ng{i}_{tt}")
            S.act(probs, ps[:, 0:16], AF.Exp, bias=att[:, st0 + 1:st0 + 2],
                  accum_out=att[:, st0 + 2:st0 + 3],
                  r=[kps, kst], w=kpb + [kst], tag=f"{tag}.ex{i}_{tt}")
            S.vop(lambda v, o=att[:, st0 + 3:st0 + 4], a=att[:, st0 + 2:st0 + 3]:
                  v.reciprocal(o, a), r=[kst], w=[kst], hazard=True,
                  tag=f"{tag}.rc{i}_{tt}")
            S.vop(lambda v, o=probs, r_=att[:, st0 + 3:st0 + 4]:
                  v.tensor_scalar(o, o, r_, None, ALU.mult),
                  r=kpb + [kst], w=kpb, hazard=True, tag=f"{tag}.nm{i}_{tt}")
            S.vop(lambda v, o=att[:, st0 + 4:st0 + 5], p_=probs:
                  v.reduce_max(o, p_, axis=AX.X), r=kpb, w=[kst], hazard=True,
                  tag=f"{tag}.p1{i}_{tt}")
            tmp = wkc[:, (tt % 2) * 512 + 16:(tt % 2) * 512 + 32]
            ktm = kpb
            S.vop(lambda v, o=tmp, p_=probs, m_=att[:, st0 + 4:st0 + 5]:
                  v.tensor_scalar(o, p_, m_, None, ALU.is_ge),
                  r=kpb + [kst], w=ktm, hazard=True, tag=f"{tag}.ge{i}_{tt}")
            S.vop(lambda v, o=tmp, p_=probs:
                  v.scalar_tensor_tensor(o, o, -2.0, p_, ALU.mult, ALU.add),
                  r=ktm + kpb, w=ktm, hazard=True, tag=f"{tag}.tm{i}_{tt}")
            S.vop(lambda v, o=att[:, st0 + 5:st0 + 6], p_=tmp:
                  v.reduce_max(o, p_, axis=AX.X), r=ktm, w=[kst], hazard=True,
                  tag=f"{tag}.p2{i}_{tt}")
            S.vop(lambda v, o=tmp, p_=probs, m_=att[:, st0 + 5:st0 + 6]:
                  v.tensor_scalar(o, p_, m_, None, ALU.is_ge),
                  r=kpb + [kst], w=ktm, hazard=True, tag=f"{tag}.g2{i}_{tt}")
            S.vop(lambda v, o=probs, m_=tmp:
                  v.tensor_tensor(o, o, m_, ALU.mult),
                  r=kpb + ktm, w=kpb, hazard=True, tag=f"{tag}.cw{i}_{tt}")
            S.trans(P.sb["PS2"][0:16, 0:128], probs, P.sb["ident_f32"][:],
                    r=kpb + ["ident_f32"], w=["PS2"], tag=f"{tag}.tp{i}_{tt}")
            S.add("scalar", lambda e, o=wkc[0:16, 1536:1664],
                  p_=P.sb["PS2"][0:16, 0:128]: e.copy(o, p_),
                  r=["PS2"], w=sl("wkc", 1536, 128), tag=f"{tag}.cp{i}_{tt}")
            S.dma("sync", cw_ap[:, t0 + tt * 128:t0 + (tt + 1) * 128],
                  wkc[0:16, 1536:1664], r=sl("wkc", 1536, 128), w=[kcw],
                  tag=f"{tag}.st{i}_{tt}")


def emit_moe_experts(S, tag, x_src, acc_dram):
    P = S.P
    x_ap, kx = x_src
    a_ap, kacc = acc_dram
    bfa, bfb, wka, wkb, wkc, wgt = (P.sb[n] for n in
                                    ["bfa", "bfb", "wka", "wkb", "wkc", "wgt"])
    vab = wka[:].bitcast(BF16)
    for tci in range(NTC):
        t0 = tci * TC
        S.dma("sync", wkb[:, 0:4096].rearrange("p (c t) -> p c t", c=ND),
              a_ap[:, t0:t0 + TC].rearrange("(c p) t -> p c t", p=128),
              r=[kacc], w=sl("wkb", 0, 4096), tag=f"{tag}.acc{tci}")
        kya = sl("wkb", 0, 4096)
        S.dma("sync", bfa[:, 0:4096].rearrange("p (c t) -> p c t", c=ND),
              x_ap[:, t0:t0 + TC].rearrange("(c p) t -> p c t", p=128),
              r=[kx], w=sl("bfa", 0, 4096), tag=f"{tag}.x{tci}")
        kxb = sl("bfa", 0, 4096)
        for e in range(16):
            S.dma("gpsimd", wgt[:, 0:4096].rearrange("p (c o) -> p c o", c=ND),
                  P.input("moe_wg")[e].rearrange("(c p) o -> p c o", p=128),
                  r=["moe_wg"], w=sl("wgt", 0, 4096), tag=f"{tag}.wg{tci}_{e}")
            S.dma("gpsimd", wgt[:, 4096:8192].rearrange("p (c o) -> p c o", c=ND),
                  P.input("moe_wu")[e].rearrange("(c p) o -> p c o", p=128),
                  r=["moe_wu"], w=sl("wgt", 4096, 4096), tag=f"{tag}.wu{tci}_{e}")
            S.dma("gpsimd", vab[:, 8192:12288].rearrange("p (c o) -> p c o", c=4),
                  P.input("moe_wd")[e].rearrange("(c p) o -> p c o", p=128),
                  r=["moe_wd"], w=sl("wka", 4096, 2048), tag=f"{tag}.wd{tci}_{e}")
            kwg = sl("wgt", 0, 4096)
            kwu = sl("wgt", 4096, 4096)
            kwd = sl("wka", 4096, 2048)
            S.dma("sync", P.sb["rows"][0:1, 0:TC], P.dram["CWF"][e:e + 1, t0:t0 + TC],
                  r=["CWF"], w=["rows"], tag=f"{tag}.cw{tci}_{e}")
            S.mm(P.sb["PS4"][:], P.sb["ones_row_f32"][:], P.sb["rows"][0:1, 0:TC],
                 True, True, r=["rows", "ones_row_f32"], w=["PS4"],
                 tag=f"{tag}.cb{tci}_{e}")
            S.add("scalar", lambda e2, o=wkc[:, 0:512], p_=P.sb["PS4"]:
                  e2.copy(o, p_[:]), r=["PS4"], w=sl("wkc", 0, 512),
                  tag=f"{tag}.cc{tci}_{e}")
            kcwb = sl("wkc", 0, 512)
            for f in range(4):
                psg = P.sb["PS0"]
                psu = P.sb["PS1"]
                for c in range(ND):
                    S.mm(psg[:], wgt[:, c * 512 + f * 128:c * 512 + (f + 1) * 128],
                         bfa[:, c * TC:(c + 1) * TC], c == 0, c == ND - 1,
                         r=kwg + kxb, w=["PS0"], tag=f"{tag}.hg{tci}_{e}_{f}")
                for c in range(ND):
                    S.mm(psu[:], wgt[:, 4096 + c * 512 + f * 128:4096 + c * 512 + (f + 1) * 128],
                         bfa[:, c * TC:(c + 1) * TC], c == 0, c == ND - 1,
                         r=kwu + kxb, w=["PS1"], tag=f"{tag}.hu{tci}_{e}_{f}")
                ssb = bfb[:, 2048 + f * 512:2048 + (f + 1) * 512]
                kss = sl("bfb", 2048 + f * 512, 512)
                S.act(ssb, psg[:], AF.Silu, r=["PS0"], w=kss,
                      tag=f"{tag}.si{tci}_{e}_{f}")
                t1b = wkb[:, 4096 + f * 512:4096 + (f + 1) * 512]
                kt1 = sl("wkb", 4096 + f * 512, 512)
                S.vop(lambda v, o=t1b, a=ssb, p_=psu:
                      v.tensor_tensor(o, a, p_[:], ALU.mult),
                      r=kss + ["PS1"], w=kt1, tag=f"{tag}.pr{tci}_{e}_{f}")
                hb = bfb[:, f * 512:(f + 1) * 512]
                khb = sl("bfb", f * 512, 512)
                S.pop(lambda e2, o=hb, a=t1b, b_=wkc[:, 0:512]:
                      e2.tensor_tensor(o, a, b_, ALU.mult),
                      r=kt1 + kcwb, w=khb, tag=f"{tag}.hc{tci}_{e}_{f}")
            kh = sl("bfb", 0, 2048)
            for o in range(ND):
                pso = P.sb[f"PS{2 + o % 2}"]
                kpso = f"PS{2 + o % 2}"
                for f in range(4):
                    S.mm(pso[:], vab[:, 8192 + f * 1024 + o * 128:8192 + f * 1024 + (o + 1) * 128],
                         bfb[:, f * 512:(f + 1) * 512], f == 0, f == 3,
                         r=kwd + kh, w=[kpso], tag=f"{tag}.y{tci}_{e}_{o}")
                S.vop(lambda v, oo=wkb[:, o * 512:(o + 1) * 512], p_=pso:
                      v.tensor_tensor(oo, oo, p_[:], ALU.add),
                      r=kya + [kpso], w=kya, tag=f"{tag}.ya{tci}_{e}_{o}")
        S.dma("sync", a_ap[:, t0:t0 + TC].rearrange("(c p) t -> p c t", p=128),
              wkb[:, 0:4096].rearrange("p (c t) -> p c t", c=ND),
              r=kya, w=[kacc], tag=f"{tag}.st{tci}")
